# version 09
# baseline (speedup 1.0000x reference)
import math
from contextlib import ExitStack

import numpy as np
import ml_dtypes
import concourse.bass as bass
import concourse.mybir as mybir
from concourse.bass_utils import run_bass_kernel_spmd

F32 = mybir.dt.float32
BF16 = mybir.dt.bfloat16
AF = mybir.ActivationFunctionType
ALU = mybir.AluOpType

T = 2048
D = 2048
H = 16
HD = 128
NTC = T // 128
NCC = D // 128
HID = 8192
NHC = HID // 128
G = 8
NGR = NHC // G
LN_EPS = 1e-5
EPS_A = 1e-6
EPS_B = 1e-5
FWHT_SCALE = float(np.float32(1.0 / np.sqrt(np.float64(HID))))
RD = float(1.0 / math.sqrt(HD))

N_SUB = 4

ENGINES = ("pe", "ve", "ac", "gp", "sy")


class Ring:

    def __init__(self, bufs):
        self.bufs = list(bufs)
        self.war = [None] * len(self.bufs)
        self.idx = 0

    def acquire(self):
        s = self.idx % len(self.bufs)
        self.idx += 1
        return s, self.bufs[s], self.war[s]

    def consumer(self, s, eng_key):
        self.war[s] = eng_key


class Prog:
    def __init__(self, nc, sems):
        self.nc = nc
        self.sems = sems
        self.cnt = {k: 0 for k in sems}
        self.keys = {}
        self.ops = {e: [] for e in ENGINES}

    ENGSEM = {"ve": "ve", "ac": "ac", "gp": "gc"}

    def op(self, eng, fn, inc=None, k=1, key=None, dma=False):
        auto = (not dma) and eng in self.ENGSEM
        if auto:
            es = self.ENGSEM[eng]
            if inc is None:
                inc = es
                k = 1
            assert inc == es
            prevv = self.cnt[es]
            if prevv > 0:
                sem0 = self.sems[es]
                self.ops[eng].append(
                    lambda e, sem0=sem0, prevv=prevv: e.wait_ge(sem0, prevv))
        if inc is not None:
            self.cnt[inc] += k
            if key is not None:
                assert key not in self.keys, f"dup key {key}"
                self.keys[key] = (inc, self.cnt[inc])
            sem = self.sems[inc]
            self.ops[eng].append(lambda e, fn=fn, sem=sem, kk=k: fn(e).then_inc(sem, kk))
        else:
            self.ops[eng].append(lambda e, fn=fn: fn(e))

    def dma(self, eng, out, in_, inc, key=None):
        self.op(eng, lambda e, out=out, in_=in_: e.dma_start(out, in_),
                inc=inc, k=16, key=key, dma=True)

    def all_dma_sems(self):
        return [n for n in self.sems if n not in ("pe", "ve", "ac", "gc")]

    def alias(self, new_key, old_key):
        self.keys[new_key] = self.keys[old_key]

    def wait(self, eng, key):
        assert key in self.keys, f"missing key {key}"
        sem_name, val = self.keys[key]
        sem = self.sems[sem_name]
        self.ops[eng].append(lambda e, sem=sem, val=val: e.wait_ge(sem, val))

    def wait_if(self, eng, key):
        if isinstance(key, list):
            for k in key:
                self.wait_if(eng, k)
            return
        if key is not None and key in self.keys:
            self.wait(eng, key)


def build(n_sub=N_SUB):
    nc = bass.Bass("TRN2", target_bir_lowering=False)

    def din(name, shape, dtype=F32):
        return nc.dram_tensor(name, shape, dtype, kind="ExternalInput").ap()

    x_in = din("x", [T, D])
    wq_a = din("wq_a", [D, D]); wk_a = din("wk_a", [D, D])
    wv_a = din("wv_a", [D, D]); wo_a = din("wo_a", [D, D])
    bq_a = din("bq_a", [1, D]); bk_a = din("bk_a", [1, D])
    bv_a = din("bv_a", [1, D]); bo_a = din("bo_a", [1, D])
    wq_b = din("wq_b", [D, D]); wk_b = din("wk_b", [D, D])
    wv_b = din("wv_b", [D, D]); wo_b = din("wo_b", [D, D])
    gam_t = din("gam_t", [128, D])
    bet_row_i = din("bet_row", [1, D])
    w1t = din("w1t", [NHC, 128, D])
    w3t = din("w3t", [NHC, 128, D])
    w2t = din("w2t", [HID, D])
    ident_c = din("ident_c", [128, 128], BF16)
    triu_c = din("triu_c", [128, 128], BF16)
    h128_c = din("h128_c", [128, 128], BF16)
    h64_c = din("h64_c", [64, 64], BF16)

    out_t = nc.dram_tensor("out", [T, D], F32, kind="ExternalOutput").ap()

    res_a = nc.dram_tensor("res_a", [T, D], F32, kind="Internal").ap()
    res_b = nc.dram_tensor("res_b", [T, D], F32, kind="Internal").ap()
    qs_s = nc.dram_tensor("qs_s", [NTC, 128, D], BF16, kind="Internal").ap()
    ks_s = nc.dram_tensor("ks_s", [NTC, 128, D], BF16, kind="Internal").ap()
    kts_s = nc.dram_tensor("kts_s", [NTC, 128, D], BF16, kind="Internal").ap()
    os_s = nc.dram_tensor("os_s", [NTC, 128, D], BF16, kind="Internal").ap()
    b_s = nc.dram_tensor("b_s", [64, 128, D], BF16, kind="Internal").ap()
    w2p_s = nc.dram_tensor("w2p_s", [NHC, 128, D], BF16, kind="Internal").ap()
    w1c_s = nc.dram_tensor("w1c_s", [NHC, 128, D], BF16, kind="Internal").ap()
    w3c_s = nc.dram_tensor("w3c_s", [NHC, 128, D], BF16, kind="Internal").ap()

    res_seq = [x_in, res_a, res_b, res_a, out_t]
    if n_sub < 4:
        res_seq[n_sub] = out_t

    with ExitStack() as es:
        E = es.enter_context
        sem_names = (["pe", "ve", "ac", "gc", "gw", "gwb", "sw2",
                      "sc10", "sc11", "sc30", "sc31", "sl"]
                     + [f"da{i}" for i in range(4)]
                     + [f"ga{i}" for i in range(4)]
                     + [f"db{i}" for i in range(2)]
                     + [f"dc{i}" for i in range(2)]
                     + [f"dd{i}" for i in range(4)]
                     + [f"dg{i}" for i in range(4)]
                     + [f"de{i}" for i in range(2)]
                     + [f"df{i}" for i in range(4)])
        sems = {n: E(nc.semaphore(n)) for n in sem_names}
        prog = Prog(nc, sems)

        ident = E(nc.sbuf_tensor("ident", [128, 128], BF16))
        triu = E(nc.sbuf_tensor("triu", [128, 128], BF16))
        h128 = E(nc.sbuf_tensor("h128", [128, 128], BF16))
        h64 = E(nc.sbuf_tensor("h64", [64, 64], BF16))
        ones_row = E(nc.sbuf_tensor("ones_row", [1, 128], BF16))
        cs = E(nc.sbuf_tensor("cs", [128, 104], F32))
        zcol = E(nc.sbuf_tensor("zcol", [128, 8], F32))
        cbias = E(nc.sbuf_tensor("cbias", [128, 2], F32))

        def col(t_, a, n):
            return t_[:, a:a + n]

        SC = {}

        def emit_const_loads():
            prog.dma("sy", ident[:, :], ident_c[:, :], "sl")
            prog.dma("sy", triu[:, :], triu_c[:, :], "sl")
            prog.dma("sy", h128[:, :], h128_c[:, :], "sl")
            prog.dma("sy", h64[:, :], h64_c[:, :], "sl", key="c_all")
            prog.op("gp", lambda e: e.memset(ones_row[:, :], 1.0), inc="gc", key="c_ones")
            prog.op("gp", lambda e: e.memset(cbias[:, 0:1], LN_EPS))
            prog.op("gp", lambda e: e.memset(cbias[:, 1:2], math.log(RD)),
                    inc="gc", key="c_eps")
            prog.wait("pe", "c_all")
            prog.wait("pe", "c_ones")
            prog.wait("ve", "c_all")
            prog.wait("ac", "c_eps")

        def emit_xb(st, res_src):
            ring_rt, ring_xn, ring_pt = SC["ring_rt"], SC["ring_xn"], SC["ring_pt"]
            sqbf = SC["sqbf"]
            xb = SC["xb"]
            for tc in range(NTC):
                par = tc % 2
                base = 0 if par == 0 else 96
                ss, sq, mu, msq, var_, lnv, rstd = (col(cs, base + i, 1) for i in range(7))
                rs, rt, rwar = ring_rt.acquire()
                prog.wait_if("sy", rwar)
                prog.dma("sy", rt[:, :], res_src[tc * 128:(tc + 1) * 128, :], f"db{rs}",
                         key=(st, "xr", tc))
                prog.wait("ve", (st, "xr", tc))
                prog.wait_if("ve", SC.get(("xb_war", par)))
                prog.op("ve", lambda e, rt=rt, ss=ss: e.reduce_sum(
                    ss, rt[:, :], axis=mybir.AxisListType.X))
                prog.op("ve", lambda e, rt=rt, sq=sq: e.scalar_tensor_tensor(
                    sqbf[:, :], rt[:, :], 1.0, rt[:, :], ALU.bypass, ALU.mult,
                    accum_out=sq))
                prog.op("ve", lambda e, mu=mu, ss=ss: e.tensor_scalar_mul(mu, ss, 1.0 / D))
                prog.op("ve", lambda e, msq=msq, mu=mu: e.scalar_tensor_tensor(
                    msq, mu, 1.0, mu, ALU.bypass, ALU.mult))
                prog.op("ve", lambda e, var_=var_, sq=sq, msq=msq: e.tensor_scalar(
                    var_, sq, 1.0 / D, msq, ALU.mult, ALU.subtract),
                    inc="ve", key=(st, "xv", tc))
                prog.wait("ac", (st, "xv", tc))
                prog.wait_if("ac", SC.get(("xb_rwar", par)))
                prog.op("ac", lambda e, lnv=lnv, var_=var_: e.activation(
                    lnv, var_, AF.Ln, bias=cbias[:, 0:1]))
                prog.op("ac", lambda e, rstd=rstd, lnv=lnv: e.activation(
                    rstd, lnv, AF.Exp, scale=-0.5),
                    inc="ac", key=(st, "xr2", tc))
                prog.wait("ve", (st, "xr2", tc))
                lastn = None
                for half in range(2):
                    xs_, xn, xwar = ring_xn.acquire()
                    prog.wait_if("ve", xwar)
                    prog.op("ve", lambda e, rt=rt, xn=xn, half=half, mu=mu, rstd=rstd: e.tensor_scalar(
                        xn[:, :], rt[:, half * 1024:(half + 1) * 1024], mu, rstd,
                        ALU.subtract, ALU.mult),
                        inc="ve", key=(st, "xn", tc, half))
                    lastn = (st, "xn", tc, half)
                    prog.wait("pe", (st, "xn", tc, half))
                    for dc8 in range(8):
                        dc = half * 8 + dc8
                        ps_, pt, pwar = ring_pt.acquire()
                        prog.wait_if("pe", pwar)
                        prog.op("pe", lambda e, pt=pt, xn=xn, dc8=dc8: e.transpose(
                            pt[:, :], xn[:, dc8 * 128:(dc8 + 1) * 128], ident[:, :]),
                            inc="pe", key=(st, "xt", tc, dc))
                        prog.wait("ve", (st, "xt", tc, dc))
                        prog.op("ve", lambda e, pt=pt, dc=dc, tc=tc: e.tensor_copy(
                            xb[:, dc * T + tc * 128: dc * T + (tc + 1) * 128], pt[:, :]),
                            inc="ve", key=(st, "xe", tc, dc))
                        ring_pt.consumer(ps_, (st, "xe", tc, dc))
                    ring_xn.consumer(xs_, (st, "xt", tc, half * 8 + 7))
                SC[("xb_war", par)] = lastn
                SC[("xb_rwar", par)] = lastn
                ring_rt.consumer(rs, lastn)

        def emit_proj(st, tag, wsrc, bias_row, evac, scale_tile=None):
            ring_pp = SC["ring_pp"]
            wbufH = SC["wbufH"]
            xb = SC["xb"]
            for dh in range(2):
                for cc in range(NCC):
                    if cc == 0:
                        if dh == 0:
                            prog.wait_if("gp", SC.get("wload_war_" + tag))
                        else:
                            prog.wait("gp", (st, "pmm", tag, 0, NTC - 1))
                    prog.dma("gp", wbufH[:, cc * 1024:(cc + 1) * 1024],
                             wsrc[cc * 128:(cc + 1) * 128, dh * 1024:(dh + 1) * 1024],
                             "gw", key=(st, "w", tag, dh, cc))
                if scale_tile is not None:
                    prog.wait("ve", (st, "w", tag, dh, NCC - 1))
                    for cc in range(NCC):
                        prog.op("ve", lambda e, cc=cc, dh=dh, wbufH=wbufH, scale_tile=scale_tile: e.tensor_tensor(
                            wbufH[:, cc * 1024:(cc + 1) * 1024],
                            wbufH[:, cc * 1024:(cc + 1) * 1024],
                            scale_tile[:, dh * 1024:(dh + 1) * 1024], ALU.mult),
                            inc="ve", key=(st, "wsc", tag, dh, cc))
                for tc in range(NTC):
                    ps_, pp, pwar = ring_pp.acquire()
                    prog.wait_if("pe", pwar)
                    if tag == "q" and dh == 0:
                        prog.wait("pe", (st, "xe", tc, NCC - 1))
                    if tc == 0:
                        if scale_tile is not None:
                            prog.wait("pe", (st, "wsc", tag, dh, NCC - 1))
                        else:
                            prog.wait("pe", (st, "w", tag, dh, NCC - 1))
                    for cc in range(NCC):
                        last = (cc == NCC - 1) and bias_row is None
                        for nq in range(2):
                            prog.op("pe", lambda e, pp=pp, cc=cc, nq=nq, tc=tc, wbufH=wbufH, last=last: e.matmul(
                                pp[:, nq * 512:(nq + 1) * 512],
                                xb[:, cc * T + tc * 128: cc * T + (tc + 1) * 128],
                                wbufH[:, cc * 1024 + nq * 512: cc * 1024 + (nq + 1) * 512],
                                start=(cc == 0), stop=last),
                                inc=None if not (last and nq == 1) else "pe",
                                key=None if not (last and nq == 1) else (st, "pmm", tag, dh, tc))
                    if bias_row is not None:
                        for nq in range(2):
                            prog.op("pe", lambda e, pp=pp, nq=nq, dh=dh, bias_row=bias_row: e.matmul(
                                pp[:, nq * 512:(nq + 1) * 512],
                                ones_row[0:1, :],
                                bias_row[0:1, dh * 1024 + nq * 512: dh * 1024 + (nq + 1) * 512],
                                start=False, stop=True),
                                inc=None if nq == 0 else "pe",
                                key=None if nq == 0 else (st, "pmm", tag, dh, tc))
                    if SC.get("_pend_evac") is not None:
                        SC["_pend_evac"]()
                    lastkey, dt_ = evac(st, tag, dh, tc, pp)
                    ring_pp.consumer(ps_, lastkey)
                    SC["_pend_evac"] = dt_
            if SC.get("_pend_evac") is not None:
                SC["_pend_evac"]()
                SC["_pend_evac"] = None
            SC["_pend_evac"] = None

        def mk_evac_qk(spill, fm, spill_tok=None):
            def evac(st, tag, dh, tc, pp):
                ring_xn, ring_pt, ring_st = SC["ring_xn"], SC["ring_pt"], SC["ring_st"]
                xs_, xn, xwar = ring_xn.acquire()
                prog.wait_if("ve", xwar)
                last_key = fm(st, tag, dh, tc, pp, xn)
                tok_key = None
                if spill_tok is not None:
                    prog.wait("sy", (st, "fmv", tag, dh, tc, 0))
                    prog.dma("sy", spill_tok[tc, :, dh * 1024:(dh + 1) * 1024],
                             xn[:, :], f"dd{(tc * 2 + dh) % 4}",
                             key=(st, "ksp", tc, dh))
                    tok_key = (st, "ksp", tc, dh)

                def do_transposes(xn=xn, xs_=xs_, tag=tag, dh=dh, tc=tc,
                                  tok_key=tok_key):
                    ts_, stg, swar = ring_st.acquire()
                    for h8 in range(8):
                        ps_, pt, pwar = ring_pt.acquire()
                        prog.wait_if("pe", pwar)
                        prog.wait("pe", (st, "fmv", tag, dh, tc, h8))
                        prog.op("pe", lambda e, pt=pt, xn=xn, h8=h8: e.transpose(
                            pt[:, :], xn[:, h8 * 128:(h8 + 1) * 128], ident[:, :]),
                            inc="pe", key=(st, "qt", tag, dh, tc, h8))
                        prog.wait("ve", (st, "qt", tag, dh, tc, h8))
                        if h8 == 0:
                            prog.wait_if("ve", swar)
                        prog.op("ve", lambda e, pt=pt, stg=stg, h8=h8: e.tensor_copy(
                            stg[:, h8 * 128:(h8 + 1) * 128], pt[:, :]),
                            inc="ve", key=(st, "qe", tag, dh, tc, h8))
                        ring_pt.consumer(ps_, (st, "qe", tag, dh, tc, h8))
                    xn_wars = [(st, "qt", tag, dh, tc, 7)]
                    if tok_key is not None:
                        xn_wars.append(tok_key)
                    ring_xn.consumer(xs_, xn_wars)
                    prog.wait("sy", (st, "qe", tag, dh, tc, 7))
                    prog.dma("sy", spill[tc, :, dh * 1024:(dh + 1) * 1024], stg[:, :],
                             f"de{ts_}", key=(st, "qsp", tag, tc, dh))
                    ring_st.consumer(ts_, (st, "qsp", tag, tc, dh))
                return last_key, do_transposes
            return evac

        def fm_elu(st, tag, dh, tc, pp, xn):
            par = (dh * NTC + tc) % 2
            tf = SC["scr_c"][:, par * 1024:(par + 1) * 1024]
            ebf = SC["ybf"][:, par * 1024:(par + 1) * 1024]
            prog.wait("ac", (st, "pmm", tag, dh, tc))
            prog.wait_if("ac", SC.get(("elu_war", par)))
            prog.op("ac", lambda e, pp=pp, tf=tf: e.copy(tf, pp[:, :]),
                    inc="ac", key=(st, "fmd", tag, dh, tc))
            prog.wait("ve", (st, "fmd", tag, dh, tc))
            prog.op("ve", lambda e, xn=xn, tf=tf: e.tensor_scalar_min(xn[:, :], tf, 0.0),
                    inc="ve", key=(st, "fmm", tag, dh, tc))
            prog.wait("ac", (st, "fmm", tag, dh, tc))
            prog.op("ac", lambda e, xn=xn, ebf=ebf: e.activation(ebf, xn[:, :], AF.Exp),
                    inc="ac", key=(st, "fme", tag, dh, tc))
            prog.wait("ve", (st, "fme", tag, dh, tc))
            prog.op("ve", lambda e, pp=pp, xn=xn, tf=tf, ebf=ebf: e.scalar_tensor_tensor(
                xn[:, :], tf, 0.0, ebf, ALU.max, ALU.add),
                inc="ve", key=(st, "fmn", tag, dh, tc))
            SC[("elu_war", par)] = (st, "fmn", tag, dh, tc)
            for hh in range(8):
                prog.alias((st, "fmv", tag, dh, tc, hh), (st, "fmn", tag, dh, tc))
            return (st, "fmd", tag, dh, tc)

        def fm_lnb(st, tag, dh, tc, pp, xn):
            par = (dh * NTC + tc) % 2
            ybf = SC["ybf"][:, par * 1024:(par + 1) * 1024]
            y2bf = SC["y2bf"][:, par * 1024:(par + 1) * 1024]
            base = 8 + par * 40
            seg = col(cs, base, 8); seg2 = col(cs, base + 8, 8)
            muh = col(cs, base + 16, 8); rstdh = col(cs, base + 24, 8)
            varh = col(cs, base + 32, 8)
            prog.wait("ac", (st, "pmm", tag, dh, tc))
            prog.wait_if("ac", SC.get(("lnb_y_war", par)))
            prog.op("ac", lambda e, pp=pp, ybf=ybf: e.copy(ybf, pp[:, :]),
                    inc="ac", key=(st, "fmd", tag, dh, tc))
            prog.wait_if("ac", SC.get(("lnb_y2_war", par)))
            prog.op("ac", lambda e, ybf=ybf, y2bf=y2bf: e.activation(y2bf, ybf, AF.Square),
                    inc="ac", key=(st, "fsq", tag, dh, tc))
            y3 = ybf.rearrange("p (s f) -> p s f", f=128)
            y23 = y2bf.rearrange("p (s f) -> p s f", f=128)
            prog.wait("ve", (st, "fmd", tag, dh, tc))
            prog.op("ve", lambda e, y3=y3, seg=seg: e.reduce_sum(seg, y3, axis=mybir.AxisListType.X))
            prog.wait("ve", (st, "fsq", tag, dh, tc))
            prog.op("ve", lambda e, y23=y23, seg2=seg2: e.reduce_sum(seg2, y23, axis=mybir.AxisListType.X),
                    inc="ve", key=(st, "fr2", tag, dh, tc))
            SC[("lnb_y2_war", par)] = (st, "fr2", tag, dh, tc)
            prog.op("ve", lambda e, muh=muh, seg=seg: e.tensor_scalar_mul(muh, seg, 1.0 / HD))
            prog.op("ve", lambda e, seg=seg, muh=muh: e.scalar_tensor_tensor(
                seg, muh, 1.0, muh, ALU.bypass, ALU.mult))
            prog.op("ve", lambda e, varh=varh, seg2=seg2, seg=seg: e.scalar_tensor_tensor(
                varh, seg2, 1.0 / HD, seg, ALU.mult, ALU.subtract),
                inc="ve", key=(st, "fv1", tag, dh, tc))
            prog.wait("ac", (st, "fv1", tag, dh, tc))
            prog.wait_if("ac", SC.get(("lnb_r_war", par)))
            prog.op("ac", lambda e, varh=varh: e.activation(varh, varh, AF.Ln, bias=cbias[:, 0:1]))
            if tag == "q":
                prog.op("ac", lambda e, rstdh=rstdh, varh=varh: e.activation(
                    rstdh, varh, AF.Exp, scale=-0.5, bias=cbias[:, 1:2]),
                    inc="ac", key=(st, "fmr", tag, dh, tc))
            else:
                prog.op("ac", lambda e, rstdh=rstdh, varh=varh: e.activation(
                    rstdh, varh, AF.Exp, scale=-0.5),
                    inc="ac", key=(st, "fmr", tag, dh, tc))
            prog.wait("ve", (st, "fmr", tag, dh, tc))
            keys = []
            for h8 in range(8):
                prog.op("ve", lambda e, h8=h8, xn=xn, ybf=ybf, muh=muh, rstdh=rstdh: e.tensor_scalar(
                    xn[:, h8 * 128:(h8 + 1) * 128],
                    ybf[:, h8 * 128:(h8 + 1) * 128],
                    muh[:, h8:h8 + 1], rstdh[:, h8:h8 + 1],
                    ALU.subtract, ALU.mult),
                    inc="ve", key=(st, "fnm", tag, dh, tc, h8))
                keys.append((st, "fnm", tag, dh, tc, h8))
            SC[("lnb_r_war", par)] = keys[-1]
            SC[("lnb_y_war", par)] = keys[-1]
            for h8 in range(8):
                prog.alias((st, "fmv", tag, dh, tc, h8), keys[h8])
            return (st, "fmd", tag, dh, tc)

        def mk_evac_v(st_v):
            vaug = SC["vaug"]

            def evac(st, tag, dh, tc, pp):
                prog.wait("ac", (st, "pmm", tag, dh, tc))
                if dh == 0 and tc == 0:
                    prog.wait("ac", (st, "vones"))
                off = tc * 2064 + dh * 8 * 129
                dst = bass.AP(vaug, off, [[NTC * H * 129, 128], [129, 8], [1, 128]])
                prog.op("ac", lambda e, pp=pp, dst=dst: e.copy(dst, pp[:, :]),
                        inc="ac", key=(st, "vev", dh, tc))
                return (st, "vev", dh, tc), None
            return evac

        def emit_attn_phaseA(st, kind):
            is_a = kind == "attn_a"
            if is_a:
                brow = SC["brow"]
                prog.dma("gp", brow[0:1, 0:D], bq_a[0:1, :], "gwb")
                prog.dma("gp", brow[0:1, D:2 * D], bk_a[0:1, :], "gwb", key=(st, "bqk"))
                prog.wait("pe", (st, "bqk"))
                emit_proj(st, "q", wq_a, brow[:, 0:D], mk_evac_qk(qs_s, fm_elu))
                SC["wload_war_k"] = (st, "pmm", "q", 1, NTC - 1)
                emit_proj(st, "k", wk_a, brow[:, D:2 * D],
                          mk_evac_qk(ks_s, fm_elu, spill_tok=kts_s))
            else:
                brow = SC["brow"]
                prog.dma("gp", SC["gamt"][:, :], gam_t[:, :], "gwb")
                prog.dma("gp", brow[0:1, 0:D], bet_row_i[0:1, :], "gwb", key=(st, "gbt"))
                prog.wait("ve", (st, "gbt"))
                prog.wait("pe", (st, "gbt"))
                emit_proj(st, "q", wq_b, brow[:, 0:D], mk_evac_qk(qs_s, fm_lnb),
                          scale_tile=SC["gamt"])
                SC["wload_war_k"] = (st, "pmm", "q", 1, NTC - 1)
                emit_proj(st, "k", wk_b, brow[:, 0:D], mk_evac_qk(ks_s, fm_lnb),
                          scale_tile=SC["gamt"])

        def emit_attn_v(st, kind):
            is_a = kind == "attn_a"
            vaug = SC["vaug"]
            va_ones = vaug[:, :].rearrange("p (t h c) -> p t h c", h=H, c=129)[:, :, :, 128:129]
            prog.op("ve", lambda e: e.memset(va_ones, 1.0), inc="ve", key=(st, "vones"))
            SC["wload_war_v"] = (st, "pmm", "k", 1, NTC - 1)
            SC["wload_war_" + "v"] = (st, "pmm", "k", 1, NTC - 1)
            if is_a:
                brow = SC["brow"]
                prog.wait("gp", (st, "pmm", "q", 1, NTC - 1))
                prog.dma("gp", brow[0:1, 0:D], bv_a[0:1, :], "gwb", key=(st, "bv"))
                prog.wait("pe", (st, "bv"))
                emit_proj(st, "v", wv_a, brow[:, 0:D], mk_evac_v(st))
            else:
                emit_proj(st, "v", wv_b, None, mk_evac_v(st))

        def emit_attn_a_phaseB(st, res_src, res_dst):
            psa2, psTb = SC["psa2"], SC["psTb"]
            pso2, kvp2 = SC["pso2"], SC["kvp2"]
            ppj = SC["ppj"]
            wbuf, vaug = SC["wbuf"], SC["vaug"]
            state, ostgs = SC["state"], SC["ostgs"]
            state_bf2, atbf, obf = SC["state_bf2"], SC["atbf"], SC["obf"]
            ring_rtB = SC["ring_rtB"]
            qblkb, kblkb, ktblk = SC["qblkb"], SC["kblkb"], SC["ktblk"]
            brow = SC["brow"]
            vdone = (st, "pmm", "v", 1, NTC - 1)
            GSZ = 3
            NG = (H + GSZ - 1) // GSZ
            CA_H = 8
            CA_C = (CA_H + 1) * 129
            prog.wait("gp", (st, "pmm", "k", 1, NTC - 1))
            prog.dma("gp", brow[0:1, D:2 * D], bo_a[0:1, :], "gwb", key=(st, "bo"))
            for gk in [vdone, (st, "qsp", "k", NTC - 2, 1), (st, "qsp", "k", NTC - 1, 1)]:
                prog.wait("gp", gk)
            for cc in range(NCC):
                prog.dma("gp", wbuf[:, cc * D:(cc + 1) * D],
                         wo_a[cc * 128:(cc + 1) * 128, :], "gw", key=(st, "wo", cc))
            guard_keys = [vdone, (st, "qe", "k", 1, NTC - 1, 7),
                          (st, "fmd", "k", 1, NTC - 1), (st, "vev", 1, NTC - 1),
                          (st, "qsp", "k", NTC - 2, 1), (st, "qsp", "k", NTC - 1, 1),
                          (st, "ksp", NTC - 2, 0), (st, "ksp", NTC - 2, 1),
                          (st, "ksp", NTC - 1, 0), (st, "ksp", NTC - 1, 1)]
            for geng in ("pe", "ve", "ac", "sy"):
                for gk in guard_keys:
                    prog.wait(geng, gk)
            prog.wait("pe", (st, "bo"))

            def emit_add(n, g):
                lo = g * GSZ * 129
                w_ = (min(GSZ, H - g * GSZ)) * 129
                prog.wait("ve", (st, "kv", n, min(g * GSZ + GSZ, H) - 1))
                if n >= 1:
                    prog.wait("ve", (st, "castA" if g <= 2 else "castB", n - 1))
                if n == 0:
                    prog.op("ve", lambda e, lo=lo, w_=w_, g=g: e.tensor_copy(
                        state[:, lo:lo + w_], kvp2[g % 2][:, 0:w_]),
                        inc="ve", key=(st, "sup", n, g))
                else:
                    prog.op("ve", lambda e, lo=lo, w_=w_, g=g: e.tensor_tensor(
                        state[:, lo:lo + w_], state[:, lo:lo + w_],
                        kvp2[g % 2][:, 0:w_], ALU.add),
                        inc="ve", key=(st, "sup", n, g))
                SC[("kvp_war", g % 2)] = (st, "sup", n, g)

            def emit_castA(n):
                prog.wait("ac", (st, "sup", n, 2))
                if n >= 1:
                    prog.wait("ac", (st, "ob", n - 1, CA_H))
                prog.op("ac", lambda e, n=n: e.copy(
                    state_bf2[n % 2][:, 0:CA_C], state[:, 0:CA_C]),
                    inc="ac", key=(st, "castA", n))

            def emit_castB(n):
                prog.wait("ac", (st, "sup", n, NG - 1))
                if n >= 1:
                    prog.wait("ac", (st, "ob", n - 1, H - 1))
                prog.op("ac", lambda e, n=n: e.copy(
                    state_bf2[n % 2][:, CA_C:H * 129], state[:, CA_C:H * 129]),
                    inc="ac", key=(st, "castB", n))

            def mk_pso(n, h, qb):
                hp = h % 2

                def pso():
                    voff = n * 2064 + h * 129
                    prog.wait_if("pe", SC.get(("a_pso", hp)))
                    prog.wait("pe", (st, "atm", n, h))
                    if n > 0:
                        prog.wait("pe", (st, "castA" if h <= CA_H else "castB", n - 1))
                        prog.op("pe", lambda e, qb=qb, h=h, hp=hp, n=n: e.matmul(
                            pso2[hp][:, 0:129], qb[:, h * 128:(h + 1) * 128],
                            state_bf2[(n - 1) % 2][:, h * 129:(h + 1) * 129],
                            start=True, stop=False))
                    prog.op("pe", lambda e, voff=voff, n=n, hp=hp: e.matmul(
                        pso2[hp][:, 0:129], atbf[:, hp * 128:(hp + 1) * 128],
                        vaug[:, voff:voff + 129],
                        start=(n == 0), stop=True), inc="pe", key=(st, "ob", n, h))
                    SC[("a_atb", hp)] = (st, "ob", n, h)
                    prog.wait("ve", (st, "ob", n, h))
                    prog.op("ve", lambda e, hp=hp: e.tensor_scalar_add(
                        zcol[:, hp * 2:hp * 2 + 1], pso2[hp][:, 128:129], EPS_A))
                    prog.op("ve", lambda e, hp=hp: e.reciprocal(
                        zcol[:, hp * 2 + 1:hp * 2 + 2], zcol[:, hp * 2:hp * 2 + 1]),
                        inc="ve", key=(st, "rz", n, h))
                    prog.wait("ac", (st, "rz", n, h))
                    prog.wait_if("ac", SC.get(("a_obf", hp)))
                    prog.op("ac", lambda e, hp=hp: e.activation(
                        obf[:, hp * 128:(hp + 1) * 128], pso2[hp][:, 0:128],
                        AF.Copy, scale=zcol[:, hp * 2 + 1:hp * 2 + 2]),
                        inc="ac", key=(st, "odv", n, h))
                    SC[("a_pso", hp)] = (st, "odv", n, h)
                return pso

            def mk_tail(n, h, ostg):
                hp = h % 2

                def tail():
                    prog.wait("pe", (st, "odv", n, h))
                    prog.wait_if("pe", SC.get("a_psTb"))
                    prog.op("pe", lambda e, hp=hp: e.transpose(
                        psTb[:, :], obf[:, hp * 128:(hp + 1) * 128], ident[:, :]),
                        inc="pe", key=(st, "ot", n, h))
                    SC[("a_obf", hp)] = (st, "ot", n, h)
                    prog.wait("ac", (st, "ot", n, h))
                    if h == 0:
                        if n >= 2:
                            prog.wait("ac", (st, "pj", n - 2, 3))
                        else:
                            prog.wait("ac", vdone)
                    prog.op("ac", lambda e, ostg=ostg, h=h: e.copy(
                        ostg[:, h * 128:(h + 1) * 128], psTb[:, :]),
                        inc="ac", key=(st, "oe", n, h))
                    SC["a_psTb"] = (st, "oe", n, h)
                    if h == 10 and n < NTC - 1:
                        emit_castA(n)
                return tail

            pend_p = [None]
            pend_t = [None, None]

            for n in range(NTC):
                last_n = n == NTC - 1
                for (bl, src_, nm, spk, pool) in ((qblkb[n % 2], qs_s, "qb", "q", "dc"),
                                                  (kblkb[n % 2], ks_s, "kb", "k", "dg"),):
                    if n >= 2:
                        prog.wait("sy", (st, "blku", nm, n - 2))
                    else:
                        prog.wait("sy", vdone)
                    prog.wait("sy", (st, "qsp", spk, n, 1))
                    prog.dma("sy", bl[:, :], src_[n, :, :], f"{pool}{n % 2}",
                             key=(st, nm, n))
                if not last_n:
                    if n >= 2:
                        prog.wait("sy", (st, "kv", n - 2, H - 1))
                    else:
                        prog.wait("sy", vdone)
                    prog.dma("sy", ktblk[n % 2][:, :], kts_s[n, :, :],
                             f"dg{2 + n % 2}", key=(st, "ktb", n))
                qb, kb = qblkb[n % 2], kblkb[n % 2]
                ostg = ostgs[n % 2]
                rs, rt, rwar = ring_rtB.acquire()
                if rwar is not None:
                    prog.wait("sy", rwar)
                else:
                    prog.wait("sy", vdone)
                prog.dma("sy", rt[:, :], res_src[n * 128:(n + 1) * 128, :], f"db{rs}",
                         key=(st, "rr", n))
                prog.wait("pe", (st, "kb", n))
                prog.wait("pe", (st, "qb", n))
                if not last_n:
                    prog.wait("pe", (st, "ktb", n))
                prog.wait("pe", (st, "vev", 0, n))
                prog.wait("pe", (st, "vev", 1, n))
                for h in range(H):
                    hp = h % 2
                    g = h // GSZ
                    prog.wait_if("pe", SC.get(("a_psa", hp)))
                    prog.op("pe", lambda e, kb=kb, qb=qb, h=h, hp=hp: e.matmul(
                        psa2[hp][:, :], kb[:, h * 128:(h + 1) * 128],
                        qb[:, h * 128:(h + 1) * 128], start=True, stop=True),
                        inc="pe", key=(st, "at", n, h))
                    if not last_n:
                        if h % GSZ == 0:
                            prog.wait_if("pe", SC.get(("kvp_war", g % 2)))
                        prog.op("pe", lambda e, h=h, g=g, n=n: e.matmul(
                            kvp2[g % 2][:, (h - g * GSZ) * 129:(h - g * GSZ + 1) * 129],
                            ktblk[n % 2][:, h * 128:(h + 1) * 128],
                            vaug[:, n * 2064 + h * 129: n * 2064 + (h + 1) * 129],
                            start=True, stop=True), inc="pe", key=(st, "kv", n, h))
                    prog.wait("ve", (st, "at", n, h))
                    prog.wait_if("ve", SC.get(("a_atb", hp)))
                    prog.op("ve", lambda e, hp=hp: e.tensor_tensor(
                        atbf[:, hp * 128:(hp + 1) * 128], psa2[hp][:, :],
                        triu[:, :], ALU.mult),
                        inc="ve", key=(st, "atm", n, h))
                    SC[("a_psa", hp)] = (st, "atm", n, h)
                    if not last_n and (h % GSZ == GSZ - 1 or h == H - 1):
                        emit_add(n, g)
                    if pend_p[0] is not None:
                        pend_p[0]()
                    pend_p[0] = mk_pso(n, h, qb)
                    if pend_t[0] is not None:
                        pend_t[0]()
                    pend_t[0] = pend_t[1]
                    pend_t[1] = mk_tail(n, h, ostg)
                pend_p[0]()
                pend_p[0] = None
                pend_t[0]()
                pend_t[1]()
                pend_t[0] = pend_t[1] = None
                if not last_n:
                    emit_castB(n)
                prog.alias((st, "blku", "qb", n), (st, "ob", n, H - 1))
                prog.alias((st, "blku", "kb", n), (st, "at", n, H - 1))
                prog.wait("pe", (st, "oe", n, H - 1))
                if n == 0:
                    prog.wait("pe", (st, "wo", NCC - 1))
                for pc in range(4):
                    if n >= 1 or pc >= 1:
                        pn_, pcc = (n, pc - 1) if pc >= 1 else (n - 1, 3)
                        prog.wait("pe", (st, "rad", pn_, pcc))
                    for hc in range(NCC):
                        prog.op("pe", lambda e, hc=hc, pc=pc, ostg=ostg: e.matmul(
                            ppj[:, :],
                            ostg[:, hc * 128:(hc + 1) * 128],
                            wbuf[:, hc * D + pc * 512: hc * D + (pc + 1) * 512],
                            start=(hc == 0), stop=False))
                    prog.op("pe", lambda e, pc=pc: e.matmul(
                        ppj[:, :], ones_row[0:1, :],
                        brow[0:1, D + pc * 512: D + (pc + 1) * 512],
                        start=False, stop=True),
                        inc="pe", key=(st, "pj", n, pc))
                    prog.wait("ve", (st, "pj", n, pc))
                    if pc == 0:
                        prog.wait("ve", (st, "rr", n))
                    prog.op("ve", lambda e, rt=rt, pc=pc: e.tensor_tensor(
                        rt[:, pc * 512:(pc + 1) * 512], rt[:, pc * 512:(pc + 1) * 512],
                        ppj[:, :], ALU.add),
                        inc="ve", key=(st, "rad", n, pc))
                prog.wait("sy", (st, "rad", n, 3))
                prog.dma("sy", res_dst[n * 128:(n + 1) * 128, :], rt[:, :], f"df{rs}",
                         key=(st, "rst", n))
                ring_rtB.consumer(rs, (st, "rst", n))

        def emit_w2prep(st):
            pps2 = SC["pps2"]
            wts = SC["p_wt"]
            stg2 = SC["p_stg"]
            for a in range(64):
                wt = wts[a % 4]
                pq = pps2[(a % 2) * 4:(a % 2) * 4 + 4]
                if a >= 4:
                    prog.wait("gp", (st, "s1m", a - 4))
                prog.dma("gp", wt[:, :], w2t[a * 128:(a + 1) * 128, :], f"da{a % 4}",
                         key=(st, "s1w", a))
                if a >= 2:
                    prog.wait("pe", (st, "s1e", a - 2))
                prog.wait("pe", (st, "s1w", a))
                for nq in range(4):
                    prog.op("pe", lambda e, wt=wt, nq=nq, pq=pq: e.matmul(
                        pq[nq][:, :], h128[:, :],
                        wt[:, nq * 512:(nq + 1) * 512], start=True, stop=True),
                        inc=None if nq < 3 else "pe",
                        key=None if nq < 3 else (st, "s1m", a))
                ee = "ac" if a % 2 == 0 else "ve"
                prog.wait(ee, (st, "s1m", a))
                bt = stg2[a % 4]
                if a >= 4:
                    prog.wait(ee, (st, "s1s", a - 4))
                for nq in range(4):
                    if ee == "ac":
                        prog.op("ac", lambda e, bt=bt, nq=nq, pq=pq: e.mul(
                            bt[:, nq * 512:(nq + 1) * 512], pq[nq][:, :], FWHT_SCALE),
                            inc=None if nq < 3 else "ac",
                            key=None if nq < 3 else (st, "s1e", a))
                    else:
                        prog.op("ve", lambda e, bt=bt, nq=nq, pq=pq: e.tensor_scalar_mul(
                            bt[:, nq * 512:(nq + 1) * 512], pq[nq][:, :], FWHT_SCALE),
                            inc=None if nq < 3 else "ve",
                            key=None if nq < 3 else (st, "s1e", a))
                prog.wait("sy", (st, "s1e", a))
                prog.dma("sy", b_s[a, :, :], bt[:, :],
                         ["de0", "de1", "dc0", "dc1"][a % 4], key=(st, "s1s", a))
            for j in range(128):
                bj = wts[j % 4]
                pq = pps2[(j % 2) * 4:(j % 2) * 4 + 4]
                if j >= 4:
                    prog.wait("sy", (st, "s2m", j - 4))
                if j < 4:
                    for aa in range(60, 64):
                        prog.wait("sy", (st, "s1s", aa))
                prog.dma("sy", bj[0:64, :], b_s[:, j, :], f"dg{j % 4}", key=(st, "s2l", j))
                if j >= 2:
                    prog.wait("pe", (st, "s2e", j - 2))
                prog.wait("pe", (st, "s2l", j))
                for nq in range(4):
                    prog.op("pe", lambda e, bj=bj, nq=nq, pq=pq: e.matmul(
                        pq[nq][0:64, :], h64[:, :],
                        bj[0:64, nq * 512:(nq + 1) * 512], start=True, stop=True),
                        inc=None if nq < 3 else "pe",
                        key=None if nq < 3 else (st, "s2m", j))
                ee = "ac" if j % 2 == 0 else "ve"
                prog.wait(ee, (st, "s2m", j))
                ct = stg2[j % 4]
                if j >= 4:
                    prog.wait(ee, (st, "s2s", j - 4))
                for nq in range(4):
                    if ee == "ac":
                        prog.op("ac", lambda e, ct=ct, nq=nq, pq=pq: e.copy(
                            ct[0:64, nq * 512:(nq + 1) * 512], pq[nq][0:64, :]),
                            inc=None if nq < 3 else "ac",
                            key=None if nq < 3 else (st, "s2e", j))
                    else:
                        prog.op("ve", lambda e, ct=ct, nq=nq, pq=pq: e.tensor_copy(
                            ct[0:64, nq * 512:(nq + 1) * 512], pq[nq][0:64, :]),
                            inc=None if nq < 3 else "ve",
                            key=None if nq < 3 else (st, "s2e", j))
                prog.wait("gp", (st, "s2e", j))
                prog.dma("gp", w2p_s[:, j, :], ct[0:64, :],
                         ["ga0", "ga1", "ga2", "ga3"][j % 4], key=(st, "s2s", j))

        def emit_ffn(st, res_src, res_dst, first_pass):
            psG, psL = SC["psG"], SC["psL"]
            ppcs = SC["ppcs"]
            acc, sg = SC["acc"], SC["sg"]
            w12t, w34t = SC["w12t"], SC["w34t"]
            hgt, w2gt = SC["hgt"], SC["w2gt"]
            ring_rtF = SC["ring_rtF"]
            xb = SC["xb"]
            xb_done = (st, "xe", NTC - 1, NCC - 1)
            xn_done = xb_done
            for half in range(2):
                for hc in range(NHC):
                    g, gi = hc // G, hc % G
                    w1tile, w3tile = w12t[hc % 2], w34t[hc % 2]
                    war = None
                    if hc >= 2:
                        war = (st, "lmm", half, hc - 2)
                    elif half == 1:
                        war = (st, "lmm", 0, NHC - 2 + hc)
                    else:
                        war = xn_done
                    prog.wait("gp", war)
                    prog.dma("gp", w1tile[:, :], w1t[hc, :, :], f"da{hc % 2}",
                             key=(st, "w1", half, hc))
                    prog.dma("gp", w3tile[:, :], w3t[hc, :, :], f"ga{hc % 2}",
                             key=(st, "w3", half, hc))
                    w2war = None
                    if g >= 1:
                        w2war = (st, "cdone", half, g - 1)
                    elif half == 1:
                        w2war = (st, "cdone", 0, NGR - 1)
                    else:
                        w2war = xn_done
                    prog.wait("sy", w2war)
                    prog.dma("sy", w2gt[gi][:, :], w2p_s[hc, :, :], "sw2",
                             key=(st, "w2", half, hc))
                    prog.wait("pe", (st, "w1", half, hc))
                    if hc == 0 and half == 0:
                        prog.wait("pe", xb_done)
                    if hc >= 1:
                        prog.wait("pe", (st, "sg", half, hc - 1))
                    elif half == 1:
                        prog.wait("pe", (st, "sg", 0, NHC - 1))
                    for cc in range(NCC):
                        for nq in range(2):
                            prog.op("pe", lambda e, cc=cc, nq=nq, half=half, w1tile=w1tile: e.matmul(
                                psG[:, nq * 512:(nq + 1) * 512],
                                w1tile[:, cc * 128:(cc + 1) * 128],
                                xb[:, cc * T + half * 1024 + nq * 512:
                                   cc * T + half * 1024 + (nq + 1) * 512],
                                start=(cc == 0), stop=(cc == NCC - 1)),
                                inc=None if (cc, nq) != (NCC - 1, 1) else "pe",
                                key=None if (cc, nq) != (NCC - 1, 1) else (st, "gmm", half, hc))
                    prog.wait("ac", (st, "gmm", half, hc))
                    hp = None
                    if hc >= 1:
                        hp = (half, hc - 1)
                    elif half == 1:
                        hp = (0, NHC - 1)
                    if hp is not None:
                        prog.wait("ac", (st, "hpr") + hp)
                    prog.op("ac", lambda e: e.activation(sg[:, :], psG[:, :], AF.Silu),
                            inc="ac", key=(st, "sg", half, hc))
                    prog.wait("pe", (st, "w3", half, hc))
                    if hp is not None:
                        prog.wait("pe", (st, "hpr") + hp)
                    for cc in range(NCC):
                        for nq in range(2):
                            prog.op("pe", lambda e, cc=cc, nq=nq, half=half, w3tile=w3tile: e.matmul(
                                psL[:, nq * 512:(nq + 1) * 512],
                                w3tile[:, cc * 128:(cc + 1) * 128],
                                xb[:, cc * T + half * 1024 + nq * 512:
                                   cc * T + half * 1024 + (nq + 1) * 512],
                                start=(cc == 0), stop=(cc == NCC - 1)),
                                inc=None if (cc, nq) != (NCC - 1, 1) else "pe",
                                key=None if (cc, nq) != (NCC - 1, 1) else (st, "lmm", half, hc))
                    prog.wait("ve", (st, "lmm", half, hc))
                    prog.wait("ve", (st, "sg", half, hc))
                    hwar = None
                    if g >= 1:
                        hwar = (st, "cdone", half, g - 1)
                    elif half == 1:
                        hwar = (st, "cdone", 0, NGR - 1)
                    else:
                        hwar = xn_done
                    if gi == 0 or (g == 0 and half == 0):
                        prog.wait("ve", hwar)
                    prog.op("ve", lambda e, htile=hgt[gi]: e.scalar_tensor_tensor(
                        htile[:, :], psL[:, :], 1.0, sg[:, :], ALU.bypass, ALU.mult),
                        inc="ve", key=(st, "hpr", half, hc))
                    if gi == G - 1:
                        prog.wait("pe", (st, "hpr", half, hc))
                        prog.wait("pe", (st, "w2", half, hc))
                        for tcl in range(8):
                            for dh in range(2):
                                ppc = ppcs[dh]
                                cwar = None
                                if tcl >= 1:
                                    cwar = (st, "cad", half, g, tcl - 1, dh)
                                elif g >= 1:
                                    cwar = (st, "cad", half, g - 1, 7, dh)
                                elif half == 1:
                                    cwar = (st, "cad", 0, NGR - 1, 7, dh)
                                if cwar is not None:
                                    prog.wait("pe", cwar)
                                for i in range(G):
                                    for nq in range(2):
                                        prog.op("pe", lambda e, ppc=ppc, i=i, nq=nq, tcl=tcl, dh=dh: e.matmul(
                                            ppc[:, nq * 512:(nq + 1) * 512],
                                            hgt[i][:, tcl * 128:(tcl + 1) * 128],
                                            w2gt[i][:, dh * 1024 + nq * 512:
                                                    dh * 1024 + (nq + 1) * 512],
                                            start=(i == 0), stop=(i == G - 1)),
                                            inc=None if (i, nq) != (G - 1, 1) else "pe",
                                            key=None if (i, nq) != (G - 1, 1) else (st, "cmm", half, g, tcl, dh))
                                prog.wait("ve", (st, "cmm", half, g, tcl, dh))
                                dsto = tcl * D + dh * 1024
                                tc_g = half * 8 + tcl
                                if g == 0:
                                    rs, rt, rwar = ring_rtF.acquire()
                                    prog.wait_if("sy", rwar)
                                    if half == 0 and g == 0 and tcl < 2:
                                        prog.wait("sy", xn_done)
                                    prog.dma("sy", rt[:, :],
                                             res_src[tc_g * 128:(tc_g + 1) * 128,
                                                     dh * 1024:(dh + 1) * 1024],
                                             f"db{rs}", key=(st, "frr", half, tcl, dh))
                                    prog.wait("ve", (st, "frr", half, tcl, dh))
                                    if half == 1:
                                        prog.wait("ve", (st, "fst", 0, tcl, dh))
                                    prog.op("ve", lambda e, ppc=ppc, dsto=dsto, rt=rt: e.tensor_tensor(
                                        acc[:, dsto:dsto + 1024], rt[:, :], ppc[:, :], ALU.add),
                                        inc="ve", key=(st, "cad", half, g, tcl, dh))
                                    ring_rtF.consumer(rs, (st, "cad", half, g, tcl, dh))
                                else:
                                    prog.op("ve", lambda e, ppc=ppc, dsto=dsto: e.tensor_tensor(
                                        acc[:, dsto:dsto + 1024], acc[:, dsto:dsto + 1024],
                                        ppc[:, :], ALU.add),
                                        inc="ve", key=(st, "cad", half, g, tcl, dh))
                                if g == NGR - 1:
                                    prog.wait("sy", (st, "cad", half, g, tcl, dh))
                                    slot = (tcl * 2 + dh) % 4
                                    prog.wait_if("sy", SC.get(("fst_prev", slot)))
                                    prog.dma("sy",
                                             res_dst[tc_g * 128:(tc_g + 1) * 128,
                                                     dh * 1024:(dh + 1) * 1024],
                                             acc[:, dsto:dsto + 1024],
                                             f"df{slot}",
                                             key=(st, "fst", half, tcl, dh))
                                    SC[("fst_prev", slot)] = (st, "fst", half, tcl, dh)
                        prog.alias((st, "cdone", half, g), (st, "cmm", half, g, 7, 1))

        def emit_attn_b_attention(st):
            psS, psO4, psT2 = SC["psS"], SC["psO4"], SC["psT2"]
            qstr, kpre, ostgb = SC["qstr"], SC["kpre"], SC["ostgb"]
            vaug, atbf, obf = SC["vaug"], SC["atbf"], SC["obf"]
            vdone = (st, "pmm", "v", 1, NTC - 1)
            guard_keys = [vdone, (st, "qe", "k", 1, NTC - 1, 7),
                          (st, "fmd", "k", 1, NTC - 1), (st, "vev", 1, NTC - 1),
                          (st, "qsp", "k", NTC - 2, 1), (st, "qsp", "k", NTC - 1, 1)]
            for geng in ("pe", "ve", "ac", "sy"):
                for gk in guard_keys:
                    prog.wait(geng, gk)

            sb_hist = [0]
            sb_prev = [None, None]
            pend_t = [None]
            pend_o = [None]
            pt_prev = [None, None]

            def prev_oq(s, h, ql):
                if h > 0:
                    return (s, h - 1, ql)
                if s > 0:
                    return (s - 1, H - 1, ql)
                return None

            for s in range(4):
                for j in range(4):
                    tc = 4 * s + j
                    if s >= 1:
                        prog.wait("sy", (st, "quse", s - 1))
                    prog.wait("sy", (st, "qsp", "q", tc, 1))
                    prog.dma("sy", qstr[:, j * D:(j + 1) * D], qs_s[tc, :, :], f"dd{j}",
                             key=(st, "qld", s, j))
                    prog.wait("sy", (st, "qsp", "k", tc, 1))
                    prog.dma("sy", kpre[:, tc * D:(tc + 1) * D], ks_s[tc, :, :], f"dg{tc % 4}",
                             key=(st, "kld", tc))
                for h in range(H):
                    if h == 0:
                        for j in range(4):
                            prog.wait("pe", (st, "qld", s, j))
                        for tc in range(4 * s + 4):
                            if tc >= 4 * s:
                                prog.wait("pe", (st, "kld", tc))
                    prog.wait("pe", (st, "vev", h // 8, 4 * s + 3))
                    for kb in range(4 * s + 4):
                        qlo = max(kb, 4 * s)
                        par = sb_hist[0] % 2
                        off = par * 512
                        sb_hist[0] += 1
                        pw = sb_prev[par]
                        if pw is not None:
                            prog.wait("pe", (st, "sq") + pw)
                            prog.wait("ac", (st, "oml") + pw)
                        for qc in range(qlo, 4 * s + 4):
                            ql = qc - 4 * s
                            prog.op("pe", lambda e, kb=kb, h=h, ql=ql, off=off: e.matmul(
                                psS[:, off + ql * 128: off + (ql + 1) * 128],
                                kpre[:, kb * D + h * 128: kb * D + (h + 1) * 128],
                                qstr[:, ql * D + h * 128: ql * D + (h + 1) * 128],
                                start=True, stop=True),
                                inc=None if qc < 4 * s + 3 else "pe",
                                key=None if qc < 4 * s + 3 else (st, "smm", s, h, kb))
                        prog.wait("ac", (st, "smm", s, h, kb))
                        ql0 = qlo - 4 * s
                        nq_ = 4 - ql0
                        prog.op("ac", lambda e, ql0=ql0, nq_=nq_, off=off: e.activation(
                            atbf[:, off + ql0 * 128: off + ql0 * 128 + nq_ * 128],
                            psS[:, off + ql0 * 128: off + ql0 * 128 + nq_ * 128], AF.Square),
                            inc="ac", key=(st, "sq", s, h, kb))
                        if kb >= 4 * s:
                            prog.wait("ve", (st, "sq", s, h, kb))
                            prog.op("ve", lambda e, kb=kb, s=s, off=off: e.tensor_tensor(
                                atbf[:, off + (kb - 4 * s) * 128: off + (kb - 4 * s + 1) * 128],
                                atbf[:, off + (kb - 4 * s) * 128: off + (kb - 4 * s + 1) * 128],
                                triu[:, :], ALU.mult),
                                inc="ve", key=(st, "msk", s, h, kb))

                        def mk_omm(s=s, h=h, kb=kb, qlo=qlo, off=off):
                            def omm():
                                if kb >= 4 * s:
                                    prog.wait("pe", (st, "msk", s, h, kb))
                                else:
                                    prog.wait("pe", (st, "sq", s, h, kb))
                                if kb == 0:
                                    for ql in range(4):
                                        pv = prev_oq(s, h, ql)
                                        if pv is not None:
                                            prog.wait("pe", (st, "odv") + pv)
                                for qc in range(qlo, 4 * s + 4):
                                    ql = qc - 4 * s
                                    voff = kb * 2064 + h * 129
                                    last = (kb == qc or qc == 4 * s + 3)
                                    prog.op("pe", lambda e, ql=ql, voff=voff, kb=kb, qc=qc, off=off: e.matmul(
                                        psO4[ql][:, 0:129],
                                        atbf[:, off + ql * 128: off + (ql + 1) * 128],
                                        vaug[:, voff:voff + 129],
                                        start=(kb == 0), stop=(kb == qc)),
                                        inc="pe" if last else None,
                                        key=(st, "omq", s, h, kb, ql) if last else None)
                                prog.alias((st, "oml", s, h, kb), (st, "omq", s, h, kb, 3))
                            return omm
                        if pend_o[0] is not None:
                            pend_o[0]()
                        pend_o[0] = mk_omm()
                        sb_prev[par] = (s, h, kb)
                    if pend_o[0] is not None:
                        pend_o[0]()
                        pend_o[0] = None
                    if pend_t[0] is not None:
                        pend_t[0]()
                    for ql in range(4):
                        qc = 4 * s + ql
                        prog.wait("ve", (st, "omq", s, h, qc, ql))
                        prog.op("ve", lambda e, ql=ql: e.tensor_scalar_add(
                            zcol[:, ql:ql + 1], psO4[ql][:, 128:129], EPS_B))
                        prog.op("ve", lambda e, ql=ql: e.reciprocal(
                            zcol[:, 4 + ql:5 + ql], zcol[:, ql:ql + 1]),
                            inc="ve", key=(st, "rz", s, h, ql))
                    for ql in range(4):
                        prog.wait("ac", (st, "rz", s, h, ql))
                        pv = prev_oq(s, h, ql)
                        if pv is not None:
                            prog.wait("ac", (st, "bot") + pv)
                        prog.op("ac", lambda e, ql=ql: e.activation(
                            obf[:, ql * 128:(ql + 1) * 128], psO4[ql][:, 0:128],
                            AF.Copy, scale=zcol[:, 4 + ql:5 + ql]),
                            inc="ac", key=(st, "odv", s, h, ql))

                    def mk_tail(s=s, h=h):
                        def tail():
                            for ql in range(4):
                                prog.wait("pe", (st, "odv", s, h, ql))
                                ppar = pt_prev[ql % 2]
                                if ppar is not None:
                                    prog.wait("pe", (st, "boe") + ppar)
                                pt_ = psT2[ql % 2]
                                prog.op("pe", lambda e, ql=ql, pt_=pt_: e.transpose(
                                    pt_[:, :], obf[:, ql * 128:(ql + 1) * 128], ident[:, :]),
                                    inc="pe", key=(st, "bot", s, h, ql))
                                prog.wait("ac", (st, "bot", s, h, ql))
                                if h == 0:
                                    if s >= 1:
                                        prog.wait("ac", (st, "ost", s - 1, ql))
                                    else:
                                        prog.wait("ac", vdone)
                                prog.op("ac", lambda e, ql=ql, h=h, pt_=pt_: e.copy(
                                    ostgb[ql][:, h * 128:(h + 1) * 128], pt_[:, :]),
                                    inc="ac", key=(st, "boe", s, h, ql))
                                pt_prev[ql % 2] = (s, h, ql)
                        return tail
                    pend_t[0] = mk_tail()
                if pend_t[0] is not None:
                    pend_t[0]()
                    pend_t[0] = None
                for ql in range(4):
                    prog.wait("sy", (st, "boe", s, H - 1, ql))
                    prog.dma("sy", os_s[4 * s + ql, :, :], ostgb[ql][:, :], f"df{ql}",
                             key=(st, "ost", s, ql))
                prog.alias((st, "quse", s), (st, "omq", s, H - 1, 4 * s + 3, 3))

        def emit_attn_b_proj(st, res_src, res_dst):
            ppj = SC["ppj"]
            wbuf = SC["wbuf"]
            oblks = SC["oblks"]
            ring_rtB = SC["ring_rtB"]
            attn_done = (st, "quse", 3)
            guard_keys = [attn_done, (st, "odv", 3, H - 1, 3),
                          (st, "boe", 3, H - 1, 3), (st, "sq", 3, H - 1, 15)] + \
                         [(st, "ost", 3, ql) for ql in range(4)]
            for geng in ("pe", "ve", "ac", "sy", "gp"):
                for gk in guard_keys:
                    prog.wait(geng, gk)
            for cc in range(NCC):
                prog.dma("gp", wbuf[:, cc * D:(cc + 1) * D],
                         wo_b[cc * 128:(cc + 1) * 128, :], "gw", key=(st, "wo", cc))
            for n in range(NTC):
                ob = oblks[n % 2]
                if n >= 2:
                    prog.wait("sy", (st, "puse", n - 2))
                else:
                    prog.wait("sy", attn_done)
                prog.wait("sy", (st, "ost", n // 4, n % 4))
                prog.dma("sy", ob[:, :], os_s[n, :, :], f"dc{n % 2}", key=(st, "old", n))
                rs, rt, rwar = ring_rtB.acquire()
                if rwar is not None:
                    prog.wait("sy", rwar)
                else:
                    prog.wait("sy", attn_done)
                prog.dma("sy", rt[:, :], res_src[n * 128:(n + 1) * 128, :], f"db{rs}",
                         key=(st, "rr", n))
                prog.wait("pe", (st, "old", n))
                if n == 0:
                    prog.wait("pe", (st, "wo", NCC - 1))
                for pc in range(4):
                    pp = ppj[pc % 2]
                    if n >= 1 or pc >= 2:
                        pn_, pcc = (n, pc - 2) if pc >= 2 else (n - 1, pc + 2)
                        prog.wait("pe", (st, "rad", pn_, pcc))
                    for hc in range(NCC):
                        prog.op("pe", lambda e, pp=pp, ob=ob, hc=hc, pc=pc: e.matmul(
                            pp[:, :],
                            ob[:, hc * 128:(hc + 1) * 128],
                            wbuf[:, hc * D + pc * 512: hc * D + (pc + 1) * 512],
                            start=(hc == 0), stop=(hc == NCC - 1)),
                            inc=None if hc != NCC - 1 else "pe",
                            key=None if hc != NCC - 1 else (st, "pj", n, pc))
                    prog.wait("ve", (st, "pj", n, pc))
                    if pc == 0:
                        prog.wait("ve", (st, "rr", n))
                    prog.op("ve", lambda e, rt=rt, pp=pp, pc=pc: e.tensor_tensor(
                        rt[:, pc * 512:(pc + 1) * 512], rt[:, pc * 512:(pc + 1) * 512],
                        pp[:, :], ALU.add),
                        inc="ve", key=(st, "rad", n, pc))
                prog.alias((st, "puse", n), (st, "pj", n, 3))
                prog.wait("sy", (st, "rad", n, 3))
                prog.dma("sy", res_dst[n * 128:(n + 1) * 128, :], rt[:, :], f"de{rs}",
                         key=(st, "rst", n))
                ring_rtB.consumer(rs, (st, "rst", n))

        flushed = {e: 0 for e in ENGINES}

        def flush_block():
            for sn in prog.all_dma_sems():
                v = prog.cnt[sn]
                if v > 0:
                    eng = "gp" if sn in ("gw", "gwb") or sn.startswith(("da", "ga")) else "sy"
                    prog.ops[eng].append(
                        lambda e, sem=prog.sems[sn], v=v: e.wait_ge(sem, v))
            start_idx = dict(flushed)
            with nc.Block() as blk:
                def mk(eng_name):
                    def run(eng):
                        for f in prog.ops[eng_name][start_idx[eng_name]:]:
                            f(eng)
                    return run
                blk.tensor(mk("pe"))
                blk.vector(mk("ve"))
                blk.scalar(mk("ac"))
                blk.gpsimd(mk("gp"))
                blk.sync(mk("sy"))
            for e in ENGINES:
                flushed[e] = len(prog.ops[e])

        emit_const_loads()
        flush_block()

        if n_sub >= 2:
            with ExitStack() as s0:
                SC["pps2"] = [s0.enter_context(nc.psum_tensor(f"p_q{i}", [128, 512], F32))
                              for i in range(8)]
                SC["p_wt"] = [s0.enter_context(nc.sbuf_tensor(f"p_w{i}", [128, D], BF16))
                              for i in range(4)]
                SC["p_stg"] = [s0.enter_context(nc.sbuf_tensor(f"p_s{i}", [128, D], BF16))
                               for i in range(4)]
                emit_w2prep(90)
                flush_block()

        stage_specs = []
        if n_sub >= 1:
            stage_specs.append(("attn_a", 0))
        if n_sub >= 2:
            stage_specs.append(("ffn", 1))
        if n_sub >= 3:
            stage_specs.append(("attn_b", 2))
        if n_sub >= 4:
            stage_specs.append(("ffn", 3))

        for kind, idx in stage_specs:
            st = idx
            res_src, res_dst = res_seq[idx], res_seq[idx + 1]
            if kind in ("attn_a", "attn_b"):
                is_a = kind == "attn_a"
                with ExitStack() as sA:
                    SC["brow"] = sA.enter_context(nc.sbuf_tensor(f"s{st}_br", [1, 2 * D], BF16))
                    for par in range(2):
                        SC.pop(("xb_war", par), None)
                        SC.pop(("xb_rwar", par), None)
                        SC.pop(("elu_war", par), None)
                        SC.pop(("lnb_y_war", par), None)
                        SC.pop(("lnb_r_war", par), None)
                        SC.pop(("lnb_y2_war", par), None)
                    SC.pop("wload_war_q", None); SC.pop("wload_war_k", None)
                    SC.pop("wload_war_v", None)
                    sXW = ExitStack()
                    SC["xb"] = sXW.enter_context(nc.sbuf_tensor(f"s{st}_xb", [128, NCC * T], BF16))
                    SC["wbufH"] = sXW.enter_context(nc.sbuf_tensor(f"s{st}_wh", [128, NCC * 1024], BF16))
                    SC["sqbf"] = sXW.enter_context(nc.sbuf_tensor(f"s{st}_sq", [128, D], BF16))
                    SC["scr_c"] = sXW.enter_context(nc.sbuf_tensor(f"s{st}_sc", [128, 2048], F32))
                    SC["ybf"] = sXW.enter_context(nc.sbuf_tensor(f"s{st}_yb", [128, 2048], BF16))
                    SC["ring_xn"] = Ring([sXW.enter_context(
                        nc.sbuf_tensor(f"s{st}_xn{i}", [128, 1024], BF16)) for i in range(2)])
                    SC["ring_st"] = Ring([sXW.enter_context(
                        nc.sbuf_tensor(f"s{st}_st{i}", [128, 1024], BF16)) for i in range(2)])
                    if not is_a:
                        SC["gamt"] = sXW.enter_context(nc.sbuf_tensor(f"s{st}_gm", [128, D], BF16))
                        SC["y2bf"] = sXW.enter_context(nc.sbuf_tensor(f"s{st}_y2", [128, 2048], BF16))
                    SC["ring_pt"] = Ring([sXW.enter_context(
                        nc.psum_tensor(f"s{st}_pt{i}", [128, 128], BF16)) for i in range(4)])
                    SC["ring_pp"] = Ring([sXW.enter_context(
                        nc.psum_tensor(f"s{st}_pp{i}", [128, 1024], F32)) for i in range(2)])
                    sR = ExitStack()
                    SC["ring_rt"] = Ring([sR.enter_context(
                        nc.sbuf_tensor(f"s{st}_rt{i}", [128, D], F32)) for i in range(2)])
                    emit_xb(st, res_src)
                    sR.close()
                    emit_attn_phaseA(st, kind)
                    SC["vaug"] = sA.enter_context(
                        nc.sbuf_tensor(f"s{st}_va", [128, NTC * H * 129], BF16, side="right"))
                    emit_attn_v(st, kind)
                    sXW.close()
                    if is_a:
                        with ExitStack() as sB:
                            SC["wbuf"] = sB.enter_context(
                                nc.sbuf_tensor(f"s{st}_wo", [128, NCC * D], BF16))
                            SC["psa2"] = [sB.enter_context(
                                nc.psum_tensor(f"s{st}_psa{i}", [128, 128], F32))
                                for i in range(2)]
                            SC["psTb"] = sB.enter_context(
                                nc.psum_tensor(f"s{st}_psTb", [128, 128], BF16))
                            SC["pso2"] = [sB.enter_context(
                                nc.psum_tensor(f"s{st}_pso{i}", [128, 129], F32))
                                for i in range(2)]
                            SC["kvp2"] = [sB.enter_context(
                                nc.psum_tensor(f"s{st}_kvp{i}", [128, 387], F32))
                                for i in range(2)]
                            SC["ppj"] = sB.enter_context(
                                nc.psum_tensor(f"s{st}_ppj", [128, 512], F32))
                            SC["state"] = sB.enter_context(
                                nc.sbuf_tensor(f"s{st}_kv", [128, H * 129], F32))
                            SC["state_bf2"] = [sB.enter_context(
                                nc.sbuf_tensor(f"s{st}_kvb{i}", [128, H * 129], BF16))
                                for i in range(2)]
                            SC["atbf"] = sB.enter_context(
                                nc.sbuf_tensor(f"s{st}_at", [128, 256], BF16))
                            SC["obf"] = sB.enter_context(
                                nc.sbuf_tensor(f"s{st}_ob", [128, 256], BF16))
                            for kk in ("a_psTb",):
                                SC.pop(kk, None)
                            for hp_ in range(2):
                                for kk in ("a_psa", "a_atb", "a_pso", "a_obf",
                                           "kvp_war"):
                                    SC.pop((kk, hp_), None)
                            SC["ostgs"] = [sB.enter_context(
                                nc.sbuf_tensor(f"s{st}_og{i}", [128, D], BF16))
                                for i in range(2)]
                            SC["qblkb"] = [sB.enter_context(
                                nc.sbuf_tensor(f"s{st}_qb{i}", [128, D], BF16))
                                for i in range(2)]
                            SC["kblkb"] = [sB.enter_context(
                                nc.sbuf_tensor(f"s{st}_kb{i}", [128, D], BF16))
                                for i in range(2)]
                            SC["ktblk"] = [sB.enter_context(
                                nc.sbuf_tensor(f"s{st}_kt{i}", [128, D], BF16))
                                for i in range(2)]
                            SC["ring_rtB"] = Ring([sB.enter_context(
                                nc.sbuf_tensor(f"s{st}_rb{i}", [128, D], F32))
                                for i in range(2)])
                            emit_attn_a_phaseB(st, res_src, res_dst)
                            flush_block()
                    else:
                        with ExitStack() as sB:
                            SC["psS"] = sB.enter_context(
                                nc.psum_tensor(f"s{st}_psS", [128, 1024], F32))
                            SC["psO4"] = [sB.enter_context(
                                nc.psum_tensor(f"s{st}_pO{i}", [128, 129], F32))
                                for i in range(4)]
                            SC["psT2"] = [sB.enter_context(
                                nc.psum_tensor(f"s{st}_psTc{i}", [128, 128], BF16))
                                for i in range(2)]
                            SC["qstr"] = sB.enter_context(
                                nc.sbuf_tensor(f"s{st}_qs", [128, 4 * D], BF16))
                            SC["kpre"] = sB.enter_context(
                                nc.sbuf_tensor(f"s{st}_kp", [128, NTC * D], BF16))
                            SC["ostgb"] = [sB.enter_context(
                                nc.sbuf_tensor(f"s{st}_ot{i}", [128, D], BF16))
                                for i in range(4)]
                            SC["atbf"] = sB.enter_context(
                                nc.sbuf_tensor(f"s{st}_atb", [128, 1024], BF16))
                            SC["obf"] = sB.enter_context(
                                nc.sbuf_tensor(f"s{st}_obb", [128, 512], BF16))
                            emit_attn_b_attention(st)
                        with ExitStack() as sC:
                            SC["ppj"] = [sC.enter_context(
                                nc.psum_tensor(f"s{st}_bpj{i}", [128, 512], F32))
                                for i in range(2)]
                            SC["wbuf"] = sC.enter_context(
                                nc.sbuf_tensor(f"s{st}_wob", [128, NCC * D], BF16))
                            SC["oblks"] = [sC.enter_context(
                                nc.sbuf_tensor(f"s{st}_obk{i}", [128, D], BF16))
                                for i in range(2)]
                            SC["ring_rtB"] = Ring([sC.enter_context(
                                nc.sbuf_tensor(f"s{st}_rc{i}", [128, D], F32))
                                for i in range(2)])
                            emit_attn_b_proj(st, res_src, res_dst)
                            flush_block()
            else:
                for par in range(2):
                    SC.pop(("xb_war", par), None)
                    SC.pop(("xb_rwar", par), None)
                for slot in range(4):
                    SC.pop(("fst_prev", slot), None)
                with ExitStack() as sF:
                    SC["xb"] = sF.enter_context(nc.sbuf_tensor(f"f{st}_xb", [128, NCC * T], BF16))
                    sX = ExitStack()
                    SC["sqbf"] = sX.enter_context(nc.sbuf_tensor(f"f{st}_sq", [128, D], BF16))
                    SC["ring_xn"] = Ring([sX.enter_context(
                        nc.sbuf_tensor(f"f{st}_xn{i}", [128, 1024], BF16)) for i in range(2)])
                    SC["ring_pt"] = Ring([sX.enter_context(
                        nc.psum_tensor(f"f{st}_pt{i}", [128, 128], BF16)) for i in range(2)])
                    SC["ring_rt"] = Ring([sX.enter_context(
                        nc.sbuf_tensor(f"f{st}_rt{i}", [128, D], F32)) for i in range(2)])
                    emit_xb(st, res_src)
                    sX.close()
                    with ExitStack() as sG:
                        SC["psG"] = sG.enter_context(nc.psum_tensor(f"f{st}_pG", [128, 1024], F32))
                        SC["psL"] = sG.enter_context(nc.psum_tensor(f"f{st}_pL", [128, 1024], F32))
                        SC["ppcs"] = [sG.enter_context(nc.psum_tensor(f"f{st}_pc{i}", [128, 1024], F32))
                                      for i in range(2)]
                        SC["acc"] = sG.enter_context(nc.sbuf_tensor(f"f{st}_ac", [128, 8 * D], F32))
                        SC["sg"] = sG.enter_context(nc.sbuf_tensor(f"f{st}_sg", [128, 1024], F32))
                        SC["w12t"] = [sG.enter_context(nc.sbuf_tensor(f"f{st}_w1{i}", [128, D], BF16))
                                      for i in range(2)]
                        SC["w34t"] = [sG.enter_context(nc.sbuf_tensor(f"f{st}_w3{i}", [128, D], BF16))
                                      for i in range(2)]
                        SC["hgt"] = [sG.enter_context(nc.sbuf_tensor(f"f{st}_hg{i}", [128, 1024], BF16))
                                     for i in range(G)]
                        SC["w2gt"] = [sG.enter_context(nc.sbuf_tensor(f"f{st}_w2{i}", [128, D], BF16))
                                      for i in range(G)]
                        SC["ring_rtF"] = Ring([sG.enter_context(
                            nc.sbuf_tensor(f"f{st}_rr{i}", [128, 1024], F32)) for i in range(2)])
                        emit_ffn(st, res_src, res_dst, first_pass=(idx == 1))
                        flush_block()

        for sn in prog.all_dma_sems():
            if prog.cnt[sn] > 0:
                prog.op("sy", lambda e, sn=sn: e.wait_ge(sems[sn], prog.cnt[sn]))
        flush_block()

    return nc


def _mk_hadamard(n):
    h = np.array([[1.0]], dtype=np.float64)
    while h.shape[0] < n:
        h = np.block([[h, h], [h, -h]])
    return h


def make_inmaps(inputs):
    bf = ml_dtypes.bfloat16
    x = np.ascontiguousarray(np.asarray(inputs["L0_attn_a_x"], dtype=np.float32))
    B = x.shape[0]

    def f32(a):
        return np.ascontiguousarray(np.asarray(a, dtype=np.float32))

    common = {
        "wq_a": f32(inputs["L0_attn_a_Wq"]), "wk_a": f32(inputs["L0_attn_a_Wk"]),
        "wv_a": f32(inputs["L0_attn_a_Wv"]), "wo_a": f32(inputs["L0_attn_a_Wo"]),
        "bq_a": f32(inputs["L0_attn_a_bq"]).reshape(1, D),
        "bk_a": f32(inputs["L0_attn_a_bk"]).reshape(1, D),
        "bv_a": f32(inputs["L0_attn_a_bv"]).reshape(1, D),
        "bo_a": f32(inputs["L0_attn_a_bo"]).reshape(1, D),
        "wq_b": np.ascontiguousarray(f32(inputs["L0_attn_b_Wq"]).T),
        "wk_b": np.ascontiguousarray(f32(inputs["L0_attn_b_Wk"]).T),
        "wv_b": np.ascontiguousarray(f32(inputs["L0_attn_b_Wv"]).T),
        "wo_b": np.ascontiguousarray(f32(inputs["L0_attn_b_Wo"]).T),
        "gam_t": np.ascontiguousarray(np.tile(f32(inputs["L0_attn_b_gamma"]), (128, H))),
        "bet_row": np.ascontiguousarray(np.tile(f32(inputs["L0_attn_b_beta"]), H)).reshape(1, D),
        "w1t": np.ascontiguousarray(
            f32(inputs["L0_ffn_W1"]).T.reshape(NCC, 128, NHC, 128).transpose(2, 1, 0, 3)
        ).reshape(NHC, 128, D),
        "w3t": np.ascontiguousarray(
            f32(inputs["L0_ffn_W3"]).T.reshape(NCC, 128, NHC, 128).transpose(2, 1, 0, 3)
        ).reshape(NHC, 128, D),
        "w2t": np.ascontiguousarray(f32(inputs["L0_ffn_W2"]).T),
        "ident_c": np.eye(128, dtype=bf),
        "triu_c": np.triu(np.ones((128, 128))).astype(bf),
        "h128_c": _mk_hadamard(128).astype(bf),
        "h64_c": _mk_hadamard(64).astype(bf),
    }
    return [dict(common, x=np.ascontiguousarray(x[b])) for b in range(B)], B


def kernel(**inputs):
    in_maps, B = make_inmaps(inputs)
    nc = build(N_SUB)
    res = run_bass_kernel_spmd(nc, in_maps, core_ids=list(range(B)))
    return np.stack([np.asarray(r["out"], dtype=np.float32) for r in res.results], axis=0)


# version 10
# speedup vs baseline: 1.0953x; 1.0953x over previous
import math
from contextlib import ExitStack

import numpy as np
import ml_dtypes
import concourse.bass as bass
import concourse.mybir as mybir
from concourse.bass_utils import run_bass_kernel_spmd

F32 = mybir.dt.float32
BF16 = mybir.dt.bfloat16
AF = mybir.ActivationFunctionType
ALU = mybir.AluOpType

T = 2048
D = 2048
H = 16
HD = 128
NTC = T // 128
NCC = D // 128
HID = 8192
NHC = HID // 128
G = 8
NGR = NHC // G
LN_EPS = 1e-5
EPS_A = 1e-6
EPS_B = 1e-5
FWHT_SCALE = float(np.float32(1.0 / np.sqrt(np.float64(HID))))
RD = float(1.0 / math.sqrt(HD))

N_SUB = 4

ENGINES = ("pe", "ve", "ac", "gp", "sy")


class Ring:

    def __init__(self, bufs):
        self.bufs = list(bufs)
        self.war = [None] * len(self.bufs)
        self.idx = 0

    def acquire(self):
        s = self.idx % len(self.bufs)
        self.idx += 1
        return s, self.bufs[s], self.war[s]

    def consumer(self, s, eng_key):
        self.war[s] = eng_key


class Prog:
    def __init__(self, nc, sems):
        self.nc = nc
        self.sems = sems
        self.cnt = {k: 0 for k in sems}
        self.keys = {}
        self.ops = {e: [] for e in ENGINES}

    ENGSEM = {"ve": "ve", "ac": "ac", "gp": "gc"}

    def op(self, eng, fn, inc=None, k=1, key=None, dma=False):
        auto = (not dma) and eng in self.ENGSEM
        if auto:
            es = self.ENGSEM[eng]
            if inc is None:
                inc = es
                k = 1
            assert inc == es
            prevv = self.cnt[es]
            if prevv > 0:
                sem0 = self.sems[es]
                self.ops[eng].append(
                    lambda e, sem0=sem0, prevv=prevv: e.wait_ge(sem0, prevv))
        if inc is not None:
            self.cnt[inc] += k
            if key is not None:
                assert key not in self.keys, f"dup key {key}"
                self.keys[key] = (inc, self.cnt[inc])
            sem = self.sems[inc]
            self.ops[eng].append(lambda e, fn=fn, sem=sem, kk=k: fn(e).then_inc(sem, kk))
        else:
            self.ops[eng].append(lambda e, fn=fn: fn(e))

    def dma(self, eng, out, in_, inc, key=None):
        self.op(eng, lambda e, out=out, in_=in_: e.dma_start(out, in_),
                inc=inc, k=16, key=key, dma=True)

    def all_dma_sems(self):
        return [n for n in self.sems if n not in ("pe", "ve", "ac", "gc")]

    def alias(self, new_key, old_key):
        self.keys[new_key] = self.keys[old_key]

    def wait(self, eng, key):
        assert key in self.keys, f"missing key {key}"
        sem_name, val = self.keys[key]
        sem = self.sems[sem_name]
        self.ops[eng].append(lambda e, sem=sem, val=val: e.wait_ge(sem, val))

    def wait_if(self, eng, key):
        if isinstance(key, list):
            for k in key:
                self.wait_if(eng, k)
            return
        if key is not None and key in self.keys:
            self.wait(eng, key)


def build(n_sub=N_SUB):
    nc = bass.Bass("TRN2", target_bir_lowering=False)

    def din(name, shape, dtype=F32):
        return nc.dram_tensor(name, shape, dtype, kind="ExternalInput").ap()

    x_in = din("x", [T, D])
    wq_a = din("wq_a", [D, D]); wk_a = din("wk_a", [D, D])
    wv_a = din("wv_a", [D, D]); wo_a = din("wo_a", [D, D])
    bq_a = din("bq_a", [1, D]); bk_a = din("bk_a", [1, D])
    bv_a = din("bv_a", [1, D]); bo_a = din("bo_a", [1, D])
    wq_b = din("wq_b", [D, D]); wk_b = din("wk_b", [D, D])
    wv_b = din("wv_b", [D, D]); wo_b = din("wo_b", [D, D])
    gam_t = din("gam_t", [128, D])
    bet_row_i = din("bet_row", [1, D])
    w1t = din("w1t", [NHC, 128, D])
    w3t = din("w3t", [NHC, 128, D])
    w2t = din("w2t", [HID, D])
    ident_c = din("ident_c", [128, 128], BF16)
    triu_c = din("triu_c", [128, 128], BF16)
    h128_c = din("h128_c", [128, 128], BF16)
    h64_c = din("h64_c", [64, 64], BF16)

    out_t = nc.dram_tensor("out", [T, D], F32, kind="ExternalOutput").ap()

    res_a = nc.dram_tensor("res_a", [T, D], F32, kind="Internal").ap()
    res_b = nc.dram_tensor("res_b", [T, D], F32, kind="Internal").ap()
    qs_s = nc.dram_tensor("qs_s", [NTC, 128, D], BF16, kind="Internal").ap()
    ks_s = nc.dram_tensor("ks_s", [NTC, 128, D], BF16, kind="Internal").ap()
    kts_s = nc.dram_tensor("kts_s", [NTC, 128, D], BF16, kind="Internal").ap()
    os_s = nc.dram_tensor("os_s", [NTC, 128, D], BF16, kind="Internal").ap()
    b_s = nc.dram_tensor("b_s", [64, 128, D], BF16, kind="Internal").ap()
    w2p_s = nc.dram_tensor("w2p_s", [NHC, 128, D], BF16, kind="Internal").ap()
    w1c_s = nc.dram_tensor("w1c_s", [NHC, 128, D], BF16, kind="Internal").ap()
    w3c_s = nc.dram_tensor("w3c_s", [NHC, 128, D], BF16, kind="Internal").ap()

    res_seq = [x_in, res_a, res_b, res_a, out_t]
    if n_sub < 4:
        res_seq[n_sub] = out_t

    with ExitStack() as es:
        E = es.enter_context
        sem_names = (["pe", "ve", "ac", "gc", "gw", "gwb", "sw2",
                      "sc10", "sc11", "sc30", "sc31", "sl"]
                     + [f"da{i}" for i in range(4)]
                     + [f"ga{i}" for i in range(4)]
                     + [f"db{i}" for i in range(4)]
                     + [f"dc{i}" for i in range(2)]
                     + [f"dd{i}" for i in range(4)]
                     + [f"dg{i}" for i in range(4)]
                     + [f"de{i}" for i in range(2)]
                     + [f"df{i}" for i in range(4)])
        sems = {n: E(nc.semaphore(n)) for n in sem_names}
        prog = Prog(nc, sems)

        ident = E(nc.sbuf_tensor("ident", [128, 128], BF16))
        triu = E(nc.sbuf_tensor("triu", [128, 128], BF16))
        h128 = E(nc.sbuf_tensor("h128", [128, 128], BF16))
        h64 = E(nc.sbuf_tensor("h64", [64, 64], BF16))
        ones_row = E(nc.sbuf_tensor("ones_row", [1, 128], BF16))
        cs = E(nc.sbuf_tensor("cs", [128, 104], F32))
        zcol = E(nc.sbuf_tensor("zcol", [128, 8], F32))
        cbias = E(nc.sbuf_tensor("cbias", [128, 2], F32))

        def col(t_, a, n):
            return t_[:, a:a + n]

        SC = {}

        def emit_const_loads():
            prog.dma("sy", ident[:, :], ident_c[:, :], "sl")
            prog.dma("sy", triu[:, :], triu_c[:, :], "sl")
            prog.dma("sy", h128[:, :], h128_c[:, :], "sl")
            prog.dma("sy", h64[:, :], h64_c[:, :], "sl", key="c_all")
            prog.op("gp", lambda e: e.memset(ones_row[:, :], 1.0), inc="gc", key="c_ones")
            prog.op("gp", lambda e: e.memset(cbias[:, 0:1], LN_EPS))
            prog.op("gp", lambda e: e.memset(cbias[:, 1:2], math.log(RD)),
                    inc="gc", key="c_eps")
            prog.wait("pe", "c_all")
            prog.wait("pe", "c_ones")
            prog.wait("ve", "c_all")
            prog.wait("ac", "c_eps")

        def emit_xb(st, res_src):
            ring_rt, ring_xn, ring_pt = SC["ring_rt"], SC["ring_xn"], SC["ring_pt"]
            sqbf = SC["sqbf"]
            xb = SC["xb"]
            for tc in range(NTC):
                par = tc % 2
                base = 0 if par == 0 else 96
                ss, sq, mu, msq, var_, lnv, rstd = (col(cs, base + i, 1) for i in range(7))
                rs, rt, rwar = ring_rt.acquire()
                prog.wait_if("sy", rwar)
                prog.dma("sy", rt[:, :], res_src[tc * 128:(tc + 1) * 128, :], f"db{rs}",
                         key=(st, "xr", tc))
                prog.wait("ve", (st, "xr", tc))
                prog.wait_if("ve", SC.get(("xb_war", par)))
                prog.op("ve", lambda e, rt=rt, ss=ss: e.reduce_sum(
                    ss, rt[:, :], axis=mybir.AxisListType.X))
                prog.op("ve", lambda e, rt=rt, sq=sq: e.scalar_tensor_tensor(
                    sqbf[:, :], rt[:, :], 1.0, rt[:, :], ALU.bypass, ALU.mult,
                    accum_out=sq))
                prog.op("ve", lambda e, mu=mu, ss=ss: e.tensor_scalar_mul(mu, ss, 1.0 / D))
                prog.op("ve", lambda e, msq=msq, mu=mu: e.scalar_tensor_tensor(
                    msq, mu, 1.0, mu, ALU.bypass, ALU.mult))
                prog.op("ve", lambda e, var_=var_, sq=sq, msq=msq: e.tensor_scalar(
                    var_, sq, 1.0 / D, msq, ALU.mult, ALU.subtract),
                    inc="ve", key=(st, "xv", tc))
                prog.wait("ac", (st, "xv", tc))
                prog.wait_if("ac", SC.get(("xb_rwar", par)))
                prog.op("ac", lambda e, lnv=lnv, var_=var_: e.activation(
                    lnv, var_, AF.Ln, bias=cbias[:, 0:1]))
                prog.op("ac", lambda e, rstd=rstd, lnv=lnv: e.activation(
                    rstd, lnv, AF.Exp, scale=-0.5),
                    inc="ac", key=(st, "xr2", tc))
                prog.wait("ve", (st, "xr2", tc))
                lastn = None
                for half in range(2):
                    xs_, xn, xwar = ring_xn.acquire()
                    prog.wait_if("ve", xwar)
                    prog.op("ve", lambda e, rt=rt, xn=xn, half=half, mu=mu, rstd=rstd: e.tensor_scalar(
                        xn[:, :], rt[:, half * 1024:(half + 1) * 1024], mu, rstd,
                        ALU.subtract, ALU.mult),
                        inc="ve", key=(st, "xn", tc, half))
                    lastn = (st, "xn", tc, half)
                    prog.wait("pe", (st, "xn", tc, half))
                    for dc8 in range(8):
                        dc = half * 8 + dc8
                        ps_, pt, pwar = ring_pt.acquire()
                        prog.wait_if("pe", pwar)
                        prog.op("pe", lambda e, pt=pt, xn=xn, dc8=dc8: e.transpose(
                            pt[:, :], xn[:, dc8 * 128:(dc8 + 1) * 128], ident[:, :]),
                            inc="pe", key=(st, "xt", tc, dc))
                        prog.wait("ac", (st, "xt", tc, dc))
                        prog.op("ac", lambda e, pt=pt, dc=dc, tc=tc: e.copy(
                            xb[:, dc * T + tc * 128: dc * T + (tc + 1) * 128], pt[:, :]),
                            inc="ac", key=(st, "xe", tc, dc))
                        ring_pt.consumer(ps_, (st, "xe", tc, dc))
                    ring_xn.consumer(xs_, (st, "xt", tc, half * 8 + 7))
                SC[("xb_war", par)] = lastn
                SC[("xb_rwar", par)] = lastn
                ring_rt.consumer(rs, lastn)

        def emit_proj(st, tag, wsrc, bias_row, evac, scale_tile=None):
            ring_pp = SC["ring_pp"]
            wbufH = SC["wbufH"]
            xb = SC["xb"]
            for dh in range(2):
                for cc in range(NCC):
                    if cc == 0:
                        if dh == 0:
                            prog.wait_if("gp", SC.get("wload_war_" + tag))
                        else:
                            prog.wait("gp", (st, "pmm", tag, 0, NTC - 1))
                    prog.dma("gp", wbufH[:, cc * 1024:(cc + 1) * 1024],
                             wsrc[cc * 128:(cc + 1) * 128, dh * 1024:(dh + 1) * 1024],
                             "gw", key=(st, "w", tag, dh, cc))
                if scale_tile is not None:
                    prog.wait("ve", (st, "w", tag, dh, NCC - 1))
                    for cc in range(NCC):
                        prog.op("ve", lambda e, cc=cc, dh=dh, wbufH=wbufH, scale_tile=scale_tile: e.tensor_tensor(
                            wbufH[:, cc * 1024:(cc + 1) * 1024],
                            wbufH[:, cc * 1024:(cc + 1) * 1024],
                            scale_tile[:, dh * 1024:(dh + 1) * 1024], ALU.mult),
                            inc="ve", key=(st, "wsc", tag, dh, cc))
                for tc in range(NTC):
                    ps_, pp, pwar = ring_pp.acquire()
                    prog.wait_if("pe", pwar)
                    if tag == "q" and dh == 0:
                        prog.wait("pe", (st, "xe", tc, NCC - 1))
                    if tc == 0:
                        if scale_tile is not None:
                            prog.wait("pe", (st, "wsc", tag, dh, NCC - 1))
                        else:
                            prog.wait("pe", (st, "w", tag, dh, NCC - 1))
                    for cc in range(NCC):
                        last = (cc == NCC - 1) and bias_row is None
                        for nq in range(2):
                            prog.op("pe", lambda e, pp=pp, cc=cc, nq=nq, tc=tc, wbufH=wbufH, last=last: e.matmul(
                                pp[:, nq * 512:(nq + 1) * 512],
                                xb[:, cc * T + tc * 128: cc * T + (tc + 1) * 128],
                                wbufH[:, cc * 1024 + nq * 512: cc * 1024 + (nq + 1) * 512],
                                start=(cc == 0), stop=last),
                                inc=None if not (last and nq == 1) else "pe",
                                key=None if not (last and nq == 1) else (st, "pmm", tag, dh, tc))
                    if bias_row is not None:
                        for nq in range(2):
                            prog.op("pe", lambda e, pp=pp, nq=nq, dh=dh, bias_row=bias_row: e.matmul(
                                pp[:, nq * 512:(nq + 1) * 512],
                                ones_row[0:1, :],
                                bias_row[0:1, dh * 1024 + nq * 512: dh * 1024 + (nq + 1) * 512],
                                start=False, stop=True),
                                inc=None if nq == 0 else "pe",
                                key=None if nq == 0 else (st, "pmm", tag, dh, tc))
                    if SC.get("_pend_evac") is not None:
                        SC["_pend_evac"]()
                    lastkey, dt_ = evac(st, tag, dh, tc, pp)
                    ring_pp.consumer(ps_, lastkey)
                    SC["_pend_evac"] = dt_
            if SC.get("_pend_evac") is not None:
                SC["_pend_evac"]()
                SC["_pend_evac"] = None
            SC["_pend_evac"] = None

        def mk_evac_qk(spill, fm, spill_tok=None):
            def evac(st, tag, dh, tc, pp):
                ring_xn, ring_pt, ring_st = SC["ring_xn"], SC["ring_pt"], SC["ring_st"]
                xs_, xn, xwar = ring_xn.acquire()
                prog.wait_if("ve", xwar)
                last_key = fm(st, tag, dh, tc, pp, xn)
                tok_key = None
                if spill_tok is not None:
                    prog.wait("gp", (st, "fmv", tag, dh, tc, 0))
                    prog.dma("gp", spill_tok[tc, :, dh * 1024:(dh + 1) * 1024],
                             xn[:, :], f"dd{(tc * 2 + dh) % 4}",
                             key=(st, "ksp", tc, dh))
                    tok_key = (st, "ksp", tc, dh)

                def do_transposes(xn=xn, xs_=xs_, tag=tag, dh=dh, tc=tc,
                                  tok_key=tok_key):
                    ts_, stg, swar = ring_st.acquire()
                    for h8 in range(8):
                        ps_, pt, pwar = ring_pt.acquire()
                        prog.wait_if("pe", pwar)
                        prog.wait("pe", (st, "fmv", tag, dh, tc, h8))
                        prog.op("pe", lambda e, pt=pt, xn=xn, h8=h8: e.transpose(
                            pt[:, :], xn[:, h8 * 128:(h8 + 1) * 128], ident[:, :]),
                            inc="pe", key=(st, "qt", tag, dh, tc, h8))
                        prog.wait("ve", (st, "qt", tag, dh, tc, h8))
                        if h8 == 0:
                            prog.wait_if("ve", swar)
                        prog.op("ve", lambda e, pt=pt, stg=stg, h8=h8: e.tensor_copy(
                            stg[:, h8 * 128:(h8 + 1) * 128], pt[:, :]),
                            inc="ve", key=(st, "qe", tag, dh, tc, h8))
                        ring_pt.consumer(ps_, (st, "qe", tag, dh, tc, h8))
                    xn_wars = [(st, "qt", tag, dh, tc, 7)]
                    if tok_key is not None:
                        xn_wars.append(tok_key)
                    ring_xn.consumer(xs_, xn_wars)
                    prog.wait("sy", (st, "qe", tag, dh, tc, 7))
                    prog.dma("sy", spill[tc, :, dh * 1024:(dh + 1) * 1024], stg[:, :],
                             f"de{ts_}", key=(st, "qsp", tag, tc, dh))
                    ring_st.consumer(ts_, (st, "qsp", tag, tc, dh))
                return last_key, do_transposes
            return evac

        def fm_elu(st, tag, dh, tc, pp, xn):
            par = (dh * NTC + tc) % 2
            tf = SC["scr_c"][:, par * 1024:(par + 1) * 1024]
            ebf = SC["ybf"][:, par * 1024:(par + 1) * 1024]
            prog.wait("ac", (st, "pmm", tag, dh, tc))
            prog.wait_if("ac", SC.get(("elu_war", par)))
            prog.op("ac", lambda e, pp=pp, tf=tf: e.copy(tf, pp[:, :]),
                    inc="ac", key=(st, "fmd", tag, dh, tc))
            prog.wait("ve", (st, "fmd", tag, dh, tc))
            prog.op("ve", lambda e, xn=xn, tf=tf: e.tensor_scalar_min(xn[:, :], tf, 0.0),
                    inc="ve", key=(st, "fmm", tag, dh, tc))
            prog.wait("ac", (st, "fmm", tag, dh, tc))
            prog.op("ac", lambda e, xn=xn, ebf=ebf: e.activation(ebf, xn[:, :], AF.Exp),
                    inc="ac", key=(st, "fme", tag, dh, tc))
            prog.wait("ve", (st, "fme", tag, dh, tc))
            prog.op("ve", lambda e, pp=pp, xn=xn, tf=tf, ebf=ebf: e.scalar_tensor_tensor(
                xn[:, :], tf, 0.0, ebf, ALU.max, ALU.add),
                inc="ve", key=(st, "fmn", tag, dh, tc))
            SC[("elu_war", par)] = (st, "fmn", tag, dh, tc)
            for hh in range(8):
                prog.alias((st, "fmv", tag, dh, tc, hh), (st, "fmn", tag, dh, tc))
            return (st, "fmd", tag, dh, tc)

        def fm_lnb(st, tag, dh, tc, pp, xn):
            par = (dh * NTC + tc) % 2
            ybf = SC["ybf"][:, par * 1024:(par + 1) * 1024]
            y2bf = SC["y2bf"][:, par * 1024:(par + 1) * 1024]
            base = 8 + par * 40
            seg = col(cs, base, 8); seg2 = col(cs, base + 8, 8)
            muh = col(cs, base + 16, 8); rstdh = col(cs, base + 24, 8)
            varh = col(cs, base + 32, 8)
            prog.wait("ac", (st, "pmm", tag, dh, tc))
            prog.wait_if("ac", SC.get(("lnb_y_war", par)))
            prog.op("ac", lambda e, pp=pp, ybf=ybf: e.copy(ybf, pp[:, :]),
                    inc="ac", key=(st, "fmd", tag, dh, tc))
            prog.wait_if("ac", SC.get(("lnb_y2_war", par)))
            prog.op("ac", lambda e, ybf=ybf, y2bf=y2bf: e.activation(y2bf, ybf, AF.Square),
                    inc="ac", key=(st, "fsq", tag, dh, tc))
            y3 = ybf.rearrange("p (s f) -> p s f", f=128)
            y23 = y2bf.rearrange("p (s f) -> p s f", f=128)
            prog.wait("ve", (st, "fmd", tag, dh, tc))
            prog.op("ve", lambda e, y3=y3, seg=seg: e.reduce_sum(seg, y3, axis=mybir.AxisListType.X))
            prog.wait("ve", (st, "fsq", tag, dh, tc))
            prog.op("ve", lambda e, y23=y23, seg2=seg2: e.reduce_sum(seg2, y23, axis=mybir.AxisListType.X),
                    inc="ve", key=(st, "fr2", tag, dh, tc))
            SC[("lnb_y2_war", par)] = (st, "fr2", tag, dh, tc)
            prog.op("ve", lambda e, muh=muh, seg=seg: e.tensor_scalar_mul(muh, seg, 1.0 / HD))
            prog.op("ve", lambda e, seg=seg, muh=muh: e.scalar_tensor_tensor(
                seg, muh, 1.0, muh, ALU.bypass, ALU.mult))
            prog.op("ve", lambda e, varh=varh, seg2=seg2, seg=seg: e.scalar_tensor_tensor(
                varh, seg2, 1.0 / HD, seg, ALU.mult, ALU.subtract),
                inc="ve", key=(st, "fv1", tag, dh, tc))
            prog.wait("ac", (st, "fv1", tag, dh, tc))
            prog.wait_if("ac", SC.get(("lnb_r_war", par)))
            prog.op("ac", lambda e, varh=varh: e.activation(varh, varh, AF.Ln, bias=cbias[:, 0:1]))
            if tag == "q":
                prog.op("ac", lambda e, rstdh=rstdh, varh=varh: e.activation(
                    rstdh, varh, AF.Exp, scale=-0.5, bias=cbias[:, 1:2]),
                    inc="ac", key=(st, "fmr", tag, dh, tc))
            else:
                prog.op("ac", lambda e, rstdh=rstdh, varh=varh: e.activation(
                    rstdh, varh, AF.Exp, scale=-0.5),
                    inc="ac", key=(st, "fmr", tag, dh, tc))
            prog.wait("ve", (st, "fmr", tag, dh, tc))
            keys = []
            for h8 in range(8):
                prog.op("ve", lambda e, h8=h8, xn=xn, ybf=ybf, muh=muh, rstdh=rstdh: e.tensor_scalar(
                    xn[:, h8 * 128:(h8 + 1) * 128],
                    ybf[:, h8 * 128:(h8 + 1) * 128],
                    muh[:, h8:h8 + 1], rstdh[:, h8:h8 + 1],
                    ALU.subtract, ALU.mult),
                    inc="ve", key=(st, "fnm", tag, dh, tc, h8))
                keys.append((st, "fnm", tag, dh, tc, h8))
            SC[("lnb_r_war", par)] = keys[-1]
            SC[("lnb_y_war", par)] = keys[-1]
            for h8 in range(8):
                prog.alias((st, "fmv", tag, dh, tc, h8), keys[h8])
            return (st, "fmd", tag, dh, tc)

        def mk_evac_v(st_v):
            vaug = SC["vaug"]

            def evac(st, tag, dh, tc, pp):
                prog.wait("ac", (st, "pmm", tag, dh, tc))
                if dh == 0 and tc == 0:
                    prog.wait("ac", (st, "vones"))
                off = tc * 2064 + dh * 8 * 129
                dst = bass.AP(vaug, off, [[NTC * H * 129, 128], [129, 8], [1, 128]])
                prog.op("ac", lambda e, pp=pp, dst=dst: e.copy(dst, pp[:, :]),
                        inc="ac", key=(st, "vev", dh, tc))
                return (st, "vev", dh, tc), None
            return evac

        def emit_attn_phaseA(st, kind):
            is_a = kind == "attn_a"
            if is_a:
                brow = SC["brow"]
                prog.dma("gp", brow[0:1, 0:D], bq_a[0:1, :], "gwb")
                prog.dma("gp", brow[0:1, D:2 * D], bk_a[0:1, :], "gwb", key=(st, "bqk"))
                prog.wait("pe", (st, "bqk"))
                emit_proj(st, "q", wq_a, brow[:, 0:D], mk_evac_qk(qs_s, fm_elu))
                SC["wload_war_k"] = (st, "pmm", "q", 1, NTC - 1)
                emit_proj(st, "k", wk_a, brow[:, D:2 * D],
                          mk_evac_qk(ks_s, fm_elu, spill_tok=kts_s))
            else:
                brow = SC["brow"]
                prog.dma("gp", SC["gamt"][:, :], gam_t[:, :], "gwb")
                prog.dma("gp", brow[0:1, 0:D], bet_row_i[0:1, :], "gwb", key=(st, "gbt"))
                prog.wait("ve", (st, "gbt"))
                prog.wait("pe", (st, "gbt"))
                emit_proj(st, "q", wq_b, brow[:, 0:D], mk_evac_qk(qs_s, fm_lnb),
                          scale_tile=SC["gamt"])
                SC["wload_war_k"] = (st, "pmm", "q", 1, NTC - 1)
                emit_proj(st, "k", wk_b, brow[:, 0:D], mk_evac_qk(ks_s, fm_lnb),
                          scale_tile=SC["gamt"])

        def emit_attn_v(st, kind):
            is_a = kind == "attn_a"
            vaug = SC["vaug"]
            va_ones = vaug[:, :].rearrange("p (t h c) -> p t h c", h=H, c=129)[:, :, :, 128:129]
            prog.op("ve", lambda e: e.memset(va_ones, 1.0), inc="ve", key=(st, "vones"))
            SC["wload_war_v"] = (st, "pmm", "k", 1, NTC - 1)
            SC["wload_war_" + "v"] = (st, "pmm", "k", 1, NTC - 1)
            if is_a:
                brow = SC["brow"]
                prog.wait("gp", (st, "pmm", "q", 1, NTC - 1))
                prog.dma("gp", brow[0:1, 0:D], bv_a[0:1, :], "gwb", key=(st, "bv"))
                prog.wait("pe", (st, "bv"))
                emit_proj(st, "v", wv_a, brow[:, 0:D], mk_evac_v(st))
            else:
                emit_proj(st, "v", wv_b, None, mk_evac_v(st))

        def emit_attn_a_phaseB(st, res_src, res_dst):
            psa2, psTb = SC["psa2"], SC["psTb"]
            pso2, kvp2 = SC["pso2"], SC["kvp2"]
            ppj = SC["ppj"]
            wbuf, vaug = SC["wbuf"], SC["vaug"]
            state, ostgs = SC["state"], SC["ostgs"]
            state_bf2, atbf, obf = SC["state_bf2"], SC["atbf"], SC["obf"]
            ring_rtB = SC["ring_rtB"]
            qblkb, kblkb, ktblk = SC["qblkb"], SC["kblkb"], SC["ktblk"]
            brow = SC["brow"]
            vdone = (st, "pmm", "v", 1, NTC - 1)
            GSZ = 3
            NG = (H + GSZ - 1) // GSZ
            CA_H = 8
            CA_C = (CA_H + 1) * 129
            prog.wait("gp", (st, "pmm", "k", 1, NTC - 1))
            prog.dma("gp", brow[0:1, D:2 * D], bo_a[0:1, :], "gwb", key=(st, "bo"))
            for gk in [vdone, (st, "qsp", "k", NTC - 2, 1), (st, "qsp", "k", NTC - 1, 1)]:
                prog.wait("gp", gk)
            for cc in range(NCC):
                prog.dma("gp", wbuf[:, cc * D:(cc + 1) * D],
                         wo_a[cc * 128:(cc + 1) * 128, :], "gw", key=(st, "wo", cc))
            guard_keys = [vdone, (st, "qe", "k", 1, NTC - 1, 7),
                          (st, "fmd", "k", 1, NTC - 1), (st, "vev", 1, NTC - 1),
                          (st, "qsp", "k", NTC - 2, 1), (st, "qsp", "k", NTC - 1, 1),
                          (st, "ksp", NTC - 2, 0), (st, "ksp", NTC - 2, 1),
                          (st, "ksp", NTC - 1, 0), (st, "ksp", NTC - 1, 1)]
            for geng in ("pe", "ve", "ac", "sy"):
                for gk in guard_keys:
                    prog.wait(geng, gk)
            prog.wait("pe", (st, "bo"))

            def emit_add(n, g):
                lo = g * GSZ * 129
                w_ = (min(GSZ, H - g * GSZ)) * 129
                prog.wait("ve", (st, "kv", n, min(g * GSZ + GSZ, H) - 1))
                if n >= 1:
                    prog.wait("ve", (st, "castA" if g <= 2 else "castB", n - 1))
                if n == 0:
                    prog.op("ve", lambda e, lo=lo, w_=w_, g=g: e.tensor_copy(
                        state[:, lo:lo + w_], kvp2[g % 2][:, 0:w_]),
                        inc="ve", key=(st, "sup", n, g))
                else:
                    prog.op("ve", lambda e, lo=lo, w_=w_, g=g: e.tensor_tensor(
                        state[:, lo:lo + w_], state[:, lo:lo + w_],
                        kvp2[g % 2][:, 0:w_], ALU.add),
                        inc="ve", key=(st, "sup", n, g))
                SC[("kvp_war", g % 2)] = (st, "sup", n, g)

            def emit_castA(n):
                prog.wait("ac", (st, "sup", n, 2))
                if n >= 1:
                    prog.wait("ac", (st, "ob", n - 1, CA_H))
                prog.op("ac", lambda e, n=n: e.copy(
                    state_bf2[n % 2][:, 0:CA_C], state[:, 0:CA_C]),
                    inc="ac", key=(st, "castA", n))

            def emit_castB(n):
                prog.wait("ac", (st, "sup", n, NG - 1))
                if n >= 1:
                    prog.wait("ac", (st, "ob", n - 1, H - 1))
                prog.op("ac", lambda e, n=n: e.copy(
                    state_bf2[n % 2][:, CA_C:H * 129], state[:, CA_C:H * 129]),
                    inc="ac", key=(st, "castB", n))

            def mk_pso(n, h, qb):
                hp = h % 2

                def pso():
                    voff = n * 2064 + h * 129
                    prog.wait_if("pe", SC.get(("a_pso", hp)))
                    prog.wait("pe", (st, "atm", n, h))
                    if n > 0:
                        prog.wait("pe", (st, "castA" if h <= CA_H else "castB", n - 1))
                        prog.op("pe", lambda e, qb=qb, h=h, hp=hp, n=n: e.matmul(
                            pso2[hp][:, 0:129], qb[:, h * 128:(h + 1) * 128],
                            state_bf2[(n - 1) % 2][:, h * 129:(h + 1) * 129],
                            start=True, stop=False))
                    prog.op("pe", lambda e, voff=voff, n=n, hp=hp: e.matmul(
                        pso2[hp][:, 0:129], atbf[:, hp * 128:(hp + 1) * 128],
                        vaug[:, voff:voff + 129],
                        start=(n == 0), stop=True), inc="pe", key=(st, "ob", n, h))
                    SC[("a_atb", hp)] = (st, "ob", n, h)
                    prog.wait("ve", (st, "ob", n, h))
                    prog.op("ve", lambda e, hp=hp: e.tensor_scalar_add(
                        zcol[:, hp * 2:hp * 2 + 1], pso2[hp][:, 128:129], EPS_A))
                    prog.op("ve", lambda e, hp=hp: e.reciprocal(
                        zcol[:, hp * 2 + 1:hp * 2 + 2], zcol[:, hp * 2:hp * 2 + 1]))
                    prog.wait_if("ve", SC.get(("a_obf", hp)))
                    prog.op("ve", lambda e, hp=hp: e.tensor_scalar_mul(
                        obf[:, hp * 128:(hp + 1) * 128], pso2[hp][:, 0:128],
                        zcol[:, hp * 2 + 1:hp * 2 + 2]),
                        inc="ve", key=(st, "odv", n, h))
                    SC[("a_pso", hp)] = (st, "odv", n, h)
                return pso

            def mk_tail(n, h, ostg):
                hp = h % 2

                def tail():
                    prog.wait("pe", (st, "odv", n, h))
                    prog.wait_if("pe", SC.get("a_psTb"))
                    prog.op("pe", lambda e, hp=hp: e.transpose(
                        psTb[:, :], obf[:, hp * 128:(hp + 1) * 128], ident[:, :]),
                        inc="pe", key=(st, "ot", n, h))
                    SC[("a_obf", hp)] = (st, "ot", n, h)
                    prog.wait("ac", (st, "ot", n, h))
                    if h == 0:
                        if n >= 2:
                            prog.wait("ac", (st, "pj", n - 2, 3))
                        else:
                            prog.wait("ac", vdone)
                    prog.op("ac", lambda e, ostg=ostg, h=h: e.copy(
                        ostg[:, h * 128:(h + 1) * 128], psTb[:, :]),
                        inc="ac", key=(st, "oe", n, h))
                    SC["a_psTb"] = (st, "oe", n, h)
                    if h == 10 and n < NTC - 1:
                        emit_castA(n)
                return tail

            pend_p = [None]
            pend_t = [None, None]
            rt_slots = {}

            def emit_loads(m):
                for (bl, src_, nm, spk, pool) in ((qblkb[m % 2], qs_s, "qb", "q", "dc"),
                                                  (kblkb[m % 2], ks_s, "kb", "k", "dg"),):
                    if m >= 2:
                        prog.wait("sy", (st, "blku", nm, m - 2))
                    else:
                        prog.wait("sy", vdone)
                    prog.wait("sy", (st, "qsp", spk, m, 1))
                    prog.dma("sy", bl[:, :], src_[m, :, :], f"{pool}{m % 2}",
                             key=(st, nm, m))
                if m < NTC - 1:
                    if m >= 2:
                        prog.wait("gp", (st, "kv", m - 2, H - 1))
                    else:
                        prog.wait("gp", vdone)
                    prog.wait("gp", (st, "ksp", m, 0))
                    prog.wait("gp", (st, "ksp", m, 1))
                    prog.dma("gp", ktblk[m % 2][:, :], kts_s[m, :, :],
                             f"dg{2 + m % 2}", key=(st, "ktb", m))
                rs, rt, rwar = ring_rtB.acquire()
                if rwar is not None:
                    prog.wait_if("gp", rwar)
                else:
                    prog.wait("gp", vdone)
                prog.dma("gp", rt[:, :], res_src[m * 128:(m + 1) * 128, :], f"db{rs}",
                         key=(st, "rr", m))
                rt_slots[m] = (rs, rt)

            emit_loads(0)
            for n in range(NTC):
                last_n = n == NTC - 1
                if not last_n:
                    emit_loads(n + 1)
                qb, kb = qblkb[n % 2], kblkb[n % 2]
                ostg = ostgs[n % 2]
                rs, rt = rt_slots.pop(n)
                prog.wait("pe", (st, "kb", n))
                prog.wait("pe", (st, "qb", n))
                if not last_n:
                    prog.wait("pe", (st, "ktb", n))
                prog.wait("pe", (st, "vev", 0, n))
                prog.wait("pe", (st, "vev", 1, n))
                for h in range(H):
                    hp = h % 2
                    g = h // GSZ
                    prog.wait_if("pe", SC.get(("a_psa", hp)))
                    prog.op("pe", lambda e, kb=kb, qb=qb, h=h, hp=hp: e.matmul(
                        psa2[hp][:, :], kb[:, h * 128:(h + 1) * 128],
                        qb[:, h * 128:(h + 1) * 128], start=True, stop=True),
                        inc="pe", key=(st, "at", n, h))
                    if not last_n:
                        if h % GSZ == 0:
                            prog.wait_if("pe", SC.get(("kvp_war", g % 2)))
                        prog.op("pe", lambda e, h=h, g=g, n=n: e.matmul(
                            kvp2[g % 2][:, (h - g * GSZ) * 129:(h - g * GSZ + 1) * 129],
                            ktblk[n % 2][:, h * 128:(h + 1) * 128],
                            vaug[:, n * 2064 + h * 129: n * 2064 + (h + 1) * 129],
                            start=True, stop=True), inc="pe", key=(st, "kv", n, h))
                    prog.wait("ve", (st, "at", n, h))
                    prog.wait_if("ve", SC.get(("a_atb", hp)))
                    prog.op("ve", lambda e, hp=hp: e.tensor_tensor(
                        atbf[:, hp * 128:(hp + 1) * 128], psa2[hp][:, :],
                        triu[:, :], ALU.mult),
                        inc="ve", key=(st, "atm", n, h))
                    SC[("a_psa", hp)] = (st, "atm", n, h)
                    if not last_n and (h % GSZ == GSZ - 1 or h == H - 1):
                        emit_add(n, g)
                    if pend_p[0] is not None:
                        pend_p[0]()
                    pend_p[0] = mk_pso(n, h, qb)
                    if pend_t[0] is not None:
                        pend_t[0]()
                    pend_t[0] = pend_t[1]
                    pend_t[1] = mk_tail(n, h, ostg)
                pend_p[0]()
                pend_p[0] = None
                pend_t[0]()
                pend_t[1]()
                pend_t[0] = pend_t[1] = None
                if not last_n:
                    emit_castB(n)
                prog.alias((st, "blku", "qb", n), (st, "ob", n, H - 1))
                prog.alias((st, "blku", "kb", n), (st, "at", n, H - 1))
                prog.wait("pe", (st, "oe", n, H - 1))
                if n == 0:
                    prog.wait("pe", (st, "wo", NCC - 1))
                for pc in range(4):
                    if n >= 1 or pc >= 1:
                        pn_, pcc = (n, pc - 1) if pc >= 1 else (n - 1, 3)
                        prog.wait("pe", (st, "rad", pn_, pcc))
                    for hc in range(NCC):
                        prog.op("pe", lambda e, hc=hc, pc=pc, ostg=ostg: e.matmul(
                            ppj[:, :],
                            ostg[:, hc * 128:(hc + 1) * 128],
                            wbuf[:, hc * D + pc * 512: hc * D + (pc + 1) * 512],
                            start=(hc == 0), stop=False))
                    prog.op("pe", lambda e, pc=pc: e.matmul(
                        ppj[:, :], ones_row[0:1, :],
                        brow[0:1, D + pc * 512: D + (pc + 1) * 512],
                        start=False, stop=True),
                        inc="pe", key=(st, "pj", n, pc))
                    prog.wait("ve", (st, "pj", n, pc))
                    if pc == 0:
                        prog.wait("ve", (st, "rr", n))
                    prog.op("ve", lambda e, rt=rt, pc=pc: e.tensor_tensor(
                        rt[:, pc * 512:(pc + 1) * 512], rt[:, pc * 512:(pc + 1) * 512],
                        ppj[:, :], ALU.add),
                        inc="ve", key=(st, "rad", n, pc))
                prog.wait("sy", (st, "rad", n, 3))
                prog.dma("sy", res_dst[n * 128:(n + 1) * 128, :], rt[:, :], f"df{rs}",
                         key=(st, "rst", n))
                ring_rtB.consumer(rs, (st, "rst", n))

        def emit_w2prep(st):
            pps2 = SC["pps2"]
            wts = SC["p_wt"]
            stg2 = SC["p_stg"]
            for a in range(64):
                wt = wts[a % 4]
                pq = pps2[(a % 2) * 4:(a % 2) * 4 + 4]
                if a >= 4:
                    prog.wait("gp", (st, "s1m", a - 4))
                prog.dma("gp", wt[:, :], w2t[a * 128:(a + 1) * 128, :], f"da{a % 4}",
                         key=(st, "s1w", a))
                if a >= 2:
                    prog.wait("pe", (st, "s1e", a - 2))
                prog.wait("pe", (st, "s1w", a))
                for nq in range(4):
                    prog.op("pe", lambda e, wt=wt, nq=nq, pq=pq: e.matmul(
                        pq[nq][:, :], h128[:, :],
                        wt[:, nq * 512:(nq + 1) * 512], start=True, stop=True),
                        inc=None if nq < 3 else "pe",
                        key=None if nq < 3 else (st, "s1m", a))
                ee = "ac" if a % 2 == 0 else "ve"
                prog.wait(ee, (st, "s1m", a))
                bt = stg2[a % 4]
                if a >= 4:
                    prog.wait(ee, (st, "s1s", a - 4))
                for nq in range(4):
                    if ee == "ac":
                        prog.op("ac", lambda e, bt=bt, nq=nq, pq=pq: e.mul(
                            bt[:, nq * 512:(nq + 1) * 512], pq[nq][:, :], FWHT_SCALE),
                            inc=None if nq < 3 else "ac",
                            key=None if nq < 3 else (st, "s1e", a))
                    else:
                        prog.op("ve", lambda e, bt=bt, nq=nq, pq=pq: e.tensor_scalar_mul(
                            bt[:, nq * 512:(nq + 1) * 512], pq[nq][:, :], FWHT_SCALE),
                            inc=None if nq < 3 else "ve",
                            key=None if nq < 3 else (st, "s1e", a))
                prog.wait("sy", (st, "s1e", a))
                prog.dma("sy", b_s[a, :, :], bt[:, :],
                         ["de0", "de1", "dc0", "dc1"][a % 4], key=(st, "s1s", a))
            for j in range(128):
                bj = wts[j % 4]
                pq = pps2[(j % 2) * 4:(j % 2) * 4 + 4]
                if j >= 4:
                    prog.wait("sy", (st, "s2m", j - 4))
                if j < 4:
                    for aa in range(60, 64):
                        prog.wait("sy", (st, "s1s", aa))
                prog.dma("sy", bj[0:64, :], b_s[:, j, :], f"dg{j % 4}", key=(st, "s2l", j))
                if j >= 2:
                    prog.wait("pe", (st, "s2e", j - 2))
                prog.wait("pe", (st, "s2l", j))
                for nq in range(4):
                    prog.op("pe", lambda e, bj=bj, nq=nq, pq=pq: e.matmul(
                        pq[nq][0:64, :], h64[:, :],
                        bj[0:64, nq * 512:(nq + 1) * 512], start=True, stop=True),
                        inc=None if nq < 3 else "pe",
                        key=None if nq < 3 else (st, "s2m", j))
                ee = "ac" if j % 2 == 0 else "ve"
                prog.wait(ee, (st, "s2m", j))
                ct = stg2[j % 4]
                if j >= 4:
                    prog.wait(ee, (st, "s2s", j - 4))
                for nq in range(4):
                    if ee == "ac":
                        prog.op("ac", lambda e, ct=ct, nq=nq, pq=pq: e.copy(
                            ct[0:64, nq * 512:(nq + 1) * 512], pq[nq][0:64, :]),
                            inc=None if nq < 3 else "ac",
                            key=None if nq < 3 else (st, "s2e", j))
                    else:
                        prog.op("ve", lambda e, ct=ct, nq=nq, pq=pq: e.tensor_copy(
                            ct[0:64, nq * 512:(nq + 1) * 512], pq[nq][0:64, :]),
                            inc=None if nq < 3 else "ve",
                            key=None if nq < 3 else (st, "s2e", j))
                prog.wait("gp", (st, "s2e", j))
                prog.dma("gp", w2p_s[:, j, :], ct[0:64, :],
                         ["ga0", "ga1", "ga2", "ga3"][j % 4], key=(st, "s2s", j))

        def emit_ffn(st, res_src, res_dst, first_pass):
            psG, psL = SC["psG"], SC["psL"]
            ppcs = SC["ppcs"]
            acc, sg = SC["acc"], SC["sg"]
            w12t, w34t = SC["w12t"], SC["w34t"]
            hgt, w2gt = SC["hgt"], SC["w2gt"]
            ring_rtF = SC["ring_rtF"]
            xb = SC["xb"]
            xb_done = (st, "xe", NTC - 1, NCC - 1)
            xn_done = xb_done
            for half in range(2):
                for hc in range(NHC):
                    g, gi = hc // G, hc % G
                    w1tile, w3tile = w12t[hc % 2], w34t[hc % 2]
                    war = None
                    if hc >= 2:
                        war = (st, "lmm", half, hc - 2)
                    elif half == 1:
                        war = (st, "lmm", 0, NHC - 2 + hc)
                    else:
                        war = xn_done
                    prog.wait("gp", war)
                    prog.dma("gp", w1tile[:, :], w1t[hc, :, :], f"da{hc % 2}",
                             key=(st, "w1", half, hc))
                    prog.dma("gp", w3tile[:, :], w3t[hc, :, :], f"ga{hc % 2}",
                             key=(st, "w3", half, hc))
                    w2war = None
                    if g >= 1:
                        w2war = (st, "cdone", half, g - 1)
                    elif half == 1:
                        w2war = (st, "cdone", 0, NGR - 1)
                    else:
                        w2war = xn_done
                    prog.wait("sy", w2war)
                    prog.dma("sy", w2gt[gi][:, :], w2p_s[hc, :, :], "sw2",
                             key=(st, "w2", half, hc))
                    prog.wait("pe", (st, "w1", half, hc))
                    if hc == 0 and half == 0:
                        prog.wait("pe", xb_done)
                    if hc >= 1:
                        prog.wait("pe", (st, "sg", half, hc - 1))
                    elif half == 1:
                        prog.wait("pe", (st, "sg", 0, NHC - 1))
                    for cc in range(NCC):
                        for nq in range(2):
                            prog.op("pe", lambda e, cc=cc, nq=nq, half=half, w1tile=w1tile: e.matmul(
                                psG[:, nq * 512:(nq + 1) * 512],
                                w1tile[:, cc * 128:(cc + 1) * 128],
                                xb[:, cc * T + half * 1024 + nq * 512:
                                   cc * T + half * 1024 + (nq + 1) * 512],
                                start=(cc == 0), stop=(cc == NCC - 1)),
                                inc=None if (cc, nq) != (NCC - 1, 1) else "pe",
                                key=None if (cc, nq) != (NCC - 1, 1) else (st, "gmm", half, hc))
                    prog.wait("ac", (st, "gmm", half, hc))
                    hp = None
                    if hc >= 1:
                        hp = (half, hc - 1)
                    elif half == 1:
                        hp = (0, NHC - 1)
                    if hp is not None:
                        prog.wait("ac", (st, "hpr") + hp)
                    prog.op("ac", lambda e: e.activation(sg[:, :], psG[:, :], AF.Silu),
                            inc="ac", key=(st, "sg", half, hc))
                    prog.wait("pe", (st, "w3", half, hc))
                    if hp is not None:
                        prog.wait("pe", (st, "hpr") + hp)
                    for cc in range(NCC):
                        for nq in range(2):
                            prog.op("pe", lambda e, cc=cc, nq=nq, half=half, w3tile=w3tile: e.matmul(
                                psL[:, nq * 512:(nq + 1) * 512],
                                w3tile[:, cc * 128:(cc + 1) * 128],
                                xb[:, cc * T + half * 1024 + nq * 512:
                                   cc * T + half * 1024 + (nq + 1) * 512],
                                start=(cc == 0), stop=(cc == NCC - 1)),
                                inc=None if (cc, nq) != (NCC - 1, 1) else "pe",
                                key=None if (cc, nq) != (NCC - 1, 1) else (st, "lmm", half, hc))
                    prog.wait("ve", (st, "lmm", half, hc))
                    prog.wait("ve", (st, "sg", half, hc))
                    hwar = None
                    if g >= 1:
                        hwar = (st, "cdone", half, g - 1)
                    elif half == 1:
                        hwar = (st, "cdone", 0, NGR - 1)
                    else:
                        hwar = xn_done
                    if gi == 0 or (g == 0 and half == 0):
                        prog.wait("ve", hwar)
                    prog.op("ve", lambda e, htile=hgt[gi]: e.scalar_tensor_tensor(
                        htile[:, :], psL[:, :], 1.0, sg[:, :], ALU.bypass, ALU.mult),
                        inc="ve", key=(st, "hpr", half, hc))
                    if gi == G - 1:
                        prog.wait("pe", (st, "hpr", half, hc))
                        prog.wait("pe", (st, "w2", half, hc))
                        for tcl in range(8):
                            for dh in range(2):
                                ppc = ppcs[dh]
                                cwar = None
                                if tcl >= 1:
                                    cwar = (st, "cad", half, g, tcl - 1, dh)
                                elif g >= 1:
                                    cwar = (st, "cad", half, g - 1, 7, dh)
                                elif half == 1:
                                    cwar = (st, "cad", 0, NGR - 1, 7, dh)
                                if cwar is not None:
                                    prog.wait("pe", cwar)
                                for i in range(G):
                                    for nq in range(2):
                                        prog.op("pe", lambda e, ppc=ppc, i=i, nq=nq, tcl=tcl, dh=dh: e.matmul(
                                            ppc[:, nq * 512:(nq + 1) * 512],
                                            hgt[i][:, tcl * 128:(tcl + 1) * 128],
                                            w2gt[i][:, dh * 1024 + nq * 512:
                                                    dh * 1024 + (nq + 1) * 512],
                                            start=(i == 0), stop=(i == G - 1)),
                                            inc=None if (i, nq) != (G - 1, 1) else "pe",
                                            key=None if (i, nq) != (G - 1, 1) else (st, "cmm", half, g, tcl, dh))
                                prog.wait("ve", (st, "cmm", half, g, tcl, dh))
                                dsto = tcl * D + dh * 1024
                                tc_g = half * 8 + tcl
                                if g == 0:
                                    rs, rt, rwar = ring_rtF.acquire()
                                    prog.wait_if("sy", rwar)
                                    if half == 0 and g == 0 and tcl < 2:
                                        prog.wait("sy", xn_done)
                                    prog.dma("sy", rt[:, :],
                                             res_src[tc_g * 128:(tc_g + 1) * 128,
                                                     dh * 1024:(dh + 1) * 1024],
                                             f"db{rs}", key=(st, "frr", half, tcl, dh))
                                    prog.wait("ve", (st, "frr", half, tcl, dh))
                                    if half == 1:
                                        prog.wait("ve", (st, "fst", 0, tcl, dh))
                                    prog.op("ve", lambda e, ppc=ppc, dsto=dsto, rt=rt: e.tensor_tensor(
                                        acc[:, dsto:dsto + 1024], rt[:, :], ppc[:, :], ALU.add),
                                        inc="ve", key=(st, "cad", half, g, tcl, dh))
                                    ring_rtF.consumer(rs, (st, "cad", half, g, tcl, dh))
                                else:
                                    prog.op("ve", lambda e, ppc=ppc, dsto=dsto: e.tensor_tensor(
                                        acc[:, dsto:dsto + 1024], acc[:, dsto:dsto + 1024],
                                        ppc[:, :], ALU.add),
                                        inc="ve", key=(st, "cad", half, g, tcl, dh))
                                if g == NGR - 1:
                                    prog.wait("sy", (st, "cad", half, g, tcl, dh))
                                    slot = (tcl * 2 + dh) % 4
                                    prog.wait_if("sy", SC.get(("fst_prev", slot)))
                                    prog.dma("sy",
                                             res_dst[tc_g * 128:(tc_g + 1) * 128,
                                                     dh * 1024:(dh + 1) * 1024],
                                             acc[:, dsto:dsto + 1024],
                                             f"df{slot}",
                                             key=(st, "fst", half, tcl, dh))
                                    SC[("fst_prev", slot)] = (st, "fst", half, tcl, dh)
                        prog.alias((st, "cdone", half, g), (st, "cmm", half, g, 7, 1))

        def emit_attn_b_attention(st):
            psS, psO4, psT2 = SC["psS"], SC["psO4"], SC["psT2"]
            qstr, kpre, ostgb = SC["qstr"], SC["kpre"], SC["ostgb"]
            vaug, atbf, obf = SC["vaug"], SC["atbf"], SC["obf"]
            vdone = (st, "pmm", "v", 1, NTC - 1)
            guard_keys = [vdone, (st, "qe", "k", 1, NTC - 1, 7),
                          (st, "fmd", "k", 1, NTC - 1), (st, "vev", 1, NTC - 1),
                          (st, "qsp", "k", NTC - 2, 1), (st, "qsp", "k", NTC - 1, 1)]
            for geng in ("pe", "ve", "ac", "sy"):
                for gk in guard_keys:
                    prog.wait(geng, gk)

            sb_hist = [0]
            sb_prev = [None, None]
            pend_t = [None]
            pend_o = [None]
            pt_prev = [None, None]

            def prev_oq(s, h, ql):
                if h > 0:
                    return (s, h - 1, ql)
                if s > 0:
                    return (s - 1, H - 1, ql)
                return None

            for s in range(4):
                for j in range(4):
                    tc = 4 * s + j
                    if s >= 1:
                        prog.wait("sy", (st, "quse", s - 1))
                    prog.wait("sy", (st, "qsp", "q", tc, 1))
                    prog.dma("sy", qstr[:, j * D:(j + 1) * D], qs_s[tc, :, :], f"dd{j}",
                             key=(st, "qld", s, j))
                    prog.wait("sy", (st, "qsp", "k", tc, 1))
                    prog.dma("sy", kpre[:, tc * D:(tc + 1) * D], ks_s[tc, :, :], f"dg{tc % 4}",
                             key=(st, "kld", tc))
                for h in range(H):
                    if h == 0:
                        for j in range(4):
                            prog.wait("pe", (st, "qld", s, j))
                        for tc in range(4 * s + 4):
                            if tc >= 4 * s:
                                prog.wait("pe", (st, "kld", tc))
                    prog.wait("pe", (st, "vev", h // 8, 4 * s + 3))
                    for kb in range(4 * s + 4):
                        qlo = max(kb, 4 * s)
                        par = sb_hist[0] % 2
                        off = par * 512
                        sb_hist[0] += 1
                        pw = sb_prev[par]
                        if pw is not None:
                            prog.wait("pe", (st, "sq") + pw)
                            prog.wait("ac", (st, "oml") + pw)
                        for qc in range(qlo, 4 * s + 4):
                            ql = qc - 4 * s
                            prog.op("pe", lambda e, kb=kb, h=h, ql=ql, off=off: e.matmul(
                                psS[:, off + ql * 128: off + (ql + 1) * 128],
                                kpre[:, kb * D + h * 128: kb * D + (h + 1) * 128],
                                qstr[:, ql * D + h * 128: ql * D + (h + 1) * 128],
                                start=True, stop=True),
                                inc=None if qc < 4 * s + 3 else "pe",
                                key=None if qc < 4 * s + 3 else (st, "smm", s, h, kb))
                        prog.wait("ac", (st, "smm", s, h, kb))
                        ql0 = qlo - 4 * s
                        nq_ = 4 - ql0
                        prog.op("ac", lambda e, ql0=ql0, nq_=nq_, off=off: e.activation(
                            atbf[:, off + ql0 * 128: off + ql0 * 128 + nq_ * 128],
                            psS[:, off + ql0 * 128: off + ql0 * 128 + nq_ * 128], AF.Square),
                            inc="ac", key=(st, "sq", s, h, kb))
                        if kb >= 4 * s:
                            prog.wait("ve", (st, "sq", s, h, kb))
                            prog.op("ve", lambda e, kb=kb, s=s, off=off: e.tensor_tensor(
                                atbf[:, off + (kb - 4 * s) * 128: off + (kb - 4 * s + 1) * 128],
                                atbf[:, off + (kb - 4 * s) * 128: off + (kb - 4 * s + 1) * 128],
                                triu[:, :], ALU.mult),
                                inc="ve", key=(st, "msk", s, h, kb))

                        def mk_omm(s=s, h=h, kb=kb, qlo=qlo, off=off):
                            def omm():
                                if kb >= 4 * s:
                                    prog.wait("pe", (st, "msk", s, h, kb))
                                else:
                                    prog.wait("pe", (st, "sq", s, h, kb))
                                if kb == 0:
                                    for ql in range(4):
                                        pv = prev_oq(s, h, ql)
                                        if pv is not None:
                                            prog.wait("pe", (st, "odv") + pv)
                                for qc in range(qlo, 4 * s + 4):
                                    ql = qc - 4 * s
                                    voff = kb * 2064 + h * 129
                                    last = (kb == qc or qc == 4 * s + 3)
                                    prog.op("pe", lambda e, ql=ql, voff=voff, kb=kb, qc=qc, off=off: e.matmul(
                                        psO4[ql][:, 0:129],
                                        atbf[:, off + ql * 128: off + (ql + 1) * 128],
                                        vaug[:, voff:voff + 129],
                                        start=(kb == 0), stop=(kb == qc)),
                                        inc="pe" if last else None,
                                        key=(st, "omq", s, h, kb, ql) if last else None)
                                prog.alias((st, "oml", s, h, kb), (st, "omq", s, h, kb, 3))
                            return omm
                        if pend_o[0] is not None:
                            pend_o[0]()
                        pend_o[0] = mk_omm()
                        sb_prev[par] = (s, h, kb)
                    if pend_o[0] is not None:
                        pend_o[0]()
                        pend_o[0] = None
                    if pend_t[0] is not None:
                        pend_t[0]()
                    for ql in range(4):
                        qc = 4 * s + ql
                        prog.wait("ve", (st, "omq", s, h, qc, ql))
                        prog.op("ve", lambda e, ql=ql: e.tensor_scalar_add(
                            zcol[:, ql:ql + 1], psO4[ql][:, 128:129], EPS_B))
                        prog.op("ve", lambda e, ql=ql: e.reciprocal(
                            zcol[:, 4 + ql:5 + ql], zcol[:, ql:ql + 1]),
                            inc="ve", key=(st, "rz", s, h, ql))
                    for ql in range(4):
                        prog.wait("ac", (st, "rz", s, h, ql))
                        pv = prev_oq(s, h, ql)
                        if pv is not None:
                            prog.wait("ac", (st, "bot") + pv)
                        prog.op("ac", lambda e, ql=ql: e.activation(
                            obf[:, ql * 128:(ql + 1) * 128], psO4[ql][:, 0:128],
                            AF.Copy, scale=zcol[:, 4 + ql:5 + ql]),
                            inc="ac", key=(st, "odv", s, h, ql))

                    def mk_tail(s=s, h=h):
                        def tail():
                            for ql in range(4):
                                prog.wait("pe", (st, "odv", s, h, ql))
                                ppar = pt_prev[ql % 2]
                                if ppar is not None:
                                    prog.wait("pe", (st, "boe") + ppar)
                                pt_ = psT2[ql % 2]
                                prog.op("pe", lambda e, ql=ql, pt_=pt_: e.transpose(
                                    pt_[:, :], obf[:, ql * 128:(ql + 1) * 128], ident[:, :]),
                                    inc="pe", key=(st, "bot", s, h, ql))
                                prog.wait("ac", (st, "bot", s, h, ql))
                                if h == 0:
                                    if s >= 1:
                                        prog.wait("ac", (st, "ost", s - 1, ql))
                                    else:
                                        prog.wait("ac", vdone)
                                prog.op("ac", lambda e, ql=ql, h=h, pt_=pt_: e.copy(
                                    ostgb[ql][:, h * 128:(h + 1) * 128], pt_[:, :]),
                                    inc="ac", key=(st, "boe", s, h, ql))
                                pt_prev[ql % 2] = (s, h, ql)
                        return tail
                    pend_t[0] = mk_tail()
                if pend_t[0] is not None:
                    pend_t[0]()
                    pend_t[0] = None
                for ql in range(4):
                    prog.wait("sy", (st, "boe", s, H - 1, ql))
                    prog.dma("sy", os_s[4 * s + ql, :, :], ostgb[ql][:, :], f"df{ql}",
                             key=(st, "ost", s, ql))
                prog.alias((st, "quse", s), (st, "omq", s, H - 1, 4 * s + 3, 3))

        def emit_attn_b_proj(st, res_src, res_dst):
            ppj = SC["ppj"]
            wbuf = SC["wbuf"]
            oblks = SC["oblks"]
            ring_rtB = SC["ring_rtB"]
            attn_done = (st, "quse", 3)
            guard_keys = [attn_done, (st, "odv", 3, H - 1, 3),
                          (st, "boe", 3, H - 1, 3), (st, "sq", 3, H - 1, 15)] + \
                         [(st, "ost", 3, ql) for ql in range(4)]
            for geng in ("pe", "ve", "ac", "sy", "gp"):
                for gk in guard_keys:
                    prog.wait(geng, gk)
            for cc in range(NCC):
                prog.dma("gp", wbuf[:, cc * D:(cc + 1) * D],
                         wo_b[cc * 128:(cc + 1) * 128, :], "gw", key=(st, "wo", cc))
            for n in range(NTC):
                ob = oblks[n % 2]
                if n >= 2:
                    prog.wait("sy", (st, "puse", n - 2))
                else:
                    prog.wait("sy", attn_done)
                prog.wait("sy", (st, "ost", n // 4, n % 4))
                prog.dma("sy", ob[:, :], os_s[n, :, :], f"dc{n % 2}", key=(st, "old", n))
                rs, rt, rwar = ring_rtB.acquire()
                if rwar is not None:
                    prog.wait("sy", rwar)
                else:
                    prog.wait("sy", attn_done)
                prog.dma("sy", rt[:, :], res_src[n * 128:(n + 1) * 128, :], f"db{rs}",
                         key=(st, "rr", n))
                prog.wait("pe", (st, "old", n))
                if n == 0:
                    prog.wait("pe", (st, "wo", NCC - 1))
                for pc in range(4):
                    pp = ppj[pc % 2]
                    if n >= 1 or pc >= 2:
                        pn_, pcc = (n, pc - 2) if pc >= 2 else (n - 1, pc + 2)
                        prog.wait("pe", (st, "rad", pn_, pcc))
                    for hc in range(NCC):
                        prog.op("pe", lambda e, pp=pp, ob=ob, hc=hc, pc=pc: e.matmul(
                            pp[:, :],
                            ob[:, hc * 128:(hc + 1) * 128],
                            wbuf[:, hc * D + pc * 512: hc * D + (pc + 1) * 512],
                            start=(hc == 0), stop=(hc == NCC - 1)),
                            inc=None if hc != NCC - 1 else "pe",
                            key=None if hc != NCC - 1 else (st, "pj", n, pc))
                    prog.wait("ve", (st, "pj", n, pc))
                    if pc == 0:
                        prog.wait("ve", (st, "rr", n))
                    prog.op("ve", lambda e, rt=rt, pp=pp, pc=pc: e.tensor_tensor(
                        rt[:, pc * 512:(pc + 1) * 512], rt[:, pc * 512:(pc + 1) * 512],
                        pp[:, :], ALU.add),
                        inc="ve", key=(st, "rad", n, pc))
                prog.alias((st, "puse", n), (st, "pj", n, 3))
                prog.wait("sy", (st, "rad", n, 3))
                prog.dma("sy", res_dst[n * 128:(n + 1) * 128, :], rt[:, :], f"de{rs}",
                         key=(st, "rst", n))
                ring_rtB.consumer(rs, (st, "rst", n))

        flushed = {e: 0 for e in ENGINES}

        def flush_block():
            for sn in prog.all_dma_sems():
                v = prog.cnt[sn]
                if v > 0:
                    eng = "gp" if sn in ("gw", "gwb") or sn.startswith(("da", "ga")) else "sy"
                    prog.ops[eng].append(
                        lambda e, sem=prog.sems[sn], v=v: e.wait_ge(sem, v))
            start_idx = dict(flushed)
            with nc.Block() as blk:
                def mk(eng_name):
                    def run(eng):
                        for f in prog.ops[eng_name][start_idx[eng_name]:]:
                            f(eng)
                    return run
                blk.tensor(mk("pe"))
                blk.vector(mk("ve"))
                blk.scalar(mk("ac"))
                blk.gpsimd(mk("gp"))
                blk.sync(mk("sy"))
            for e in ENGINES:
                flushed[e] = len(prog.ops[e])

        emit_const_loads()
        flush_block()

        if n_sub >= 2:
            with ExitStack() as s0:
                SC["pps2"] = [s0.enter_context(nc.psum_tensor(f"p_q{i}", [128, 512], F32))
                              for i in range(8)]
                SC["p_wt"] = [s0.enter_context(nc.sbuf_tensor(f"p_w{i}", [128, D], BF16))
                              for i in range(4)]
                SC["p_stg"] = [s0.enter_context(nc.sbuf_tensor(f"p_s{i}", [128, D], BF16))
                               for i in range(4)]
                emit_w2prep(90)
                flush_block()

        stage_specs = []
        if n_sub >= 1:
            stage_specs.append(("attn_a", 0))
        if n_sub >= 2:
            stage_specs.append(("ffn", 1))
        if n_sub >= 3:
            stage_specs.append(("attn_b", 2))
        if n_sub >= 4:
            stage_specs.append(("ffn", 3))

        for kind, idx in stage_specs:
            st = idx
            res_src, res_dst = res_seq[idx], res_seq[idx + 1]
            if kind in ("attn_a", "attn_b"):
                is_a = kind == "attn_a"
                with ExitStack() as sA:
                    SC["brow"] = sA.enter_context(nc.sbuf_tensor(f"s{st}_br", [1, 2 * D], BF16))
                    for par in range(2):
                        SC.pop(("xb_war", par), None)
                        SC.pop(("xb_rwar", par), None)
                        SC.pop(("elu_war", par), None)
                        SC.pop(("lnb_y_war", par), None)
                        SC.pop(("lnb_r_war", par), None)
                        SC.pop(("lnb_y2_war", par), None)
                    SC.pop("wload_war_q", None); SC.pop("wload_war_k", None)
                    SC.pop("wload_war_v", None)
                    sXW = ExitStack()
                    SC["xb"] = sXW.enter_context(nc.sbuf_tensor(f"s{st}_xb", [128, NCC * T], BF16))
                    SC["wbufH"] = sXW.enter_context(nc.sbuf_tensor(f"s{st}_wh", [128, NCC * 1024], BF16))
                    SC["sqbf"] = sXW.enter_context(nc.sbuf_tensor(f"s{st}_sq", [128, D], BF16))
                    SC["scr_c"] = sXW.enter_context(nc.sbuf_tensor(f"s{st}_sc", [128, 2048], F32))
                    SC["ybf"] = sXW.enter_context(nc.sbuf_tensor(f"s{st}_yb", [128, 2048], BF16))
                    SC["ring_xn"] = Ring([sXW.enter_context(
                        nc.sbuf_tensor(f"s{st}_xn{i}", [128, 1024], BF16)) for i in range(2)])
                    SC["ring_st"] = Ring([sXW.enter_context(
                        nc.sbuf_tensor(f"s{st}_st{i}", [128, 1024], BF16)) for i in range(2)])
                    if not is_a:
                        SC["gamt"] = sXW.enter_context(nc.sbuf_tensor(f"s{st}_gm", [128, D], BF16))
                        SC["y2bf"] = sXW.enter_context(nc.sbuf_tensor(f"s{st}_y2", [128, 2048], BF16))
                    SC["ring_pt"] = Ring([sXW.enter_context(
                        nc.psum_tensor(f"s{st}_pt{i}", [128, 128], BF16)) for i in range(4)])
                    SC["ring_pp"] = Ring([sXW.enter_context(
                        nc.psum_tensor(f"s{st}_pp{i}", [128, 1024], F32)) for i in range(2)])
                    sR = ExitStack()
                    SC["ring_rt"] = Ring([sR.enter_context(
                        nc.sbuf_tensor(f"s{st}_rt{i}", [128, D], F32)) for i in range(4)])
                    emit_xb(st, res_src)
                    sR.close()
                    emit_attn_phaseA(st, kind)
                    SC["vaug"] = sA.enter_context(
                        nc.sbuf_tensor(f"s{st}_va", [128, NTC * H * 129], BF16, side="right"))
                    emit_attn_v(st, kind)
                    sXW.close()
                    if is_a:
                        with ExitStack() as sB:
                            SC["wbuf"] = sB.enter_context(
                                nc.sbuf_tensor(f"s{st}_wo", [128, NCC * D], BF16))
                            SC["psa2"] = [sB.enter_context(
                                nc.psum_tensor(f"s{st}_psa{i}", [128, 128], F32))
                                for i in range(2)]
                            SC["psTb"] = sB.enter_context(
                                nc.psum_tensor(f"s{st}_psTb", [128, 128], BF16))
                            SC["pso2"] = [sB.enter_context(
                                nc.psum_tensor(f"s{st}_pso{i}", [128, 129], F32))
                                for i in range(2)]
                            SC["kvp2"] = [sB.enter_context(
                                nc.psum_tensor(f"s{st}_kvp{i}", [128, 387], F32))
                                for i in range(2)]
                            SC["ppj"] = sB.enter_context(
                                nc.psum_tensor(f"s{st}_ppj", [128, 512], F32))
                            SC["state"] = sB.enter_context(
                                nc.sbuf_tensor(f"s{st}_kv", [128, H * 129], F32))
                            SC["state_bf2"] = [sB.enter_context(
                                nc.sbuf_tensor(f"s{st}_kvb{i}", [128, H * 129], BF16))
                                for i in range(2)]
                            SC["atbf"] = sB.enter_context(
                                nc.sbuf_tensor(f"s{st}_at", [128, 256], BF16))
                            SC["obf"] = sB.enter_context(
                                nc.sbuf_tensor(f"s{st}_ob", [128, 256], BF16))
                            for kk in ("a_psTb",):
                                SC.pop(kk, None)
                            for hp_ in range(2):
                                for kk in ("a_psa", "a_atb", "a_pso", "a_obf",
                                           "kvp_war"):
                                    SC.pop((kk, hp_), None)
                            SC["ostgs"] = [sB.enter_context(
                                nc.sbuf_tensor(f"s{st}_og{i}", [128, D], BF16))
                                for i in range(2)]
                            SC["qblkb"] = [sB.enter_context(
                                nc.sbuf_tensor(f"s{st}_qb{i}", [128, D], BF16))
                                for i in range(2)]
                            SC["kblkb"] = [sB.enter_context(
                                nc.sbuf_tensor(f"s{st}_kb{i}", [128, D], BF16))
                                for i in range(2)]
                            SC["ktblk"] = [sB.enter_context(
                                nc.sbuf_tensor(f"s{st}_kt{i}", [128, D], BF16))
                                for i in range(2)]
                            SC["ring_rtB"] = Ring([sB.enter_context(
                                nc.sbuf_tensor(f"s{st}_rb{i}", [128, D], F32))
                                for i in range(2)])
                            emit_attn_a_phaseB(st, res_src, res_dst)
                            flush_block()
                    else:
                        with ExitStack() as sB:
                            SC["psS"] = sB.enter_context(
                                nc.psum_tensor(f"s{st}_psS", [128, 1024], F32))
                            SC["psO4"] = [sB.enter_context(
                                nc.psum_tensor(f"s{st}_pO{i}", [128, 129], F32))
                                for i in range(4)]
                            SC["psT2"] = [sB.enter_context(
                                nc.psum_tensor(f"s{st}_psTc{i}", [128, 128], BF16))
                                for i in range(2)]
                            SC["qstr"] = sB.enter_context(
                                nc.sbuf_tensor(f"s{st}_qs", [128, 4 * D], BF16))
                            SC["kpre"] = sB.enter_context(
                                nc.sbuf_tensor(f"s{st}_kp", [128, NTC * D], BF16))
                            SC["ostgb"] = [sB.enter_context(
                                nc.sbuf_tensor(f"s{st}_ot{i}", [128, D], BF16))
                                for i in range(4)]
                            SC["atbf"] = sB.enter_context(
                                nc.sbuf_tensor(f"s{st}_atb", [128, 1024], BF16))
                            SC["obf"] = sB.enter_context(
                                nc.sbuf_tensor(f"s{st}_obb", [128, 512], BF16))
                            emit_attn_b_attention(st)
                        with ExitStack() as sC:
                            SC["ppj"] = [sC.enter_context(
                                nc.psum_tensor(f"s{st}_bpj{i}", [128, 512], F32))
                                for i in range(2)]
                            SC["wbuf"] = sC.enter_context(
                                nc.sbuf_tensor(f"s{st}_wob", [128, NCC * D], BF16))
                            SC["oblks"] = [sC.enter_context(
                                nc.sbuf_tensor(f"s{st}_obk{i}", [128, D], BF16))
                                for i in range(2)]
                            SC["ring_rtB"] = Ring([sC.enter_context(
                                nc.sbuf_tensor(f"s{st}_rc{i}", [128, D], F32))
                                for i in range(2)])
                            emit_attn_b_proj(st, res_src, res_dst)
                            flush_block()
            else:
                for par in range(2):
                    SC.pop(("xb_war", par), None)
                    SC.pop(("xb_rwar", par), None)
                for slot in range(4):
                    SC.pop(("fst_prev", slot), None)
                with ExitStack() as sF:
                    SC["xb"] = sF.enter_context(nc.sbuf_tensor(f"f{st}_xb", [128, NCC * T], BF16))
                    sX = ExitStack()
                    SC["sqbf"] = sX.enter_context(nc.sbuf_tensor(f"f{st}_sq", [128, D], BF16))
                    SC["ring_xn"] = Ring([sX.enter_context(
                        nc.sbuf_tensor(f"f{st}_xn{i}", [128, 1024], BF16)) for i in range(2)])
                    SC["ring_pt"] = Ring([sX.enter_context(
                        nc.psum_tensor(f"f{st}_pt{i}", [128, 128], BF16)) for i in range(2)])
                    SC["ring_rt"] = Ring([sX.enter_context(
                        nc.sbuf_tensor(f"f{st}_rt{i}", [128, D], F32)) for i in range(4)])
                    emit_xb(st, res_src)
                    sX.close()
                    with ExitStack() as sG:
                        SC["psG"] = sG.enter_context(nc.psum_tensor(f"f{st}_pG", [128, 1024], F32))
                        SC["psL"] = sG.enter_context(nc.psum_tensor(f"f{st}_pL", [128, 1024], F32))
                        SC["ppcs"] = [sG.enter_context(nc.psum_tensor(f"f{st}_pc{i}", [128, 1024], F32))
                                      for i in range(2)]
                        SC["acc"] = sG.enter_context(nc.sbuf_tensor(f"f{st}_ac", [128, 8 * D], F32))
                        SC["sg"] = sG.enter_context(nc.sbuf_tensor(f"f{st}_sg", [128, 1024], F32))
                        SC["w12t"] = [sG.enter_context(nc.sbuf_tensor(f"f{st}_w1{i}", [128, D], BF16))
                                      for i in range(2)]
                        SC["w34t"] = [sG.enter_context(nc.sbuf_tensor(f"f{st}_w3{i}", [128, D], BF16))
                                      for i in range(2)]
                        SC["hgt"] = [sG.enter_context(nc.sbuf_tensor(f"f{st}_hg{i}", [128, 1024], BF16))
                                     for i in range(G)]
                        SC["w2gt"] = [sG.enter_context(nc.sbuf_tensor(f"f{st}_w2{i}", [128, D], BF16))
                                      for i in range(G)]
                        SC["ring_rtF"] = Ring([sG.enter_context(
                            nc.sbuf_tensor(f"f{st}_rr{i}", [128, 1024], F32)) for i in range(2)])
                        emit_ffn(st, res_src, res_dst, first_pass=(idx == 1))
                        flush_block()

        for sn in prog.all_dma_sems():
            if prog.cnt[sn] > 0:
                prog.op("sy", lambda e, sn=sn: e.wait_ge(sems[sn], prog.cnt[sn]))
        flush_block()

    return nc


def _mk_hadamard(n):
    h = np.array([[1.0]], dtype=np.float64)
    while h.shape[0] < n:
        h = np.block([[h, h], [h, -h]])
    return h


def make_inmaps(inputs):
    bf = ml_dtypes.bfloat16
    x = np.ascontiguousarray(np.asarray(inputs["L0_attn_a_x"], dtype=np.float32))
    B = x.shape[0]

    def f32(a):
        return np.ascontiguousarray(np.asarray(a, dtype=np.float32))

    common = {
        "wq_a": f32(inputs["L0_attn_a_Wq"]), "wk_a": f32(inputs["L0_attn_a_Wk"]),
        "wv_a": f32(inputs["L0_attn_a_Wv"]), "wo_a": f32(inputs["L0_attn_a_Wo"]),
        "bq_a": f32(inputs["L0_attn_a_bq"]).reshape(1, D),
        "bk_a": f32(inputs["L0_attn_a_bk"]).reshape(1, D),
        "bv_a": f32(inputs["L0_attn_a_bv"]).reshape(1, D),
        "bo_a": f32(inputs["L0_attn_a_bo"]).reshape(1, D),
        "wq_b": np.ascontiguousarray(f32(inputs["L0_attn_b_Wq"]).T),
        "wk_b": np.ascontiguousarray(f32(inputs["L0_attn_b_Wk"]).T),
        "wv_b": np.ascontiguousarray(f32(inputs["L0_attn_b_Wv"]).T),
        "wo_b": np.ascontiguousarray(f32(inputs["L0_attn_b_Wo"]).T),
        "gam_t": np.ascontiguousarray(np.tile(f32(inputs["L0_attn_b_gamma"]), (128, H))),
        "bet_row": np.ascontiguousarray(np.tile(f32(inputs["L0_attn_b_beta"]), H)).reshape(1, D),
        "w1t": np.ascontiguousarray(
            f32(inputs["L0_ffn_W1"]).T.reshape(NCC, 128, NHC, 128).transpose(2, 1, 0, 3)
        ).reshape(NHC, 128, D),
        "w3t": np.ascontiguousarray(
            f32(inputs["L0_ffn_W3"]).T.reshape(NCC, 128, NHC, 128).transpose(2, 1, 0, 3)
        ).reshape(NHC, 128, D),
        "w2t": np.ascontiguousarray(f32(inputs["L0_ffn_W2"]).T),
        "ident_c": np.eye(128, dtype=bf),
        "triu_c": np.triu(np.ones((128, 128))).astype(bf),
        "h128_c": _mk_hadamard(128).astype(bf),
        "h64_c": _mk_hadamard(64).astype(bf),
    }
    return [dict(common, x=np.ascontiguousarray(x[b])) for b in range(B)], B


def kernel(**inputs):
    in_maps, B = make_inmaps(inputs)
    nc = build(N_SUB)
    res = run_bass_kernel_spmd(nc, in_maps, core_ids=list(range(B)))
    return np.stack([np.asarray(r["out"], dtype=np.float32) for r in res.results], axis=0)


# version 12
# speedup vs baseline: 1.1464x; 1.0466x over previous
import math
from contextlib import ExitStack

import numpy as np
import ml_dtypes
import concourse.bass as bass
import concourse.mybir as mybir
from concourse.bass_utils import run_bass_kernel_spmd

F32 = mybir.dt.float32
BF16 = mybir.dt.bfloat16
AF = mybir.ActivationFunctionType
ALU = mybir.AluOpType

T = 2048
D = 2048
H = 16
HD = 128
NTC = T // 128
NCC = D // 128
HID = 8192
NHC = HID // 128
G = 8
NGR = NHC // G
LN_EPS = 1e-5
EPS_A = 1e-6
EPS_B = 1e-5
FWHT_SCALE = float(np.float32(1.0 / np.sqrt(np.float64(HID))))
RD = float(1.0 / math.sqrt(HD))

N_SUB = 4

ENGINES = ("pe", "ve", "ac", "gp", "sy")


class Ring:

    def __init__(self, bufs):
        self.bufs = list(bufs)
        self.war = [None] * len(self.bufs)
        self.idx = 0

    def acquire(self):
        s = self.idx % len(self.bufs)
        self.idx += 1
        return s, self.bufs[s], self.war[s]

    def consumer(self, s, eng_key):
        self.war[s] = eng_key


class Prog:
    def __init__(self, nc, sems):
        self.nc = nc
        self.sems = sems
        self.cnt = {k: 0 for k in sems}
        self.keys = {}
        self.ops = {e: [] for e in ENGINES}

    ENGSEM = {"ve": "ve", "ac": "ac", "gp": "gc"}

    def op(self, eng, fn, inc=None, k=1, key=None, dma=False):
        auto = (not dma) and eng in self.ENGSEM
        if auto:
            es = self.ENGSEM[eng]
            if inc is None:
                inc = es
                k = 1
            assert inc == es
            prevv = self.cnt[es]
            if prevv > 0:
                sem0 = self.sems[es]
                self.ops[eng].append(
                    lambda e, sem0=sem0, prevv=prevv: e.wait_ge(sem0, prevv))
        if inc is not None:
            self.cnt[inc] += k
            if key is not None:
                assert key not in self.keys, f"dup key {key}"
                self.keys[key] = (inc, self.cnt[inc])
            sem = self.sems[inc]
            self.ops[eng].append(lambda e, fn=fn, sem=sem, kk=k: fn(e).then_inc(sem, kk))
        else:
            self.ops[eng].append(lambda e, fn=fn: fn(e))

    def dma(self, eng, out, in_, inc, key=None):
        self.op(eng, lambda e, out=out, in_=in_: e.dma_start(out, in_),
                inc=inc, k=16, key=key, dma=True)

    def all_dma_sems(self):
        return [n for n in self.sems if n not in ("pe", "ve", "ac", "gc")]

    def alias(self, new_key, old_key):
        self.keys[new_key] = self.keys[old_key]

    def wait(self, eng, key):
        assert key in self.keys, f"missing key {key}"
        sem_name, val = self.keys[key]
        sem = self.sems[sem_name]
        self.ops[eng].append(lambda e, sem=sem, val=val: e.wait_ge(sem, val))

    def wait_if(self, eng, key):
        if isinstance(key, list):
            for k in key:
                self.wait_if(eng, k)
            return
        if key is not None and key in self.keys:
            self.wait(eng, key)


def build(n_sub=N_SUB):
    nc = bass.Bass("TRN2", target_bir_lowering=False)

    def din(name, shape, dtype=F32):
        return nc.dram_tensor(name, shape, dtype, kind="ExternalInput").ap()

    x_in = din("x", [T, D])
    wq_a = din("wq_a", [D, D]); wk_a = din("wk_a", [D, D])
    wv_a = din("wv_a", [D, D]); wo_a = din("wo_a", [D, D])
    bq_a = din("bq_a", [1, D]); bk_a = din("bk_a", [1, D])
    bv_a = din("bv_a", [1, D]); bo_a = din("bo_a", [1, D])
    wq_b = din("wq_b", [D, D]); wk_b = din("wk_b", [D, D])
    wv_b = din("wv_b", [D, D]); wo_b = din("wo_b", [D, D])
    gam_t = din("gam_t", [128, D])
    bet_row_i = din("bet_row", [1, D])
    w1t = din("w1t", [NHC, 128, D])
    w3t = din("w3t", [NHC, 128, D])
    w2t = din("w2t", [HID, D])
    ident_c = din("ident_c", [128, 128], BF16)
    triu_c = din("triu_c", [128, 128], BF16)
    h128_c = din("h128_c", [128, 128], BF16)
    h64_c = din("h64_c", [64, 64], BF16)

    out_t = nc.dram_tensor("out", [T, D], F32, kind="ExternalOutput").ap()

    res_a = nc.dram_tensor("res_a", [T, D], F32, kind="Internal").ap()
    res_b = nc.dram_tensor("res_b", [T, D], F32, kind="Internal").ap()
    qs_s = nc.dram_tensor("qs_s", [NTC, 128, D], BF16, kind="Internal").ap()
    ks_s = nc.dram_tensor("ks_s", [NTC, 128, D], BF16, kind="Internal").ap()
    kts_s = nc.dram_tensor("kts_s", [NTC, 128, D], BF16, kind="Internal").ap()
    q4_s = nc.dram_tensor("q4_s", [4, 128, H * 512], BF16, kind="Internal").ap()
    os_s = nc.dram_tensor("os_s", [NTC, 128, D], BF16, kind="Internal").ap()
    b_s = nc.dram_tensor("b_s", [64, 128, D], BF16, kind="Internal").ap()
    w2p_s = nc.dram_tensor("w2p_s", [NHC, 128, D], BF16, kind="Internal").ap()
    w1c_s = nc.dram_tensor("w1c_s", [NHC, 128, D], BF16, kind="Internal").ap()
    w3c_s = nc.dram_tensor("w3c_s", [NHC, 128, D], BF16, kind="Internal").ap()

    res_seq = [x_in, res_a, res_b, res_a, out_t]
    if n_sub < 4:
        res_seq[n_sub] = out_t

    with ExitStack() as es:
        E = es.enter_context
        sem_names = (["pe", "ve", "ac", "gc", "gw", "gwb", "sw2",
                      "sc10", "sc11", "sc30", "sc31", "sl"]
                     + [f"da{i}" for i in range(4)]
                     + [f"ga{i}" for i in range(4)]
                     + [f"db{i}" for i in range(4)]
                     + [f"dc{i}" for i in range(2)]
                     + [f"dd{i}" for i in range(4)]
                     + [f"dg{i}" for i in range(4)]
                     + [f"de{i}" for i in range(2)]
                     + [f"df{i}" for i in range(4)])
        sems = {n: E(nc.semaphore(n)) for n in sem_names}
        prog = Prog(nc, sems)

        ident = E(nc.sbuf_tensor("ident", [128, 128], BF16))
        triu = E(nc.sbuf_tensor("triu", [128, 128], BF16))
        h128 = E(nc.sbuf_tensor("h128", [128, 128], BF16))
        h64 = E(nc.sbuf_tensor("h64", [64, 64], BF16))
        ones_row = E(nc.sbuf_tensor("ones_row", [1, 128], BF16))
        cs = E(nc.sbuf_tensor("cs", [128, 104], F32))
        zcol = E(nc.sbuf_tensor("zcol", [128, 8], F32))
        cbias = E(nc.sbuf_tensor("cbias", [128, 2], F32))

        def col(t_, a, n):
            return t_[:, a:a + n]

        SC = {}

        def emit_const_loads():
            prog.dma("sy", ident[:, :], ident_c[:, :], "sl")
            prog.dma("sy", triu[:, :], triu_c[:, :], "sl")
            prog.dma("sy", h128[:, :], h128_c[:, :], "sl")
            prog.dma("sy", h64[:, :], h64_c[:, :], "sl", key="c_all")
            prog.op("gp", lambda e: e.memset(ones_row[:, :], 1.0), inc="gc", key="c_ones")
            prog.op("gp", lambda e: e.memset(cbias[:, 0:1], LN_EPS))
            prog.op("gp", lambda e: e.memset(cbias[:, 1:2], math.log(RD)),
                    inc="gc", key="c_eps")
            prog.wait("pe", "c_all")
            prog.wait("pe", "c_ones")
            prog.wait("ve", "c_all")
            prog.wait("ac", "c_eps")

        def emit_xb(st, res_src):
            ring_rt, ring_xn, ring_pt = SC["ring_rt"], SC["ring_xn"], SC["ring_pt"]
            sqbf = SC["sqbf"]
            xb = SC["xb"]
            for tc in range(NTC):
                par = tc % 2
                base = 0 if par == 0 else 96
                ss, sq, mu, msq, var_, lnv, rstd = (col(cs, base + i, 1) for i in range(7))
                rs, rt, rwar = ring_rt.acquire()
                prog.wait_if("sy", rwar)
                prog.dma("sy", rt[:, :], res_src[tc * 128:(tc + 1) * 128, :], f"db{rs}",
                         key=(st, "xr", tc))
                prog.wait("ve", (st, "xr", tc))
                prog.wait_if("ve", SC.get(("xb_war", par)))
                prog.op("ve", lambda e, rt=rt, ss=ss: e.reduce_sum(
                    ss, rt[:, :], axis=mybir.AxisListType.X))
                prog.op("ve", lambda e, rt=rt, sq=sq: e.scalar_tensor_tensor(
                    sqbf[:, :], rt[:, :], 1.0, rt[:, :], ALU.bypass, ALU.mult,
                    accum_out=sq))
                prog.op("ve", lambda e, mu=mu, ss=ss: e.tensor_scalar_mul(mu, ss, 1.0 / D))
                prog.op("ve", lambda e, msq=msq, mu=mu: e.scalar_tensor_tensor(
                    msq, mu, 1.0, mu, ALU.bypass, ALU.mult))
                prog.op("ve", lambda e, var_=var_, sq=sq, msq=msq: e.tensor_scalar(
                    var_, sq, 1.0 / D, msq, ALU.mult, ALU.subtract),
                    inc="ve", key=(st, "xv", tc))
                prog.wait("ac", (st, "xv", tc))
                prog.wait_if("ac", SC.get(("xb_rwar", par)))
                prog.op("ac", lambda e, lnv=lnv, var_=var_: e.activation(
                    lnv, var_, AF.Ln, bias=cbias[:, 0:1]))
                prog.op("ac", lambda e, rstd=rstd, lnv=lnv: e.activation(
                    rstd, lnv, AF.Exp, scale=-0.5),
                    inc="ac", key=(st, "xr2", tc))
                prog.wait("ve", (st, "xr2", tc))
                lastn = None
                for half in range(2):
                    xs_, xn, xwar = ring_xn.acquire()
                    prog.wait_if("ve", xwar)
                    prog.op("ve", lambda e, rt=rt, xn=xn, half=half, mu=mu, rstd=rstd: e.tensor_scalar(
                        xn[:, :], rt[:, half * 1024:(half + 1) * 1024], mu, rstd,
                        ALU.subtract, ALU.mult),
                        inc="ve", key=(st, "xn", tc, half))
                    lastn = (st, "xn", tc, half)
                    prog.wait("pe", (st, "xn", tc, half))
                    for dc8 in range(8):
                        dc = half * 8 + dc8
                        ps_, pt, pwar = ring_pt.acquire()
                        prog.wait_if("pe", pwar)
                        prog.op("pe", lambda e, pt=pt, xn=xn, dc8=dc8: e.transpose(
                            pt[:, :], xn[:, dc8 * 128:(dc8 + 1) * 128], ident[:, :]),
                            inc="pe", key=(st, "xt", tc, dc))
                        prog.wait("ac", (st, "xt", tc, dc))
                        prog.op("ac", lambda e, pt=pt, dc=dc, tc=tc: e.copy(
                            xb[:, dc * T + tc * 128: dc * T + (tc + 1) * 128], pt[:, :]),
                            inc="ac", key=(st, "xe", tc, dc))
                        ring_pt.consumer(ps_, (st, "xe", tc, dc))
                    ring_xn.consumer(xs_, (st, "xt", tc, half * 8 + 7))
                SC[("xb_war", par)] = lastn
                SC[("xb_rwar", par)] = lastn
                ring_rt.consumer(rs, lastn)

        def emit_proj(st, tag, wsrc, bias_row, evac, scale_tile=None):
            ring_pp = SC["ring_pp"]
            wbufH = SC["wbufH"]
            xb = SC["xb"]
            for dh in range(2):
                for cc in range(NCC):
                    if cc == 0:
                        if dh == 0:
                            prog.wait_if("gp", SC.get("wload_war_" + tag))
                        else:
                            prog.wait("gp", (st, "pmm", tag, 0, NTC - 1))
                    prog.dma("gp", wbufH[:, cc * 1024:(cc + 1) * 1024],
                             wsrc[cc * 128:(cc + 1) * 128, dh * 1024:(dh + 1) * 1024],
                             "gw", key=(st, "w", tag, dh, cc))
                if scale_tile is not None:
                    prog.wait("ve", (st, "w", tag, dh, NCC - 1))
                    for cc in range(NCC):
                        prog.op("ve", lambda e, cc=cc, dh=dh, wbufH=wbufH, scale_tile=scale_tile: e.tensor_tensor(
                            wbufH[:, cc * 1024:(cc + 1) * 1024],
                            wbufH[:, cc * 1024:(cc + 1) * 1024],
                            scale_tile[:, dh * 1024:(dh + 1) * 1024], ALU.mult),
                            inc="ve", key=(st, "wsc", tag, dh, cc))
                for tc in range(NTC):
                    ps_, pp, pwar = ring_pp.acquire()
                    prog.wait_if("pe", pwar)
                    if tag == "q" and dh == 0:
                        prog.wait("pe", (st, "xe", tc, NCC - 1))
                    if tc == 0:
                        if scale_tile is not None:
                            prog.wait("pe", (st, "wsc", tag, dh, NCC - 1))
                        else:
                            prog.wait("pe", (st, "w", tag, dh, NCC - 1))
                    for cc in range(NCC):
                        last = (cc == NCC - 1) and bias_row is None
                        for nq in range(2):
                            prog.op("pe", lambda e, pp=pp, cc=cc, nq=nq, tc=tc, wbufH=wbufH, last=last: e.matmul(
                                pp[:, nq * 512:(nq + 1) * 512],
                                xb[:, cc * T + tc * 128: cc * T + (tc + 1) * 128],
                                wbufH[:, cc * 1024 + nq * 512: cc * 1024 + (nq + 1) * 512],
                                start=(cc == 0), stop=last),
                                inc=None if not (last and nq == 1) else "pe",
                                key=None if not (last and nq == 1) else (st, "pmm", tag, dh, tc))
                    if bias_row is not None:
                        for nq in range(2):
                            prog.op("pe", lambda e, pp=pp, nq=nq, dh=dh, bias_row=bias_row: e.matmul(
                                pp[:, nq * 512:(nq + 1) * 512],
                                ones_row[0:1, :],
                                bias_row[0:1, dh * 1024 + nq * 512: dh * 1024 + (nq + 1) * 512],
                                start=False, stop=True),
                                inc=None if nq == 0 else "pe",
                                key=None if nq == 0 else (st, "pmm", tag, dh, tc))
                    if SC.get("_pend_evac") is not None:
                        SC["_pend_evac"]()
                    lastkey, dt_ = evac(st, tag, dh, tc, pp)
                    ring_pp.consumer(ps_, lastkey)
                    SC["_pend_evac"] = dt_
            if SC.get("_pend_evac") is not None:
                SC["_pend_evac"]()
                SC["_pend_evac"] = None
            SC["_pend_evac"] = None

        def mk_evac_qk(spill, fm, spill_tok=None, spill_q4=False):
            def evac(st, tag, dh, tc, pp):
                ring_xn, ring_pt, ring_st = SC["ring_xn"], SC["ring_pt"], SC["ring_st"]
                xs_, xn, xwar = ring_xn.acquire()
                prog.wait_if("ve", xwar)
                last_key = fm(st, tag, dh, tc, pp, xn)
                tok_key = None
                if spill_tok is not None:
                    prog.wait("gp", (st, "fmv", tag, dh, tc, 0))
                    prog.dma("gp", spill_tok[tc, :, dh * 1024:(dh + 1) * 1024],
                             xn[:, :], f"dd{(tc * 2 + dh) % 4}",
                             key=(st, "ksp", tc, dh))
                    tok_key = (st, "ksp", tc, dh)

                def do_transposes(xn=xn, xs_=xs_, tag=tag, dh=dh, tc=tc,
                                  tok_key=tok_key):
                    ts_, stg, swar = ring_st.acquire()
                    for h8 in range(8):
                        ps_, pt, pwar = ring_pt.acquire()
                        prog.wait_if("pe", pwar)
                        prog.wait("pe", (st, "fmv", tag, dh, tc, h8))
                        prog.op("pe", lambda e, pt=pt, xn=xn, h8=h8: e.transpose(
                            pt[:, :], xn[:, h8 * 128:(h8 + 1) * 128], ident[:, :]),
                            inc="pe", key=(st, "qt", tag, dh, tc, h8))
                        prog.wait("ve", (st, "qt", tag, dh, tc, h8))
                        if h8 == 0:
                            prog.wait_if("ve", swar)
                        prog.op("ve", lambda e, pt=pt, stg=stg, h8=h8: e.tensor_copy(
                            stg[:, h8 * 128:(h8 + 1) * 128], pt[:, :]),
                            inc="ve", key=(st, "qe", tag, dh, tc, h8))
                        ring_pt.consumer(ps_, (st, "qe", tag, dh, tc, h8))
                    xn_wars = [(st, "qt", tag, dh, tc, 7)]
                    if tok_key is not None:
                        xn_wars.append(tok_key)
                    ring_xn.consumer(xs_, xn_wars)
                    prog.wait("sy", (st, "qe", tag, dh, tc, 7))
                    if spill_q4:
                        dstq = q4_s[tc // 4].rearrange(
                            "p (h t) -> p h t", t=512)[
                            :, dh * 8:(dh + 1) * 8,
                            (tc % 4) * 128:(tc % 4 + 1) * 128]
                        srcq = stg[:, :].rearrange("p (h t) -> p h t", t=128)
                        prog.dma("sy", dstq, srcq,
                                 f"de{ts_}", key=(st, "qsp", tag, tc, dh))
                    else:
                        prog.dma("sy", spill[tc, :, dh * 1024:(dh + 1) * 1024],
                                 stg[:, :], f"de{ts_}", key=(st, "qsp", tag, tc, dh))
                    ring_st.consumer(ts_, (st, "qsp", tag, tc, dh))
                return last_key, do_transposes
            return evac

        def fm_elu(st, tag, dh, tc, pp, xn):
            par = (dh * NTC + tc) % 2
            tf = SC["scr_c"][:, par * 1024:(par + 1) * 1024]
            ebf = SC["ybf"][:, par * 1024:(par + 1) * 1024]
            prog.wait("ac", (st, "pmm", tag, dh, tc))
            prog.wait_if("ac", SC.get(("elu_war", par)))
            prog.op("ac", lambda e, pp=pp, tf=tf: e.copy(tf, pp[:, :]),
                    inc="ac", key=(st, "fmd", tag, dh, tc))
            prog.wait("ve", (st, "fmd", tag, dh, tc))
            prog.op("ve", lambda e, xn=xn, tf=tf: e.tensor_scalar_min(xn[:, :], tf, 0.0),
                    inc="ve", key=(st, "fmm", tag, dh, tc))
            prog.wait("ac", (st, "fmm", tag, dh, tc))
            prog.op("ac", lambda e, xn=xn, ebf=ebf: e.activation(ebf, xn[:, :], AF.Exp),
                    inc="ac", key=(st, "fme", tag, dh, tc))
            prog.wait("ve", (st, "fme", tag, dh, tc))
            prog.op("ve", lambda e, pp=pp, xn=xn, tf=tf, ebf=ebf: e.scalar_tensor_tensor(
                xn[:, :], tf, 0.0, ebf, ALU.max, ALU.add),
                inc="ve", key=(st, "fmn", tag, dh, tc))
            SC[("elu_war", par)] = (st, "fmn", tag, dh, tc)
            for hh in range(8):
                prog.alias((st, "fmv", tag, dh, tc, hh), (st, "fmn", tag, dh, tc))
            return (st, "fmd", tag, dh, tc)

        def fm_lnb(st, tag, dh, tc, pp, xn):
            par = (dh * NTC + tc) % 2
            ybf = SC["ybf"][:, par * 1024:(par + 1) * 1024]
            y2bf = SC["y2bf"][:, par * 1024:(par + 1) * 1024]
            base = 8 + par * 40
            seg = col(cs, base, 8); seg2 = col(cs, base + 8, 8)
            muh = col(cs, base + 16, 8); rstdh = col(cs, base + 24, 8)
            varh = col(cs, base + 32, 8)
            prog.wait("ac", (st, "pmm", tag, dh, tc))
            prog.wait_if("ac", SC.get(("lnb_y_war", par)))
            prog.op("ac", lambda e, pp=pp, ybf=ybf: e.copy(ybf, pp[:, :]),
                    inc="ac", key=(st, "fmd", tag, dh, tc))
            prog.wait_if("ac", SC.get(("lnb_y2_war", par)))
            prog.op("ac", lambda e, ybf=ybf, y2bf=y2bf: e.activation(y2bf, ybf, AF.Square),
                    inc="ac", key=(st, "fsq", tag, dh, tc))
            y3 = ybf.rearrange("p (s f) -> p s f", f=128)
            y23 = y2bf.rearrange("p (s f) -> p s f", f=128)
            prog.wait("ve", (st, "fmd", tag, dh, tc))
            prog.op("ve", lambda e, y3=y3, seg=seg: e.reduce_sum(seg, y3, axis=mybir.AxisListType.X))
            prog.wait("ve", (st, "fsq", tag, dh, tc))
            prog.op("ve", lambda e, y23=y23, seg2=seg2: e.reduce_sum(seg2, y23, axis=mybir.AxisListType.X),
                    inc="ve", key=(st, "fr2", tag, dh, tc))
            SC[("lnb_y2_war", par)] = (st, "fr2", tag, dh, tc)
            prog.op("ve", lambda e, muh=muh, seg=seg: e.tensor_scalar_mul(muh, seg, 1.0 / HD))
            prog.op("ve", lambda e, seg=seg, muh=muh: e.scalar_tensor_tensor(
                seg, muh, 1.0, muh, ALU.bypass, ALU.mult))
            prog.op("ve", lambda e, varh=varh, seg2=seg2, seg=seg: e.scalar_tensor_tensor(
                varh, seg2, 1.0 / HD, seg, ALU.mult, ALU.subtract),
                inc="ve", key=(st, "fv1", tag, dh, tc))
            prog.wait("ac", (st, "fv1", tag, dh, tc))
            prog.wait_if("ac", SC.get(("lnb_r_war", par)))
            prog.op("ac", lambda e, varh=varh: e.activation(varh, varh, AF.Ln, bias=cbias[:, 0:1]))
            if tag == "q":
                prog.op("ac", lambda e, rstdh=rstdh, varh=varh: e.activation(
                    rstdh, varh, AF.Exp, scale=-0.5, bias=cbias[:, 1:2]),
                    inc="ac", key=(st, "fmr", tag, dh, tc))
            else:
                prog.op("ac", lambda e, rstdh=rstdh, varh=varh: e.activation(
                    rstdh, varh, AF.Exp, scale=-0.5),
                    inc="ac", key=(st, "fmr", tag, dh, tc))
            prog.wait("ve", (st, "fmr", tag, dh, tc))
            keys = []
            for h8 in range(8):
                prog.op("ve", lambda e, h8=h8, xn=xn, ybf=ybf, muh=muh, rstdh=rstdh: e.tensor_scalar(
                    xn[:, h8 * 128:(h8 + 1) * 128],
                    ybf[:, h8 * 128:(h8 + 1) * 128],
                    muh[:, h8:h8 + 1], rstdh[:, h8:h8 + 1],
                    ALU.subtract, ALU.mult),
                    inc="ve", key=(st, "fnm", tag, dh, tc, h8))
                keys.append((st, "fnm", tag, dh, tc, h8))
            SC[("lnb_r_war", par)] = keys[-1]
            SC[("lnb_y_war", par)] = keys[-1]
            for h8 in range(8):
                prog.alias((st, "fmv", tag, dh, tc, h8), keys[h8])
            return (st, "fmd", tag, dh, tc)

        def mk_evac_v(st_v):
            vaug = SC["vaug"]

            def evac(st, tag, dh, tc, pp):
                prog.wait("ac", (st, "pmm", tag, dh, tc))
                if dh == 0 and tc == 0:
                    prog.wait("ac", (st, "vones"))
                off = tc * 2064 + dh * 8 * 129
                dst = bass.AP(vaug, off, [[NTC * H * 129, 128], [129, 8], [1, 128]])
                prog.op("ac", lambda e, pp=pp, dst=dst: e.copy(dst, pp[:, :]),
                        inc="ac", key=(st, "vev", dh, tc))
                return (st, "vev", dh, tc), None
            return evac

        def emit_attn_phaseA(st, kind):
            is_a = kind == "attn_a"
            if is_a:
                brow = SC["brow"]
                prog.dma("gp", brow[0:1, 0:D], bq_a[0:1, :], "gwb")
                prog.dma("gp", brow[0:1, D:2 * D], bk_a[0:1, :], "gwb", key=(st, "bqk"))
                prog.wait("pe", (st, "bqk"))
                emit_proj(st, "q", wq_a, brow[:, 0:D], mk_evac_qk(qs_s, fm_elu))
                SC["wload_war_k"] = (st, "pmm", "q", 1, NTC - 1)
                emit_proj(st, "k", wk_a, brow[:, D:2 * D],
                          mk_evac_qk(ks_s, fm_elu, spill_tok=kts_s))
            else:
                brow = SC["brow"]
                prog.dma("gp", SC["gamt"][:, :], gam_t[:, :], "gwb")
                prog.dma("gp", brow[0:1, 0:D], bet_row_i[0:1, :], "gwb", key=(st, "gbt"))
                prog.wait("ve", (st, "gbt"))
                prog.wait("pe", (st, "gbt"))
                emit_proj(st, "q", wq_b, brow[:, 0:D],
                          mk_evac_qk(None, fm_lnb, spill_q4=True),
                          scale_tile=SC["gamt"])
                SC["wload_war_k"] = (st, "pmm", "q", 1, NTC - 1)
                emit_proj(st, "k", wk_b, brow[:, 0:D], mk_evac_qk(ks_s, fm_lnb),
                          scale_tile=SC["gamt"])

        def emit_attn_v(st, kind):
            is_a = kind == "attn_a"
            vaug = SC["vaug"]
            va_ones = vaug[:, :].rearrange("p (t h c) -> p t h c", h=H, c=129)[:, :, :, 128:129]
            prog.op("ve", lambda e: e.memset(va_ones, 1.0), inc="ve", key=(st, "vones"))
            SC["wload_war_v"] = (st, "pmm", "k", 1, NTC - 1)
            SC["wload_war_" + "v"] = (st, "pmm", "k", 1, NTC - 1)
            if is_a:
                brow = SC["brow"]
                prog.wait("gp", (st, "pmm", "q", 1, NTC - 1))
                prog.dma("gp", brow[0:1, 0:D], bv_a[0:1, :], "gwb", key=(st, "bv"))
                prog.wait("pe", (st, "bv"))
                emit_proj(st, "v", wv_a, brow[:, 0:D], mk_evac_v(st))
            else:
                emit_proj(st, "v", wv_b, None, mk_evac_v(st))

        def emit_attn_a_phaseB(st, res_src, res_dst):
            psa2, psTb = SC["psa2"], SC["psTb"]
            pso2, kvp2 = SC["pso2"], SC["kvp2"]
            ppj = SC["ppj"]
            wbuf, vaug = SC["wbuf"], SC["vaug"]
            state, ostgs = SC["state"], SC["ostgs"]
            state_bf2, atbf, obf = SC["state_bf2"], SC["atbf"], SC["obf"]
            ring_rtB = SC["ring_rtB"]
            qblkb, kblkb, ktblk = SC["qblkb"], SC["kblkb"], SC["ktblk"]
            brow = SC["brow"]
            vdone = (st, "pmm", "v", 1, NTC - 1)
            GSZ = 3
            NG = (H + GSZ - 1) // GSZ
            CA_H = 8
            CA_C = (CA_H + 1) * 129
            prog.wait("gp", (st, "pmm", "k", 1, NTC - 1))
            prog.dma("gp", brow[0:1, D:2 * D], bo_a[0:1, :], "gwb", key=(st, "bo"))
            for gk in [vdone, (st, "qsp", "k", NTC - 2, 1), (st, "qsp", "k", NTC - 1, 1)]:
                prog.wait("gp", gk)
            for cc in range(NCC):
                prog.dma("gp", wbuf[:, cc * D:(cc + 1) * D],
                         wo_a[cc * 128:(cc + 1) * 128, :], "gw", key=(st, "wo", cc))
            guard_keys = [vdone, (st, "qe", "k", 1, NTC - 1, 7),
                          (st, "fmd", "k", 1, NTC - 1), (st, "vev", 1, NTC - 1),
                          (st, "qsp", "k", NTC - 2, 1), (st, "qsp", "k", NTC - 1, 1),
                          (st, "ksp", NTC - 2, 0), (st, "ksp", NTC - 2, 1),
                          (st, "ksp", NTC - 1, 0), (st, "ksp", NTC - 1, 1)]
            for geng in ("pe", "ve", "ac", "sy"):
                for gk in guard_keys:
                    prog.wait(geng, gk)
            prog.wait("pe", (st, "bo"))

            def emit_add(n, g):
                lo = g * GSZ * 129
                w_ = (min(GSZ, H - g * GSZ)) * 129
                prog.wait("ve", (st, "kv", n, min(g * GSZ + GSZ, H) - 1))
                if n >= 1:
                    prog.wait("ve", (st, "castA" if g <= 2 else "castB", n - 1))
                if n == 0:
                    prog.op("ve", lambda e, lo=lo, w_=w_, g=g: e.tensor_copy(
                        state[:, lo:lo + w_], kvp2[g % 2][:, 0:w_]),
                        inc="ve", key=(st, "sup", n, g))
                else:
                    prog.op("ve", lambda e, lo=lo, w_=w_, g=g: e.tensor_tensor(
                        state[:, lo:lo + w_], state[:, lo:lo + w_],
                        kvp2[g % 2][:, 0:w_], ALU.add),
                        inc="ve", key=(st, "sup", n, g))
                SC[("kvp_war", g % 2)] = (st, "sup", n, g)

            def emit_castA(n):
                prog.wait("ac", (st, "sup", n, 2))
                if n >= 1:
                    prog.wait("ac", (st, "ob", n - 1, CA_H))
                prog.op("ac", lambda e, n=n: e.copy(
                    state_bf2[n % 2][:, 0:CA_C], state[:, 0:CA_C]),
                    inc="ac", key=(st, "castA", n))

            def emit_castB(n):
                prog.wait("ac", (st, "sup", n, NG - 1))
                if n >= 1:
                    prog.wait("ac", (st, "ob", n - 1, H - 1))
                prog.op("ac", lambda e, n=n: e.copy(
                    state_bf2[n % 2][:, CA_C:H * 129], state[:, CA_C:H * 129]),
                    inc="ac", key=(st, "castB", n))

            def mk_pso(n, h, qb):
                hp = h % 2

                def pso():
                    voff = n * 2064 + h * 129
                    prog.wait_if("pe", SC.get(("a_pso", hp)))
                    prog.wait("pe", (st, "atm", n, h))
                    if n > 0:
                        prog.wait("pe", (st, "castA" if h <= CA_H else "castB", n - 1))
                        prog.op("pe", lambda e, qb=qb, h=h, hp=hp, n=n: e.matmul(
                            pso2[hp][:, 0:129], qb[:, h * 128:(h + 1) * 128],
                            state_bf2[(n - 1) % 2][:, h * 129:(h + 1) * 129],
                            start=True, stop=False))
                    prog.op("pe", lambda e, voff=voff, n=n, hp=hp: e.matmul(
                        pso2[hp][:, 0:129], atbf[:, hp * 128:(hp + 1) * 128],
                        vaug[:, voff:voff + 129],
                        start=(n == 0), stop=True), inc="pe", key=(st, "ob", n, h))
                    SC[("a_atb", hp)] = (st, "ob", n, h)
                    prog.wait("ve", (st, "ob", n, h))
                    prog.op("ve", lambda e, hp=hp: e.tensor_scalar_add(
                        zcol[:, hp * 2:hp * 2 + 1], pso2[hp][:, 128:129], EPS_A))
                    prog.op("ve", lambda e, hp=hp: e.reciprocal(
                        zcol[:, hp * 2 + 1:hp * 2 + 2], zcol[:, hp * 2:hp * 2 + 1]))
                    prog.wait_if("ve", SC.get(("a_obf", hp)))
                    prog.op("ve", lambda e, hp=hp: e.tensor_scalar_mul(
                        obf[:, hp * 128:(hp + 1) * 128], pso2[hp][:, 0:128],
                        zcol[:, hp * 2 + 1:hp * 2 + 2]),
                        inc="ve", key=(st, "odv", n, h))
                    SC[("a_pso", hp)] = (st, "odv", n, h)
                return pso

            def mk_tail(n, h, ostg):
                hp = h % 2

                def tail():
                    prog.wait("pe", (st, "odv", n, h))
                    prog.wait_if("pe", SC.get("a_psTb"))
                    prog.op("pe", lambda e, hp=hp: e.transpose(
                        psTb[:, :], obf[:, hp * 128:(hp + 1) * 128], ident[:, :]),
                        inc="pe", key=(st, "ot", n, h))
                    SC[("a_obf", hp)] = (st, "ot", n, h)
                    prog.wait("ac", (st, "ot", n, h))
                    if h == 0:
                        if n >= 2:
                            prog.wait("ac", (st, "pj", n - 2, 3))
                        else:
                            prog.wait("ac", vdone)
                    prog.op("ac", lambda e, ostg=ostg, h=h: e.copy(
                        ostg[:, h * 128:(h + 1) * 128], psTb[:, :]),
                        inc="ac", key=(st, "oe", n, h))
                    SC["a_psTb"] = (st, "oe", n, h)
                    if h == 10 and n < NTC - 1:
                        emit_castA(n)
                return tail

            pend_p = [None]
            pend_t = [None, None]
            rt_slots = {}

            def emit_loads(m):
                for (bl, src_, nm, spk, pool) in ((qblkb[m % 2], qs_s, "qb", "q", "dc"),
                                                  (kblkb[m % 2], ks_s, "kb", "k", "dg"),):
                    if m >= 2:
                        prog.wait("sy", (st, "blku", nm, m - 2))
                    else:
                        prog.wait("sy", vdone)
                    prog.wait("sy", (st, "qsp", spk, m, 1))
                    prog.dma("sy", bl[:, :], src_[m, :, :], f"{pool}{m % 2}",
                             key=(st, nm, m))
                if m < NTC - 1:
                    if m >= 2:
                        prog.wait("gp", (st, "kv", m - 2, H - 1))
                    else:
                        prog.wait("gp", vdone)
                    prog.wait("gp", (st, "ksp", m, 0))
                    prog.wait("gp", (st, "ksp", m, 1))
                    prog.dma("gp", ktblk[m % 2][:, :], kts_s[m, :, :],
                             f"dg{2 + m % 2}", key=(st, "ktb", m))
                rs, rt, rwar = ring_rtB.acquire()
                if rwar is not None:
                    prog.wait_if("gp", rwar)
                else:
                    prog.wait("gp", vdone)
                prog.dma("gp", rt[:, :], res_src[m * 128:(m + 1) * 128, :], f"db{rs}",
                         key=(st, "rr", m))
                rt_slots[m] = (rs, rt)

            emit_loads(0)
            for n in range(NTC):
                last_n = n == NTC - 1
                if not last_n:
                    emit_loads(n + 1)
                qb, kb = qblkb[n % 2], kblkb[n % 2]
                ostg = ostgs[n % 2]
                rs, rt = rt_slots.pop(n)
                prog.wait("pe", (st, "kb", n))
                prog.wait("pe", (st, "qb", n))
                if not last_n:
                    prog.wait("pe", (st, "ktb", n))
                prog.wait("pe", (st, "vev", 0, n))
                prog.wait("pe", (st, "vev", 1, n))
                for h in range(H):
                    hp = h % 2
                    g = h // GSZ
                    prog.wait_if("pe", SC.get(("a_psa", hp)))
                    prog.op("pe", lambda e, kb=kb, qb=qb, h=h, hp=hp: e.matmul(
                        psa2[hp][:, :], kb[:, h * 128:(h + 1) * 128],
                        qb[:, h * 128:(h + 1) * 128], start=True, stop=True),
                        inc="pe", key=(st, "at", n, h))
                    if not last_n:
                        if h % GSZ == 0:
                            prog.wait_if("pe", SC.get(("kvp_war", g % 2)))
                        prog.op("pe", lambda e, h=h, g=g, n=n: e.matmul(
                            kvp2[g % 2][:, (h - g * GSZ) * 129:(h - g * GSZ + 1) * 129],
                            ktblk[n % 2][:, h * 128:(h + 1) * 128],
                            vaug[:, n * 2064 + h * 129: n * 2064 + (h + 1) * 129],
                            start=True, stop=True), inc="pe", key=(st, "kv", n, h))
                    prog.wait("ve", (st, "at", n, h))
                    prog.wait_if("ve", SC.get(("a_atb", hp)))
                    prog.op("ve", lambda e, hp=hp: e.tensor_tensor(
                        atbf[:, hp * 128:(hp + 1) * 128], psa2[hp][:, :],
                        triu[:, :], ALU.mult),
                        inc="ve", key=(st, "atm", n, h))
                    SC[("a_psa", hp)] = (st, "atm", n, h)
                    if not last_n and (h % GSZ == GSZ - 1 or h == H - 1):
                        emit_add(n, g)
                    if pend_p[0] is not None:
                        pend_p[0]()
                    pend_p[0] = mk_pso(n, h, qb)
                    if pend_t[0] is not None:
                        pend_t[0]()
                    pend_t[0] = pend_t[1]
                    pend_t[1] = mk_tail(n, h, ostg)
                pend_p[0]()
                pend_p[0] = None
                pend_t[0]()
                pend_t[1]()
                pend_t[0] = pend_t[1] = None
                if not last_n:
                    emit_castB(n)
                prog.alias((st, "blku", "qb", n), (st, "ob", n, H - 1))
                prog.alias((st, "blku", "kb", n), (st, "at", n, H - 1))
                prog.wait("pe", (st, "oe", n, H - 1))
                if n == 0:
                    prog.wait("pe", (st, "wo", NCC - 1))
                for pc in range(4):
                    if n >= 1 or pc >= 1:
                        pn_, pcc = (n, pc - 1) if pc >= 1 else (n - 1, 3)
                        prog.wait("pe", (st, "rad", pn_, pcc))
                    for hc in range(NCC):
                        prog.op("pe", lambda e, hc=hc, pc=pc, ostg=ostg: e.matmul(
                            ppj[:, :],
                            ostg[:, hc * 128:(hc + 1) * 128],
                            wbuf[:, hc * D + pc * 512: hc * D + (pc + 1) * 512],
                            start=(hc == 0), stop=False))
                    prog.op("pe", lambda e, pc=pc: e.matmul(
                        ppj[:, :], ones_row[0:1, :],
                        brow[0:1, D + pc * 512: D + (pc + 1) * 512],
                        start=False, stop=True),
                        inc="pe", key=(st, "pj", n, pc))
                    prog.wait("ve", (st, "pj", n, pc))
                    if pc == 0:
                        prog.wait("ve", (st, "rr", n))
                    prog.op("ve", lambda e, rt=rt, pc=pc: e.tensor_tensor(
                        rt[:, pc * 512:(pc + 1) * 512], rt[:, pc * 512:(pc + 1) * 512],
                        ppj[:, :], ALU.add),
                        inc="ve", key=(st, "rad", n, pc))
                prog.wait("sy", (st, "rad", n, 3))
                prog.dma("sy", res_dst[n * 128:(n + 1) * 128, :], rt[:, :], f"df{rs}",
                         key=(st, "rst", n))
                ring_rtB.consumer(rs, (st, "rst", n))

        def emit_w2prep(st):
            pps2 = SC["pps2"]
            wts = SC["p_wt"]
            stg2 = SC["p_stg"]
            for a in range(64):
                wt = wts[a % 4]
                pq = pps2[(a % 2) * 4:(a % 2) * 4 + 4]
                if a >= 4:
                    prog.wait("gp", (st, "s1m", a - 4))
                prog.dma("gp", wt[:, :], w2t[a * 128:(a + 1) * 128, :], f"da{a % 4}",
                         key=(st, "s1w", a))
                if a >= 2:
                    prog.wait("pe", (st, "s1e", a - 2))
                prog.wait("pe", (st, "s1w", a))
                for nq in range(4):
                    prog.op("pe", lambda e, wt=wt, nq=nq, pq=pq: e.matmul(
                        pq[nq][:, :], h128[:, :],
                        wt[:, nq * 512:(nq + 1) * 512], start=True, stop=True),
                        inc=None if nq < 3 else "pe",
                        key=None if nq < 3 else (st, "s1m", a))
                ee = "ac" if a % 2 == 0 else "ve"
                prog.wait(ee, (st, "s1m", a))
                bt = stg2[a % 4]
                if a >= 4:
                    prog.wait(ee, (st, "s1s", a - 4))
                for nq in range(4):
                    if ee == "ac":
                        prog.op("ac", lambda e, bt=bt, nq=nq, pq=pq: e.mul(
                            bt[:, nq * 512:(nq + 1) * 512], pq[nq][:, :], FWHT_SCALE),
                            inc=None if nq < 3 else "ac",
                            key=None if nq < 3 else (st, "s1e", a))
                    else:
                        prog.op("ve", lambda e, bt=bt, nq=nq, pq=pq: e.tensor_scalar_mul(
                            bt[:, nq * 512:(nq + 1) * 512], pq[nq][:, :], FWHT_SCALE),
                            inc=None if nq < 3 else "ve",
                            key=None if nq < 3 else (st, "s1e", a))
                prog.wait("sy", (st, "s1e", a))
                prog.dma("sy", b_s[a, :, :], bt[:, :],
                         ["de0", "de1", "dc0", "dc1"][a % 4], key=(st, "s1s", a))
            for j in range(128):
                bj = wts[j % 4]
                pq = pps2[(j % 2) * 4:(j % 2) * 4 + 4]
                if j >= 4:
                    prog.wait("sy", (st, "s2m", j - 4))
                if j < 4:
                    for aa in range(60, 64):
                        prog.wait("sy", (st, "s1s", aa))
                prog.dma("sy", bj[0:64, :], b_s[:, j, :], f"dg{j % 4}", key=(st, "s2l", j))
                if j >= 2:
                    prog.wait("pe", (st, "s2e", j - 2))
                prog.wait("pe", (st, "s2l", j))
                for nq in range(4):
                    prog.op("pe", lambda e, bj=bj, nq=nq, pq=pq: e.matmul(
                        pq[nq][0:64, :], h64[:, :],
                        bj[0:64, nq * 512:(nq + 1) * 512], start=True, stop=True),
                        inc=None if nq < 3 else "pe",
                        key=None if nq < 3 else (st, "s2m", j))
                ee = "ac" if j % 2 == 0 else "ve"
                prog.wait(ee, (st, "s2m", j))
                ct = stg2[j % 4]
                if j >= 4:
                    prog.wait(ee, (st, "s2s", j - 4))
                for nq in range(4):
                    if ee == "ac":
                        prog.op("ac", lambda e, ct=ct, nq=nq, pq=pq: e.copy(
                            ct[0:64, nq * 512:(nq + 1) * 512], pq[nq][0:64, :]),
                            inc=None if nq < 3 else "ac",
                            key=None if nq < 3 else (st, "s2e", j))
                    else:
                        prog.op("ve", lambda e, ct=ct, nq=nq, pq=pq: e.tensor_copy(
                            ct[0:64, nq * 512:(nq + 1) * 512], pq[nq][0:64, :]),
                            inc=None if nq < 3 else "ve",
                            key=None if nq < 3 else (st, "s2e", j))
                prog.wait("gp", (st, "s2e", j))
                prog.dma("gp", w2p_s[:, j, :], ct[0:64, :],
                         ["ga0", "ga1", "ga2", "ga3"][j % 4], key=(st, "s2s", j))

        def emit_ffn(st, res_src, res_dst, first_pass):
            psG, psL = SC["psG"], SC["psL"]
            ppcs = SC["ppcs"]
            acc, sg = SC["acc"], SC["sg"]
            w12t, w34t = SC["w12t"], SC["w34t"]
            hgt, w2gt = SC["hgt"], SC["w2gt"]
            ring_rtF = SC["ring_rtF"]
            xb = SC["xb"]
            xb_done = (st, "xe", NTC - 1, NCC - 1)
            xn_done = xb_done
            for half in range(2):
                for hc in range(NHC):
                    g, gi = hc // G, hc % G
                    w1tile, w3tile = w12t[hc % 2], w34t[hc % 2]
                    war = None
                    if hc >= 2:
                        war = (st, "lmm", half, hc - 2)
                    elif half == 1:
                        war = (st, "lmm", 0, NHC - 2 + hc)
                    else:
                        war = xn_done
                    prog.wait("gp", war)
                    prog.dma("gp", w1tile[:, :], w1t[hc, :, :], f"da{hc % 2}",
                             key=(st, "w1", half, hc))
                    prog.dma("gp", w3tile[:, :], w3t[hc, :, :], f"ga{hc % 2}",
                             key=(st, "w3", half, hc))
                    w2war = None
                    if g >= 1:
                        w2war = (st, "cdone", half, g - 1)
                    elif half == 1:
                        w2war = (st, "cdone", 0, NGR - 1)
                    else:
                        w2war = xn_done
                    prog.wait("sy", w2war)
                    prog.dma("sy", w2gt[gi][:, :], w2p_s[hc, :, :], "sw2",
                             key=(st, "w2", half, hc))
                    prog.wait("pe", (st, "w1", half, hc))
                    if hc == 0 and half == 0:
                        prog.wait("pe", xb_done)
                    if hc >= 1:
                        prog.wait("pe", (st, "sg", half, hc - 1))
                    elif half == 1:
                        prog.wait("pe", (st, "sg", 0, NHC - 1))
                    for cc in range(NCC):
                        for nq in range(2):
                            prog.op("pe", lambda e, cc=cc, nq=nq, half=half, w1tile=w1tile: e.matmul(
                                psG[:, nq * 512:(nq + 1) * 512],
                                w1tile[:, cc * 128:(cc + 1) * 128],
                                xb[:, cc * T + half * 1024 + nq * 512:
                                   cc * T + half * 1024 + (nq + 1) * 512],
                                start=(cc == 0), stop=(cc == NCC - 1)),
                                inc=None if (cc, nq) != (NCC - 1, 1) else "pe",
                                key=None if (cc, nq) != (NCC - 1, 1) else (st, "gmm", half, hc))
                    prog.wait("ac", (st, "gmm", half, hc))
                    hp = None
                    if hc >= 1:
                        hp = (half, hc - 1)
                    elif half == 1:
                        hp = (0, NHC - 1)
                    if hp is not None:
                        prog.wait("ac", (st, "hpr") + hp)
                    prog.op("ac", lambda e: e.activation(sg[:, :], psG[:, :], AF.Silu),
                            inc="ac", key=(st, "sg", half, hc))
                    prog.wait("pe", (st, "w3", half, hc))
                    if hp is not None:
                        prog.wait("pe", (st, "hpr") + hp)
                    for cc in range(NCC):
                        for nq in range(2):
                            prog.op("pe", lambda e, cc=cc, nq=nq, half=half, w3tile=w3tile: e.matmul(
                                psL[:, nq * 512:(nq + 1) * 512],
                                w3tile[:, cc * 128:(cc + 1) * 128],
                                xb[:, cc * T + half * 1024 + nq * 512:
                                   cc * T + half * 1024 + (nq + 1) * 512],
                                start=(cc == 0), stop=(cc == NCC - 1)),
                                inc=None if (cc, nq) != (NCC - 1, 1) else "pe",
                                key=None if (cc, nq) != (NCC - 1, 1) else (st, "lmm", half, hc))
                    prog.wait("ve", (st, "lmm", half, hc))
                    prog.wait("ve", (st, "sg", half, hc))
                    hwar = None
                    if g >= 1:
                        hwar = (st, "cdone", half, g - 1)
                    elif half == 1:
                        hwar = (st, "cdone", 0, NGR - 1)
                    else:
                        hwar = xn_done
                    if gi == 0 or (g == 0 and half == 0):
                        prog.wait("ve", hwar)
                    prog.op("ve", lambda e, htile=hgt[gi]: e.scalar_tensor_tensor(
                        htile[:, :], psL[:, :], 1.0, sg[:, :], ALU.bypass, ALU.mult),
                        inc="ve", key=(st, "hpr", half, hc))
                    if gi == G - 1:
                        prog.wait("pe", (st, "hpr", half, hc))
                        prog.wait("pe", (st, "w2", half, hc))
                        for tcl in range(8):
                            for dh in range(2):
                                ppc = ppcs[dh]
                                cwar = None
                                if tcl >= 1:
                                    cwar = (st, "cad", half, g, tcl - 1, dh)
                                elif g >= 1:
                                    cwar = (st, "cad", half, g - 1, 7, dh)
                                elif half == 1:
                                    cwar = (st, "cad", 0, NGR - 1, 7, dh)
                                if cwar is not None:
                                    prog.wait("pe", cwar)
                                for i in range(G):
                                    for nq in range(2):
                                        prog.op("pe", lambda e, ppc=ppc, i=i, nq=nq, tcl=tcl, dh=dh: e.matmul(
                                            ppc[:, nq * 512:(nq + 1) * 512],
                                            hgt[i][:, tcl * 128:(tcl + 1) * 128],
                                            w2gt[i][:, dh * 1024 + nq * 512:
                                                    dh * 1024 + (nq + 1) * 512],
                                            start=(i == 0), stop=(i == G - 1)),
                                            inc=None if (i, nq) != (G - 1, 1) else "pe",
                                            key=None if (i, nq) != (G - 1, 1) else (st, "cmm", half, g, tcl, dh))
                                prog.wait("ve", (st, "cmm", half, g, tcl, dh))
                                dsto = tcl * D + dh * 1024
                                tc_g = half * 8 + tcl
                                if g == 0:
                                    rs, rt, rwar = ring_rtF.acquire()
                                    prog.wait_if("sy", rwar)
                                    if half == 0 and g == 0 and tcl < 2:
                                        prog.wait("sy", xn_done)
                                    prog.dma("sy", rt[:, :],
                                             res_src[tc_g * 128:(tc_g + 1) * 128,
                                                     dh * 1024:(dh + 1) * 1024],
                                             f"db{rs}", key=(st, "frr", half, tcl, dh))
                                    prog.wait("ve", (st, "frr", half, tcl, dh))
                                    if half == 1:
                                        prog.wait("ve", (st, "fst", 0, tcl, dh))
                                    prog.op("ve", lambda e, ppc=ppc, dsto=dsto, rt=rt: e.tensor_tensor(
                                        acc[:, dsto:dsto + 1024], rt[:, :], ppc[:, :], ALU.add),
                                        inc="ve", key=(st, "cad", half, g, tcl, dh))
                                    ring_rtF.consumer(rs, (st, "cad", half, g, tcl, dh))
                                else:
                                    prog.op("ve", lambda e, ppc=ppc, dsto=dsto: e.tensor_tensor(
                                        acc[:, dsto:dsto + 1024], acc[:, dsto:dsto + 1024],
                                        ppc[:, :], ALU.add),
                                        inc="ve", key=(st, "cad", half, g, tcl, dh))
                                if g == NGR - 1:
                                    prog.wait("sy", (st, "cad", half, g, tcl, dh))
                                    slot = (tcl * 2 + dh) % 4
                                    prog.wait_if("sy", SC.get(("fst_prev", slot)))
                                    prog.dma("sy",
                                             res_dst[tc_g * 128:(tc_g + 1) * 128,
                                                     dh * 1024:(dh + 1) * 1024],
                                             acc[:, dsto:dsto + 1024],
                                             f"df{slot}",
                                             key=(st, "fst", half, tcl, dh))
                                    SC[("fst_prev", slot)] = (st, "fst", half, tcl, dh)
                        prog.alias((st, "cdone", half, g), (st, "cmm", half, g, 7, 1))

        def emit_attn_b_attention(st):
            psS, psO2, psT2 = SC["psS"], SC["psO2"], SC["psT2"]
            qstr, kpre, ostgb = SC["qstr"], SC["kpre"], SC["ostgb"]
            vaug, atbf, obf = SC["vaug"], SC["atbf"], SC["obf"]
            sqf = SC["sqf"]
            vdone = (st, "pmm", "v", 1, NTC - 1)
            guard_keys = [vdone, (st, "qe", "k", 1, NTC - 1, 7),
                          (st, "fmd", "k", 1, NTC - 1), (st, "vev", 1, NTC - 1),
                          (st, "qsp", "k", NTC - 2, 1), (st, "qsp", "k", NTC - 1, 1),
                          (st, "qsp", "q", NTC - 2, 1), (st, "qsp", "q", NTC - 1, 1)]
            for geng in ("pe", "ve", "ac", "sy", "gp"):
                for gk in guard_keys:
                    prog.wait(geng, gk)
            for tc in range(NTC):
                prog.wait("gp", (st, "qsp", "k", tc, 1))
                prog.dma("gp", kpre[:, tc * D:(tc + 1) * D], ks_s[tc, :, :],
                         f"dg{tc % 4}", key=(st, "kld", tc))

            pend_t = [None, None]

            for s in range(4):
                nkb = 4 * s + 4
                if s >= 1:
                    prog.wait("sy", (st, "quse", s - 1))
                for tc in range(4 * s, nkb):
                    prog.wait("sy", (st, "qsp", "q", tc, 1))
                prog.dma("sy", qstr[:, :], q4_s[s, :, :], f"dd{s % 2}",
                         key=(st, "qld", s))
                prog.wait("pe", (st, "qld", s))
                for tc in range(4 * s, nkb):
                    prog.wait("pe", (st, "kld", tc))
                    prog.wait("pe", (st, "vev", 0, tc))
                    prog.wait("pe", (st, "vev", 1, tc))
                for h in range(H):
                    for kb in range(nkb):
                        par = kb % 2
                        ig = kb - 4 * s
                        qoff = max(ig, 0) * 128
                        prog.wait_if("pe", SC.get(("b_psS", par)))
                        prog.op("pe", lambda e, kb=kb, h=h, par=par, qoff=qoff: e.matmul(
                            psS[:, par * 512 + qoff: par * 512 + 512],
                            kpre[:, kb * D + h * 128: kb * D + (h + 1) * 128],
                            qstr[:, h * 512 + qoff: h * 512 + 512],
                            start=True, stop=True),
                            inc="pe", key=(st, "smm", s, h, kb))
                        wdt = 512 - qoff
                        if par == 0:
                            prog.wait("ve", (st, "smm", s, h, kb))
                            prog.op("ve", lambda e, par=par, qoff=qoff, wdt=wdt: e.tensor_copy(
                                sqf[:, 0:wdt], psS[:, par * 512 + qoff: par * 512 + 512]),
                                inc="ve", key=(st, "sqc", s, h, kb))
                            prog.op("ve", lambda e, kb=kb, qoff=qoff, wdt=wdt: e.tensor_tensor(
                                atbf[:, kb * 512 + qoff: kb * 512 + 512],
                                sqf[:, 0:wdt], sqf[:, 0:wdt], ALU.mult),
                                inc="ve", key=(st, "sq", s, h, kb))
                            SC[("b_psS", par)] = (st, "sqc", s, h, kb)
                        else:
                            prog.wait("ac", (st, "smm", s, h, kb))
                            prog.op("ac", lambda e, kb=kb, par=par, qoff=qoff: e.activation(
                                atbf[:, kb * 512 + qoff: kb * 512 + 512],
                                psS[:, par * 512 + qoff: par * 512 + 512], AF.Square),
                                inc="ac", key=(st, "sq", s, h, kb))
                            SC[("b_psS", par)] = (st, "sq", s, h, kb)
                        if ig >= 0:
                            if par == 1:
                                prog.wait("ve", (st, "sq", s, h, kb))
                            prog.op("ve", lambda e, kb=kb, ig=ig: e.tensor_tensor(
                                atbf[:, kb * 512 + ig * 128: kb * 512 + (ig + 1) * 128],
                                atbf[:, kb * 512 + ig * 128: kb * 512 + (ig + 1) * 128],
                                triu[:, :], ALU.mult),
                                inc="ve", key=(st, "msk", s, h, kb))
                    for ql in range(4):
                        qc = 4 * s + ql
                        op_ = ql % 2
                        prog.wait_if("pe", SC.get(("b_psO", op_)))
                        prog.wait("pe", (st, "msk", s, h, qc))
                        prog.wait("pe", (st, "sq", s, h, qc))
                        if qc >= 1:
                            prog.wait("pe", (st, "sq", s, h, qc - 1))
                        for kb in range(qc + 1):
                            prog.op("pe", lambda e, ql=ql, kb=kb, h=h, op_=op_, qc=qc: e.matmul(
                                psO2[op_][:, 0:129],
                                atbf[:, kb * 512 + ql * 128: kb * 512 + (ql + 1) * 128],
                                vaug[:, kb * 2064 + h * 129: kb * 2064 + (h + 1) * 129],
                                start=(kb == 0), stop=(kb == qc)),
                                inc="pe" if kb == qc else None,
                                key=(st, "oq", s, h, ql) if kb == qc else None)
                        prog.wait("ve", (st, "oq", s, h, ql))
                        prog.op("ve", lambda e, op_=op_: e.tensor_scalar_add(
                            zcol[:, op_:op_ + 1], psO2[op_][:, 128:129], EPS_B))
                        prog.op("ve", lambda e, op_=op_: e.reciprocal(
                            zcol[:, 4 + op_:5 + op_], zcol[:, op_:op_ + 1]))
                        prog.wait_if("ve", SC.get(("b_obf", op_)))
                        prog.op("ve", lambda e, op_=op_: e.tensor_scalar_mul(
                            obf[:, op_ * 128:(op_ + 1) * 128], psO2[op_][:, 0:128],
                            zcol[:, 4 + op_:5 + op_]),
                            inc="ve", key=(st, "odv", s, h, ql))
                        SC[("b_psO", op_)] = (st, "odv", s, h, ql)

                        def mk_tail(s=s, h=h, ql=ql, op_=op_):
                            def tail():
                                prog.wait("pe", (st, "odv", s, h, ql))
                                prog.wait_if("pe", SC.get(("b_psT", op_)))
                                pt_ = psT2[op_]
                                prog.op("pe", lambda e, pt_=pt_, op_=op_: e.transpose(
                                    pt_[:, :], obf[:, op_ * 128:(op_ + 1) * 128],
                                    ident[:, :]),
                                    inc="pe", key=(st, "bot", s, h, ql))
                                SC[("b_obf", op_)] = (st, "bot", s, h, ql)
                                prog.wait("ac", (st, "bot", s, h, ql))
                                if h == 0:
                                    if s >= 1:
                                        prog.wait("ac", (st, "ost", s - 1, ql))
                                    else:
                                        prog.wait("ac", vdone)
                                prog.op("ac", lambda e, ql=ql, h=h, pt_=pt_: e.copy(
                                    ostgb[ql][:, h * 128:(h + 1) * 128], pt_[:, :]),
                                    inc="ac", key=(st, "boe", s, h, ql))
                                SC[("b_psT", op_)] = (st, "boe", s, h, ql)
                            return tail
                        if pend_t[0] is not None:
                            pend_t[0]()
                        pend_t[0] = pend_t[1]
                        pend_t[1] = mk_tail()
                pend_t[0]()
                pend_t[1]()
                pend_t[0] = pend_t[1] = None
                for ql in range(4):
                    prog.wait("sy", (st, "boe", s, H - 1, ql))
                    prog.dma("sy", os_s[4 * s + ql, :, :], ostgb[ql][:, :], f"df{ql}",
                             key=(st, "ost", s, ql))
                prog.alias((st, "quse", s), (st, "smm", s, H - 1, nkb - 1))

        def emit_attn_b_proj(st, res_src, res_dst):
            ppj = SC["ppj"]
            wbuf = SC["wbuf"]
            oblks = SC["oblks"]
            ring_rtB = SC["ring_rtB"]
            attn_done = (st, "quse", 3)
            guard_keys = [attn_done, (st, "odv", 3, H - 1, 3),
                          (st, "boe", 3, H - 1, 3), (st, "sq", 3, H - 1, 15)] + \
                         [(st, "ost", 3, ql) for ql in range(4)]
            for geng in ("pe", "ve", "ac", "sy", "gp"):
                for gk in guard_keys:
                    prog.wait(geng, gk)
            for cc in range(NCC):
                prog.dma("gp", wbuf[:, cc * D:(cc + 1) * D],
                         wo_b[cc * 128:(cc + 1) * 128, :], "gw", key=(st, "wo", cc))

            rt_slots = {}

            def emit_loads(m):
                ob = oblks[m % 2]
                if m >= 2:
                    prog.wait("sy", (st, "puse", m - 2))
                else:
                    prog.wait("sy", attn_done)
                prog.wait("sy", (st, "ost", m // 4, m % 4))
                prog.dma("sy", ob[:, :], os_s[m, :, :], f"dc{m % 2}", key=(st, "old", m))
                rs, rt, rwar = ring_rtB.acquire()
                if rwar is not None:
                    prog.wait_if("gp", rwar)
                else:
                    prog.wait("gp", attn_done)
                prog.dma("gp", rt[:, :], res_src[m * 128:(m + 1) * 128, :], f"db{rs}",
                         key=(st, "rr", m))
                rt_slots[m] = (rs, rt)

            emit_loads(0)
            for n in range(NTC):
                ob = oblks[n % 2]
                if n < NTC - 1:
                    emit_loads(n + 1)
                rs, rt = rt_slots.pop(n)
                prog.wait("pe", (st, "old", n))
                if n == 0:
                    prog.wait("pe", (st, "wo", NCC - 1))
                for pc in range(4):
                    pp = ppj[pc % 2]
                    if n >= 1 or pc >= 2:
                        pn_, pcc = (n, pc - 2) if pc >= 2 else (n - 1, pc + 2)
                        prog.wait("pe", (st, "rad", pn_, pcc))
                    for hc in range(NCC):
                        prog.op("pe", lambda e, pp=pp, ob=ob, hc=hc, pc=pc: e.matmul(
                            pp[:, :],
                            ob[:, hc * 128:(hc + 1) * 128],
                            wbuf[:, hc * D + pc * 512: hc * D + (pc + 1) * 512],
                            start=(hc == 0), stop=(hc == NCC - 1)),
                            inc=None if hc != NCC - 1 else "pe",
                            key=None if hc != NCC - 1 else (st, "pj", n, pc))
                    prog.wait("ve", (st, "pj", n, pc))
                    if pc == 0:
                        prog.wait("ve", (st, "rr", n))
                    prog.op("ve", lambda e, rt=rt, pp=pp, pc=pc: e.tensor_tensor(
                        rt[:, pc * 512:(pc + 1) * 512], rt[:, pc * 512:(pc + 1) * 512],
                        pp[:, :], ALU.add),
                        inc="ve", key=(st, "rad", n, pc))
                prog.alias((st, "puse", n), (st, "pj", n, 3))
                prog.wait("sy", (st, "rad", n, 3))
                prog.dma("sy", res_dst[n * 128:(n + 1) * 128, :], rt[:, :], f"de{rs}",
                         key=(st, "rst", n))
                ring_rtB.consumer(rs, (st, "rst", n))

        flushed = {e: 0 for e in ENGINES}

        def flush_block():
            for sn in prog.all_dma_sems():
                v = prog.cnt[sn]
                if v > 0:
                    eng = "gp" if sn in ("gw", "gwb") or sn.startswith(("da", "ga")) else "sy"
                    prog.ops[eng].append(
                        lambda e, sem=prog.sems[sn], v=v: e.wait_ge(sem, v))
            start_idx = dict(flushed)
            with nc.Block() as blk:
                def mk(eng_name):
                    def run(eng):
                        for f in prog.ops[eng_name][start_idx[eng_name]:]:
                            f(eng)
                    return run
                blk.tensor(mk("pe"))
                blk.vector(mk("ve"))
                blk.scalar(mk("ac"))
                blk.gpsimd(mk("gp"))
                blk.sync(mk("sy"))
            for e in ENGINES:
                flushed[e] = len(prog.ops[e])

        emit_const_loads()
        flush_block()

        if n_sub >= 2:
            with ExitStack() as s0:
                SC["pps2"] = [s0.enter_context(nc.psum_tensor(f"p_q{i}", [128, 512], F32))
                              for i in range(8)]
                SC["p_wt"] = [s0.enter_context(nc.sbuf_tensor(f"p_w{i}", [128, D], BF16))
                              for i in range(4)]
                SC["p_stg"] = [s0.enter_context(nc.sbuf_tensor(f"p_s{i}", [128, D], BF16))
                               for i in range(4)]
                emit_w2prep(90)
                flush_block()

        stage_specs = []
        if n_sub >= 1:
            stage_specs.append(("attn_a", 0))
        if n_sub >= 2:
            stage_specs.append(("ffn", 1))
        if n_sub >= 3:
            stage_specs.append(("attn_b", 2))
        if n_sub >= 4:
            stage_specs.append(("ffn", 3))

        for kind, idx in stage_specs:
            st = idx
            res_src, res_dst = res_seq[idx], res_seq[idx + 1]
            if kind in ("attn_a", "attn_b"):
                is_a = kind == "attn_a"
                with ExitStack() as sA:
                    SC["brow"] = sA.enter_context(nc.sbuf_tensor(f"s{st}_br", [1, 2 * D], BF16))
                    for par in range(2):
                        SC.pop(("xb_war", par), None)
                        SC.pop(("xb_rwar", par), None)
                        SC.pop(("elu_war", par), None)
                        SC.pop(("lnb_y_war", par), None)
                        SC.pop(("lnb_r_war", par), None)
                        SC.pop(("lnb_y2_war", par), None)
                    SC.pop("wload_war_q", None); SC.pop("wload_war_k", None)
                    SC.pop("wload_war_v", None)
                    sXW = ExitStack()
                    SC["xb"] = sXW.enter_context(nc.sbuf_tensor(f"s{st}_xb", [128, NCC * T], BF16))
                    SC["wbufH"] = sXW.enter_context(nc.sbuf_tensor(f"s{st}_wh", [128, NCC * 1024], BF16))
                    SC["sqbf"] = sXW.enter_context(nc.sbuf_tensor(f"s{st}_sq", [128, D], BF16))
                    SC["scr_c"] = sXW.enter_context(nc.sbuf_tensor(f"s{st}_sc", [128, 2048], F32))
                    SC["ybf"] = sXW.enter_context(nc.sbuf_tensor(f"s{st}_yb", [128, 2048], BF16))
                    SC["ring_xn"] = Ring([sXW.enter_context(
                        nc.sbuf_tensor(f"s{st}_xn{i}", [128, 1024], BF16)) for i in range(2)])
                    SC["ring_st"] = Ring([sXW.enter_context(
                        nc.sbuf_tensor(f"s{st}_st{i}", [128, 1024], BF16)) for i in range(2)])
                    if not is_a:
                        SC["gamt"] = sXW.enter_context(nc.sbuf_tensor(f"s{st}_gm", [128, D], BF16))
                        SC["y2bf"] = sXW.enter_context(nc.sbuf_tensor(f"s{st}_y2", [128, 2048], BF16))
                    SC["ring_pt"] = Ring([sXW.enter_context(
                        nc.psum_tensor(f"s{st}_pt{i}", [128, 128], BF16)) for i in range(4)])
                    SC["ring_pp"] = Ring([sXW.enter_context(
                        nc.psum_tensor(f"s{st}_pp{i}", [128, 1024], F32)) for i in range(2)])
                    sR = ExitStack()
                    SC["ring_rt"] = Ring([sR.enter_context(
                        nc.sbuf_tensor(f"s{st}_rt{i}", [128, D], F32)) for i in range(4)])
                    emit_xb(st, res_src)
                    sR.close()
                    emit_attn_phaseA(st, kind)
                    SC["vaug"] = sA.enter_context(
                        nc.sbuf_tensor(f"s{st}_va", [128, NTC * H * 129], BF16, side="right"))
                    emit_attn_v(st, kind)
                    sXW.close()
                    if is_a:
                        with ExitStack() as sB:
                            SC["wbuf"] = sB.enter_context(
                                nc.sbuf_tensor(f"s{st}_wo", [128, NCC * D], BF16))
                            SC["psa2"] = [sB.enter_context(
                                nc.psum_tensor(f"s{st}_psa{i}", [128, 128], F32))
                                for i in range(2)]
                            SC["psTb"] = sB.enter_context(
                                nc.psum_tensor(f"s{st}_psTb", [128, 128], BF16))
                            SC["pso2"] = [sB.enter_context(
                                nc.psum_tensor(f"s{st}_pso{i}", [128, 129], F32))
                                for i in range(2)]
                            SC["kvp2"] = [sB.enter_context(
                                nc.psum_tensor(f"s{st}_kvp{i}", [128, 387], F32))
                                for i in range(2)]
                            SC["ppj"] = sB.enter_context(
                                nc.psum_tensor(f"s{st}_ppj", [128, 512], F32))
                            SC["state"] = sB.enter_context(
                                nc.sbuf_tensor(f"s{st}_kv", [128, H * 129], F32))
                            SC["state_bf2"] = [sB.enter_context(
                                nc.sbuf_tensor(f"s{st}_kvb{i}", [128, H * 129], BF16))
                                for i in range(2)]
                            SC["atbf"] = sB.enter_context(
                                nc.sbuf_tensor(f"s{st}_at", [128, 256], BF16))
                            SC["obf"] = sB.enter_context(
                                nc.sbuf_tensor(f"s{st}_ob", [128, 256], BF16))
                            for kk in ("a_psTb",):
                                SC.pop(kk, None)
                            for hp_ in range(2):
                                for kk in ("a_psa", "a_atb", "a_pso", "a_obf",
                                           "kvp_war"):
                                    SC.pop((kk, hp_), None)
                            SC["ostgs"] = [sB.enter_context(
                                nc.sbuf_tensor(f"s{st}_og{i}", [128, D], BF16))
                                for i in range(2)]
                            SC["qblkb"] = [sB.enter_context(
                                nc.sbuf_tensor(f"s{st}_qb{i}", [128, D], BF16))
                                for i in range(2)]
                            SC["kblkb"] = [sB.enter_context(
                                nc.sbuf_tensor(f"s{st}_kb{i}", [128, D], BF16))
                                for i in range(2)]
                            SC["ktblk"] = [sB.enter_context(
                                nc.sbuf_tensor(f"s{st}_kt{i}", [128, D], BF16))
                                for i in range(2)]
                            SC["ring_rtB"] = Ring([sB.enter_context(
                                nc.sbuf_tensor(f"s{st}_rb{i}", [128, D], F32))
                                for i in range(2)])
                            emit_attn_a_phaseB(st, res_src, res_dst)
                            flush_block()
                    else:
                        with ExitStack() as sB:
                            SC["psS"] = sB.enter_context(
                                nc.psum_tensor(f"s{st}_psS", [128, 1024], F32))
                            SC["psO2"] = [sB.enter_context(
                                nc.psum_tensor(f"s{st}_pO{i}", [128, 129], F32))
                                for i in range(2)]
                            SC["psT2"] = [sB.enter_context(
                                nc.psum_tensor(f"s{st}_psTc{i}", [128, 128], BF16))
                                for i in range(2)]
                            SC["qstr"] = sB.enter_context(
                                nc.sbuf_tensor(f"s{st}_qs", [128, H * 512], BF16))
                            SC["kpre"] = sB.enter_context(
                                nc.sbuf_tensor(f"s{st}_kp", [128, NTC * D], BF16))
                            SC["ostgb"] = [sB.enter_context(
                                nc.sbuf_tensor(f"s{st}_ot{i}", [128, D], BF16))
                                for i in range(4)]
                            SC["atbf"] = sB.enter_context(
                                nc.sbuf_tensor(f"s{st}_atb", [128, NTC * 512], BF16))
                            SC["obf"] = sB.enter_context(
                                nc.sbuf_tensor(f"s{st}_obb", [128, 256], BF16))
                            SC["sqf"] = sB.enter_context(
                                nc.sbuf_tensor(f"s{st}_sqf", [128, 512], F32))
                            for kk in range(2):
                                for nm in ("b_psS", "b_psO", "b_psT", "b_obf"):
                                    SC.pop((nm, kk), None)
                            emit_attn_b_attention(st)
                        with ExitStack() as sC:
                            SC["ppj"] = [sC.enter_context(
                                nc.psum_tensor(f"s{st}_bpj{i}", [128, 512], F32))
                                for i in range(2)]
                            SC["wbuf"] = sC.enter_context(
                                nc.sbuf_tensor(f"s{st}_wob", [128, NCC * D], BF16))
                            SC["oblks"] = [sC.enter_context(
                                nc.sbuf_tensor(f"s{st}_obk{i}", [128, D], BF16))
                                for i in range(2)]
                            SC["ring_rtB"] = Ring([sC.enter_context(
                                nc.sbuf_tensor(f"s{st}_rc{i}", [128, D], F32))
                                for i in range(2)])
                            emit_attn_b_proj(st, res_src, res_dst)
                            flush_block()
            else:
                for par in range(2):
                    SC.pop(("xb_war", par), None)
                    SC.pop(("xb_rwar", par), None)
                for slot in range(4):
                    SC.pop(("fst_prev", slot), None)
                with ExitStack() as sF:
                    SC["xb"] = sF.enter_context(nc.sbuf_tensor(f"f{st}_xb", [128, NCC * T], BF16))
                    sX = ExitStack()
                    SC["sqbf"] = sX.enter_context(nc.sbuf_tensor(f"f{st}_sq", [128, D], BF16))
                    SC["ring_xn"] = Ring([sX.enter_context(
                        nc.sbuf_tensor(f"f{st}_xn{i}", [128, 1024], BF16)) for i in range(2)])
                    SC["ring_pt"] = Ring([sX.enter_context(
                        nc.psum_tensor(f"f{st}_pt{i}", [128, 128], BF16)) for i in range(2)])
                    SC["ring_rt"] = Ring([sX.enter_context(
                        nc.sbuf_tensor(f"f{st}_rt{i}", [128, D], F32)) for i in range(4)])
                    emit_xb(st, res_src)
                    sX.close()
                    with ExitStack() as sG:
                        SC["psG"] = sG.enter_context(nc.psum_tensor(f"f{st}_pG", [128, 1024], F32))
                        SC["psL"] = sG.enter_context(nc.psum_tensor(f"f{st}_pL", [128, 1024], F32))
                        SC["ppcs"] = [sG.enter_context(nc.psum_tensor(f"f{st}_pc{i}", [128, 1024], F32))
                                      for i in range(2)]
                        SC["acc"] = sG.enter_context(nc.sbuf_tensor(f"f{st}_ac", [128, 8 * D], F32))
                        SC["sg"] = sG.enter_context(nc.sbuf_tensor(f"f{st}_sg", [128, 1024], F32))
                        SC["w12t"] = [sG.enter_context(nc.sbuf_tensor(f"f{st}_w1{i}", [128, D], BF16))
                                      for i in range(2)]
                        SC["w34t"] = [sG.enter_context(nc.sbuf_tensor(f"f{st}_w3{i}", [128, D], BF16))
                                      for i in range(2)]
                        SC["hgt"] = [sG.enter_context(nc.sbuf_tensor(f"f{st}_hg{i}", [128, 1024], BF16))
                                     for i in range(G)]
                        SC["w2gt"] = [sG.enter_context(nc.sbuf_tensor(f"f{st}_w2{i}", [128, D], BF16))
                                      for i in range(G)]
                        SC["ring_rtF"] = Ring([sG.enter_context(
                            nc.sbuf_tensor(f"f{st}_rr{i}", [128, 1024], F32)) for i in range(2)])
                        emit_ffn(st, res_src, res_dst, first_pass=(idx == 1))
                        flush_block()

        for sn in prog.all_dma_sems():
            if prog.cnt[sn] > 0:
                prog.op("sy", lambda e, sn=sn: e.wait_ge(sems[sn], prog.cnt[sn]))
        flush_block()

    return nc


def _mk_hadamard(n):
    h = np.array([[1.0]], dtype=np.float64)
    while h.shape[0] < n:
        h = np.block([[h, h], [h, -h]])
    return h


def make_inmaps(inputs):
    bf = ml_dtypes.bfloat16
    x = np.ascontiguousarray(np.asarray(inputs["L0_attn_a_x"], dtype=np.float32))
    B = x.shape[0]

    def f32(a):
        return np.ascontiguousarray(np.asarray(a, dtype=np.float32))

    common = {
        "wq_a": f32(inputs["L0_attn_a_Wq"]), "wk_a": f32(inputs["L0_attn_a_Wk"]),
        "wv_a": f32(inputs["L0_attn_a_Wv"]), "wo_a": f32(inputs["L0_attn_a_Wo"]),
        "bq_a": f32(inputs["L0_attn_a_bq"]).reshape(1, D),
        "bk_a": f32(inputs["L0_attn_a_bk"]).reshape(1, D),
        "bv_a": f32(inputs["L0_attn_a_bv"]).reshape(1, D),
        "bo_a": f32(inputs["L0_attn_a_bo"]).reshape(1, D),
        "wq_b": np.ascontiguousarray(f32(inputs["L0_attn_b_Wq"]).T),
        "wk_b": np.ascontiguousarray(f32(inputs["L0_attn_b_Wk"]).T),
        "wv_b": np.ascontiguousarray(f32(inputs["L0_attn_b_Wv"]).T),
        "wo_b": np.ascontiguousarray(f32(inputs["L0_attn_b_Wo"]).T),
        "gam_t": np.ascontiguousarray(np.tile(f32(inputs["L0_attn_b_gamma"]), (128, H))),
        "bet_row": np.ascontiguousarray(np.tile(f32(inputs["L0_attn_b_beta"]), H)).reshape(1, D),
        "w1t": np.ascontiguousarray(
            f32(inputs["L0_ffn_W1"]).T.reshape(NCC, 128, NHC, 128).transpose(2, 1, 0, 3)
        ).reshape(NHC, 128, D),
        "w3t": np.ascontiguousarray(
            f32(inputs["L0_ffn_W3"]).T.reshape(NCC, 128, NHC, 128).transpose(2, 1, 0, 3)
        ).reshape(NHC, 128, D),
        "w2t": np.ascontiguousarray(f32(inputs["L0_ffn_W2"]).T),
        "ident_c": np.eye(128, dtype=bf),
        "triu_c": np.triu(np.ones((128, 128))).astype(bf),
        "h128_c": _mk_hadamard(128).astype(bf),
        "h64_c": _mk_hadamard(64).astype(bf),
    }
    return [dict(common, x=np.ascontiguousarray(x[b])) for b in range(B)], B


def kernel(**inputs):
    in_maps, B = make_inmaps(inputs)
    nc = build(N_SUB)
    res = run_bass_kernel_spmd(nc, in_maps, core_ids=list(range(B)))
    return np.stack([np.asarray(r["out"], dtype=np.float32) for r in res.results], axis=0)


# version 16
# speedup vs baseline: 1.1929x; 1.0406x over previous
import math
from contextlib import ExitStack

import numpy as np
import ml_dtypes
import concourse.bass as bass
import concourse.mybir as mybir
from concourse.bass_utils import run_bass_kernel_spmd

F32 = mybir.dt.float32
BF16 = mybir.dt.bfloat16
AF = mybir.ActivationFunctionType
ALU = mybir.AluOpType

T = 2048
D = 2048
H = 16
HD = 128
NTC = T // 128
NCC = D // 128
HID = 8192
NHC = HID // 128
G = 8
NGR = NHC // G
LN_EPS = 1e-5
EPS_A = 1e-6
EPS_B = 1e-5
FWHT_SCALE = float(np.float32(1.0 / np.sqrt(np.float64(HID))))
RD = float(1.0 / math.sqrt(HD))

N_SUB = 4

ENGINES = ("pe", "ve", "ac", "gp", "sy")


class Ring:

    def __init__(self, bufs):
        self.bufs = list(bufs)
        self.war = [None] * len(self.bufs)
        self.idx = 0

    def acquire(self):
        s = self.idx % len(self.bufs)
        self.idx += 1
        return s, self.bufs[s], self.war[s]

    def consumer(self, s, eng_key):
        self.war[s] = eng_key


class Prog:
    def __init__(self, nc, sems):
        self.nc = nc
        self.sems = sems
        self.cnt = {k: 0 for k in sems}
        self.keys = {}
        self.ops = {e: [] for e in ENGINES}

    ENGSEM = {"ve": "ve", "ac": "ac", "gp": "gc"}

    def op(self, eng, fn, inc=None, k=1, key=None, dma=False):
        auto = (not dma) and eng in self.ENGSEM
        if auto:
            es = self.ENGSEM[eng]
            if inc is None:
                inc = es
                k = 1
            assert inc == es
            prevv = self.cnt[es]
            if prevv > 0:
                sem0 = self.sems[es]
                self.ops[eng].append(
                    lambda e, sem0=sem0, prevv=prevv: e.wait_ge(sem0, prevv))
        if inc is not None:
            self.cnt[inc] += k
            if key is not None:
                assert key not in self.keys, f"dup key {key}"
                self.keys[key] = (inc, self.cnt[inc])
            sem = self.sems[inc]
            self.ops[eng].append(lambda e, fn=fn, sem=sem, kk=k: fn(e).then_inc(sem, kk))
        else:
            self.ops[eng].append(lambda e, fn=fn: fn(e))

    def dma(self, eng, out, in_, inc, key=None):
        self.op(eng, lambda e, out=out, in_=in_: e.dma_start(out, in_),
                inc=inc, k=16, key=key, dma=True)

    def all_dma_sems(self):
        return [n for n in self.sems if n not in ("pe", "ve", "ac", "gc")]

    def alias(self, new_key, old_key):
        self.keys[new_key] = self.keys[old_key]

    def wait(self, eng, key):
        assert key in self.keys, f"missing key {key}"
        sem_name, val = self.keys[key]
        sem = self.sems[sem_name]
        self.ops[eng].append(lambda e, sem=sem, val=val: e.wait_ge(sem, val))

    def wait_if(self, eng, key):
        if isinstance(key, list):
            for k in key:
                self.wait_if(eng, k)
            return
        if key is not None and key in self.keys:
            self.wait(eng, key)


def build(n_sub=N_SUB):
    nc = bass.Bass("TRN2", target_bir_lowering=False)

    def din(name, shape, dtype=F32):
        return nc.dram_tensor(name, shape, dtype, kind="ExternalInput").ap()

    x_in = din("x", [T, D])
    wq_a = din("wq_a", [D, D]); wk_a = din("wk_a", [D, D])
    wv_a = din("wv_a", [D, D]); wo_a = din("wo_a", [D, D])
    bq_a = din("bq_a", [1, D]); bk_a = din("bk_a", [1, D])
    bv_a = din("bv_a", [1, D]); bo_a = din("bo_a", [1, D])
    wq_b = din("wq_b", [D, D]); wk_b = din("wk_b", [D, D])
    wv_b = din("wv_b", [D, D]); wo_b = din("wo_b", [D, D])
    gam_t = din("gam_t", [128, D])
    bet_row_i = din("bet_row", [1, D])
    w1t = din("w1t", [NHC, 128, D])
    w3t = din("w3t", [NHC, 128, D])
    w2t = din("w2t", [HID, D])
    ident_c = din("ident_c", [128, 128], BF16)
    triu_c = din("triu_c", [128, 128], BF16)
    h128_c = din("h128_c", [128, 128], BF16)
    h64_c = din("h64_c", [64, 64], BF16)

    out_t = nc.dram_tensor("out", [T, D], F32, kind="ExternalOutput").ap()

    res_a = nc.dram_tensor("res_a", [T, D], F32, kind="Internal").ap()
    res_b = nc.dram_tensor("res_b", [T, D], F32, kind="Internal").ap()
    qs_s = nc.dram_tensor("qs_s", [NTC, 128, D], BF16, kind="Internal").ap()
    ks_s = nc.dram_tensor("ks_s", [NTC, 128, D], BF16, kind="Internal").ap()
    kts_s = nc.dram_tensor("kts_s", [NTC, 128, D], BF16, kind="Internal").ap()
    q4_s = nc.dram_tensor("q4_s", [4, 128, H * 512], BF16, kind="Internal").ap()
    os_s = nc.dram_tensor("os_s", [NTC, 128, D], BF16, kind="Internal").ap()
    b_s = nc.dram_tensor("b_s", [64, 128, D], BF16, kind="Internal").ap()
    w2p_s = nc.dram_tensor("w2p_s", [NHC, 128, D], BF16, kind="Internal").ap()
    w1c_s = nc.dram_tensor("w1c_s", [NHC, 128, D], BF16, kind="Internal").ap()
    w3c_s = nc.dram_tensor("w3c_s", [NHC, 128, D], BF16, kind="Internal").ap()

    res_seq = [x_in, res_a, res_b, res_a, out_t]
    if n_sub < 4:
        res_seq[n_sub] = out_t

    with ExitStack() as es:
        E = es.enter_context
        sem_names = (["pe", "ve", "ac", "gc", "gw", "gwb", "sw2",
                      "sc10", "sc11", "sc30", "sc31", "sl"]
                     + [f"da{i}" for i in range(4)]
                     + [f"ga{i}" for i in range(4)]
                     + [f"db{i}" for i in range(4)]
                     + [f"dc{i}" for i in range(2)]
                     + [f"dd{i}" for i in range(4)]
                     + [f"dg{i}" for i in range(4)]
                     + [f"de{i}" for i in range(2)]
                     + [f"df{i}" for i in range(4)])
        sems = {n: E(nc.semaphore(n)) for n in sem_names}
        prog = Prog(nc, sems)

        ident = E(nc.sbuf_tensor("ident", [128, 128], BF16))
        triu = E(nc.sbuf_tensor("triu", [128, 128], BF16))
        h128 = E(nc.sbuf_tensor("h128", [128, 128], BF16))
        h64 = E(nc.sbuf_tensor("h64", [64, 64], BF16))
        ones_row = E(nc.sbuf_tensor("ones_row", [1, 128], BF16))
        cs = E(nc.sbuf_tensor("cs", [128, 104], F32))
        zcol = E(nc.sbuf_tensor("zcol", [128, 8], F32))
        cbias = E(nc.sbuf_tensor("cbias", [128, 2], F32))

        def col(t_, a, n):
            return t_[:, a:a + n]

        SC = {}

        def emit_const_loads():
            prog.dma("sy", ident[:, :], ident_c[:, :], "sl")
            prog.dma("sy", triu[:, :], triu_c[:, :], "sl")
            prog.dma("sy", h128[:, :], h128_c[:, :], "sl")
            prog.dma("sy", h64[:, :], h64_c[:, :], "sl", key="c_all")
            prog.op("gp", lambda e: e.memset(ones_row[:, :], 1.0), inc="gc", key="c_ones")
            prog.op("gp", lambda e: e.memset(cbias[:, 0:1], LN_EPS))
            prog.op("gp", lambda e: e.memset(cbias[:, 1:2], math.log(RD)),
                    inc="gc", key="c_eps")
            prog.wait("pe", "c_all")
            prog.wait("pe", "c_ones")
            prog.wait("ve", "c_all")
            prog.wait("ac", "c_eps")

        def emit_xb(st, res_src):
            ring_rt, ring_xn, ring_pt = SC["ring_rt"], SC["ring_xn"], SC["ring_pt"]
            sqbf = SC["sqbf"]
            xb = SC["xb"]
            for tc in range(NTC):
                par = tc % 2
                base = 0 if par == 0 else 96
                ss, sq, mu, msq, var_, lnv, rstd = (col(cs, base + i, 1) for i in range(7))
                rs, rt, rwar = ring_rt.acquire()
                prog.wait_if("sy", rwar)
                prog.dma("sy", rt[:, :], res_src[tc * 128:(tc + 1) * 128, :], f"db{rs}",
                         key=(st, "xr", tc))
                prog.wait("ve", (st, "xr", tc))
                prog.wait_if("ve", SC.get(("xb_war", par)))
                prog.op("ve", lambda e, rt=rt, ss=ss: e.reduce_sum(
                    ss, rt[:, :], axis=mybir.AxisListType.X))
                prog.op("ve", lambda e, rt=rt, sq=sq: e.scalar_tensor_tensor(
                    sqbf[:, :], rt[:, :], 1.0, rt[:, :], ALU.bypass, ALU.mult,
                    accum_out=sq))
                prog.op("ve", lambda e, mu=mu, ss=ss: e.tensor_scalar_mul(mu, ss, 1.0 / D))
                prog.op("ve", lambda e, msq=msq, mu=mu: e.scalar_tensor_tensor(
                    msq, mu, 1.0, mu, ALU.bypass, ALU.mult))
                prog.op("ve", lambda e, var_=var_, sq=sq, msq=msq: e.tensor_scalar(
                    var_, sq, 1.0 / D, msq, ALU.mult, ALU.subtract),
                    inc="ve", key=(st, "xv", tc))
                prog.wait("ac", (st, "xv", tc))
                prog.wait_if("ac", SC.get(("xb_rwar", par)))
                prog.op("ac", lambda e, lnv=lnv, var_=var_: e.activation(
                    lnv, var_, AF.Ln, bias=cbias[:, 0:1]))
                prog.op("ac", lambda e, rstd=rstd, lnv=lnv: e.activation(
                    rstd, lnv, AF.Exp, scale=-0.5),
                    inc="ac", key=(st, "xr2", tc))
                prog.wait("ve", (st, "xr2", tc))
                lastn = None
                for half in range(2):
                    xs_, xn, xwar = ring_xn.acquire()
                    prog.wait_if("ve", xwar)
                    prog.op("ve", lambda e, rt=rt, xn=xn, half=half, mu=mu, rstd=rstd: e.tensor_scalar(
                        xn[:, :], rt[:, half * 1024:(half + 1) * 1024], mu, rstd,
                        ALU.subtract, ALU.mult),
                        inc="ve", key=(st, "xn", tc, half))
                    lastn = (st, "xn", tc, half)
                    prog.wait("pe", (st, "xn", tc, half))
                    for g4 in range(2):
                        gidx = half * 2 + g4
                        dcb = half * 8 + g4 * 4
                        ps_, pt, pwar = ring_pt.acquire()
                        prog.wait_if("pe", pwar)
                        for k4 in range(4):
                            dc8 = g4 * 4 + k4
                            prog.op("pe", lambda e, pt=pt, xn=xn, dc8=dc8, k4=k4: e.transpose(
                                pt[:, k4 * 128:(k4 + 1) * 128],
                                xn[:, dc8 * 128:(dc8 + 1) * 128], ident[:, :]),
                                inc="pe" if k4 == 3 else None,
                                key=(st, "xt", tc, dcb + 3) if k4 == 3 else None)
                        prog.wait("ac", (st, "xt", tc, dcb + 3))
                        dst = xb[:, :].rearrange("p (d t) -> p d t", t=T)[
                            :, dcb:dcb + 4, tc * 128:(tc + 1) * 128]
                        src = pt[:, :].rearrange("p (g t) -> p g t", t=128)
                        prog.op("ac", lambda e, dst=dst, src=src: e.copy(dst, src),
                                inc="ac", key=(st, "xe", tc, gidx))
                        ring_pt.consumer(ps_, (st, "xe", tc, gidx))
                    ring_xn.consumer(xs_, (st, "xt", tc, half * 8 + 7))
                SC[("xb_war", par)] = lastn
                SC[("xb_rwar", par)] = lastn
                ring_rt.consumer(rs, lastn)

        def emit_proj(st, tag, wsrc, bias_row, evac, scale_tile=None):
            ring_pp = SC["ring_pp"]
            wbufH = SC["wbufH"]
            xb = SC["xb"]
            for dh in range(2):
                for cc in range(NCC):
                    if cc == 0:
                        if dh == 0:
                            prog.wait_if("gp", SC.get("wload_war_" + tag))
                        else:
                            prog.wait("gp", (st, "pmm", tag, 0, NTC - 1))
                    prog.dma("gp", wbufH[:, cc * 1024:(cc + 1) * 1024],
                             wsrc[cc * 128:(cc + 1) * 128, dh * 1024:(dh + 1) * 1024],
                             "gw", key=(st, "w", tag, dh, cc))
                if scale_tile is not None:
                    prog.wait("ve", (st, "w", tag, dh, NCC - 1))
                    for cc in range(NCC):
                        prog.op("ve", lambda e, cc=cc, dh=dh, wbufH=wbufH, scale_tile=scale_tile: e.tensor_tensor(
                            wbufH[:, cc * 1024:(cc + 1) * 1024],
                            wbufH[:, cc * 1024:(cc + 1) * 1024],
                            scale_tile[:, dh * 1024:(dh + 1) * 1024], ALU.mult),
                            inc="ve", key=(st, "wsc", tag, dh, cc))
                for tc in range(NTC):
                    ps_, pp, pwar = ring_pp.acquire()
                    prog.wait_if("pe", pwar)
                    if tag == "q" and dh == 0:
                        prog.wait("pe", (st, "xe", tc, 3))
                    if tc == 0:
                        if scale_tile is not None:
                            prog.wait("pe", (st, "wsc", tag, dh, NCC - 1))
                        else:
                            prog.wait("pe", (st, "w", tag, dh, NCC - 1))
                    for cc in range(NCC):
                        last = (cc == NCC - 1) and bias_row is None
                        for nq in range(2):
                            prog.op("pe", lambda e, pp=pp, cc=cc, nq=nq, tc=tc, wbufH=wbufH, last=last: e.matmul(
                                pp[:, nq * 512:(nq + 1) * 512],
                                xb[:, cc * T + tc * 128: cc * T + (tc + 1) * 128],
                                wbufH[:, cc * 1024 + nq * 512: cc * 1024 + (nq + 1) * 512],
                                start=(cc == 0), stop=last),
                                inc=None if not (last and nq == 1) else "pe",
                                key=None if not (last and nq == 1) else (st, "pmm", tag, dh, tc))
                    if bias_row is not None:
                        for nq in range(2):
                            prog.op("pe", lambda e, pp=pp, nq=nq, dh=dh, bias_row=bias_row: e.matmul(
                                pp[:, nq * 512:(nq + 1) * 512],
                                ones_row[0:1, :],
                                bias_row[0:1, dh * 1024 + nq * 512: dh * 1024 + (nq + 1) * 512],
                                start=False, stop=True),
                                inc=None if nq == 0 else "pe",
                                key=None if nq == 0 else (st, "pmm", tag, dh, tc))
                    if SC.get("_pend_evac") is not None:
                        SC["_pend_evac"]()
                    lastkey, dt_ = evac(st, tag, dh, tc, pp)
                    ring_pp.consumer(ps_, lastkey)
                    SC["_pend_evac"] = dt_
            if SC.get("_pend_evac") is not None:
                SC["_pend_evac"]()
                SC["_pend_evac"] = None
            SC["_pend_evac"] = None

        def mk_evac_qk(spill, fm, spill_tok=None, spill_q4=False):
            def evac(st, tag, dh, tc, pp):
                ring_xn, ring_pt, ring_st = SC["ring_xn"], SC["ring_pt"], SC["ring_st"]
                xs_, xn, xwar = ring_xn.acquire()
                prog.wait_if("ve", xwar)
                last_key = fm(st, tag, dh, tc, pp, xn)
                tok_key = None
                if spill_tok is not None:
                    prog.wait("gp", (st, "fmv", tag, dh, tc, 0))
                    prog.dma("gp", spill_tok[tc, :, dh * 1024:(dh + 1) * 1024],
                             xn[:, :], f"dd{(tc * 2 + dh) % 4}",
                             key=(st, "ksp", tc, dh))
                    tok_key = (st, "ksp", tc, dh)

                def do_transposes(xn=xn, xs_=xs_, tag=tag, dh=dh, tc=tc,
                                  tok_key=tok_key):
                    ts_, stg, swar = ring_st.acquire()
                    for g4 in range(2):
                        ps_, pt, pwar = ring_pt.acquire()
                        prog.wait_if("pe", pwar)
                        for k4 in range(4):
                            h8 = g4 * 4 + k4
                            prog.wait("pe", (st, "fmv", tag, dh, tc, h8))
                            prog.op("pe", lambda e, pt=pt, xn=xn, h8=h8, k4=k4: e.transpose(
                                pt[:, k4 * 128:(k4 + 1) * 128],
                                xn[:, h8 * 128:(h8 + 1) * 128], ident[:, :]),
                                inc="pe" if k4 == 3 else None,
                                key=(st, "qt", tag, dh, tc, g4 * 4 + 3) if k4 == 3 else None)
                        prog.wait("ve", (st, "qt", tag, dh, tc, g4 * 4 + 3))
                        if g4 == 0:
                            prog.wait_if("ve", swar)
                        prog.op("ve", lambda e, pt=pt, stg=stg, g4=g4: e.tensor_copy(
                            stg[:, g4 * 512:(g4 + 1) * 512], pt[:, :]),
                            inc="ve", key=(st, "qe", tag, dh, tc, g4))
                        ring_pt.consumer(ps_, (st, "qe", tag, dh, tc, g4))
                    xn_wars = [(st, "qt", tag, dh, tc, 7)]
                    if tok_key is not None:
                        xn_wars.append(tok_key)
                    ring_xn.consumer(xs_, xn_wars)
                    prog.wait("sy", (st, "qe", tag, dh, tc, 1))
                    if spill_q4:
                        dstq = q4_s[tc // 4].rearrange(
                            "p (h t) -> p h t", t=512)[
                            :, dh * 8:(dh + 1) * 8,
                            (tc % 4) * 128:(tc % 4 + 1) * 128]
                        srcq = stg[:, :].rearrange("p (h t) -> p h t", t=128)
                        prog.dma("sy", dstq, srcq,
                                 f"de{ts_}", key=(st, "qsp", tag, tc, dh))
                    else:
                        prog.dma("sy", spill[tc, :, dh * 1024:(dh + 1) * 1024],
                                 stg[:, :], f"de{ts_}", key=(st, "qsp", tag, tc, dh))
                    ring_st.consumer(ts_, (st, "qsp", tag, tc, dh))
                return last_key, do_transposes
            return evac

        def fm_elu(st, tag, dh, tc, pp, xn):
            par = (dh * NTC + tc) % 2
            tf = SC["scr_c"][:, par * 1024:(par + 1) * 1024]
            ebf = SC["ybf"][:, par * 1024:(par + 1) * 1024]
            prog.wait("ac", (st, "pmm", tag, dh, tc))
            prog.wait_if("ac", SC.get(("elu_war", par)))
            prog.op("ac", lambda e, pp=pp, tf=tf: e.copy(tf, pp[:, :]),
                    inc="ac", key=(st, "fmd", tag, dh, tc))
            prog.wait("ve", (st, "fmd", tag, dh, tc))
            prog.op("ve", lambda e, xn=xn, tf=tf: e.tensor_scalar_min(xn[:, :], tf, 0.0),
                    inc="ve", key=(st, "fmm", tag, dh, tc))
            prog.wait("ac", (st, "fmm", tag, dh, tc))
            prog.op("ac", lambda e, xn=xn, ebf=ebf: e.activation(ebf, xn[:, :], AF.Exp),
                    inc="ac", key=(st, "fme", tag, dh, tc))
            prog.wait("ve", (st, "fme", tag, dh, tc))
            prog.op("ve", lambda e, pp=pp, xn=xn, tf=tf, ebf=ebf: e.scalar_tensor_tensor(
                xn[:, :], tf, 0.0, ebf, ALU.max, ALU.add),
                inc="ve", key=(st, "fmn", tag, dh, tc))
            SC[("elu_war", par)] = (st, "fmn", tag, dh, tc)
            for hh in range(8):
                prog.alias((st, "fmv", tag, dh, tc, hh), (st, "fmn", tag, dh, tc))
            return (st, "fmd", tag, dh, tc)

        def fm_lnb(st, tag, dh, tc, pp, xn):
            par = (dh * NTC + tc) % 2
            ybf = SC["ybf"][:, par * 1024:(par + 1) * 1024]
            y2bf = SC["y2bf"][:, par * 1024:(par + 1) * 1024]
            base = 8 + par * 40
            seg = col(cs, base, 8); seg2 = col(cs, base + 8, 8)
            muh = col(cs, base + 16, 8); rstdh = col(cs, base + 24, 8)
            varh = col(cs, base + 32, 8)
            prog.wait("ac", (st, "pmm", tag, dh, tc))
            prog.wait_if("ac", SC.get(("lnb_y_war", par)))
            prog.op("ac", lambda e, pp=pp, ybf=ybf: e.copy(ybf, pp[:, :]),
                    inc="ac", key=(st, "fmd", tag, dh, tc))
            prog.wait_if("ac", SC.get(("lnb_y2_war", par)))
            prog.op("ac", lambda e, ybf=ybf, y2bf=y2bf: e.activation(y2bf, ybf, AF.Square),
                    inc="ac", key=(st, "fsq", tag, dh, tc))
            y3 = ybf.rearrange("p (s f) -> p s f", f=128)
            y23 = y2bf.rearrange("p (s f) -> p s f", f=128)
            prog.wait("ve", (st, "fmd", tag, dh, tc))
            prog.op("ve", lambda e, y3=y3, seg=seg: e.reduce_sum(seg, y3, axis=mybir.AxisListType.X))
            prog.wait("ve", (st, "fsq", tag, dh, tc))
            prog.op("ve", lambda e, y23=y23, seg2=seg2: e.reduce_sum(seg2, y23, axis=mybir.AxisListType.X),
                    inc="ve", key=(st, "fr2", tag, dh, tc))
            SC[("lnb_y2_war", par)] = (st, "fr2", tag, dh, tc)
            prog.op("ve", lambda e, muh=muh, seg=seg: e.tensor_scalar_mul(muh, seg, 1.0 / HD))
            prog.op("ve", lambda e, seg=seg, muh=muh: e.scalar_tensor_tensor(
                seg, muh, 1.0, muh, ALU.bypass, ALU.mult))
            prog.op("ve", lambda e, varh=varh, seg2=seg2, seg=seg: e.scalar_tensor_tensor(
                varh, seg2, 1.0 / HD, seg, ALU.mult, ALU.subtract),
                inc="ve", key=(st, "fv1", tag, dh, tc))
            prog.wait("ac", (st, "fv1", tag, dh, tc))
            prog.wait_if("ac", SC.get(("lnb_r_war", par)))
            prog.op("ac", lambda e, varh=varh: e.activation(varh, varh, AF.Ln, bias=cbias[:, 0:1]))
            if tag == "q":
                prog.op("ac", lambda e, rstdh=rstdh, varh=varh: e.activation(
                    rstdh, varh, AF.Exp, scale=-0.5, bias=cbias[:, 1:2]),
                    inc="ac", key=(st, "fmr", tag, dh, tc))
            else:
                prog.op("ac", lambda e, rstdh=rstdh, varh=varh: e.activation(
                    rstdh, varh, AF.Exp, scale=-0.5),
                    inc="ac", key=(st, "fmr", tag, dh, tc))
            prog.wait("ve", (st, "fmr", tag, dh, tc))
            keys = []
            for h8 in range(8):
                prog.op("ve", lambda e, h8=h8, xn=xn, ybf=ybf, muh=muh, rstdh=rstdh: e.tensor_scalar(
                    xn[:, h8 * 128:(h8 + 1) * 128],
                    ybf[:, h8 * 128:(h8 + 1) * 128],
                    muh[:, h8:h8 + 1], rstdh[:, h8:h8 + 1],
                    ALU.subtract, ALU.mult),
                    inc="ve", key=(st, "fnm", tag, dh, tc, h8))
                keys.append((st, "fnm", tag, dh, tc, h8))
            SC[("lnb_r_war", par)] = keys[-1]
            SC[("lnb_y_war", par)] = keys[-1]
            for h8 in range(8):
                prog.alias((st, "fmv", tag, dh, tc, h8), keys[h8])
            return (st, "fmd", tag, dh, tc)

        def mk_evac_v(st_v):
            vaug = SC["vaug"]

            def evac(st, tag, dh, tc, pp):
                prog.wait("ac", (st, "pmm", tag, dh, tc))
                if dh == 0 and tc == 0:
                    prog.wait("ac", (st, "vones"))
                off = tc * 2064 + dh * 8 * 129
                dst = bass.AP(vaug, off, [[NTC * H * 129, 128], [129, 8], [1, 128]])
                prog.op("ac", lambda e, pp=pp, dst=dst: e.copy(dst, pp[:, :]),
                        inc="ac", key=(st, "vev", dh, tc))
                return (st, "vev", dh, tc), None
            return evac

        def emit_attn_phaseA(st, kind):
            is_a = kind == "attn_a"
            if is_a:
                brow = SC["brow"]
                prog.dma("gp", brow[0:1, 0:D], bq_a[0:1, :], "gwb")
                prog.dma("gp", brow[0:1, D:2 * D], bk_a[0:1, :], "gwb", key=(st, "bqk"))
                prog.wait("pe", (st, "bqk"))
                emit_proj(st, "q", wq_a, brow[:, 0:D], mk_evac_qk(qs_s, fm_elu))
                SC["wload_war_k"] = (st, "pmm", "q", 1, NTC - 1)
                emit_proj(st, "k", wk_a, brow[:, D:2 * D],
                          mk_evac_qk(ks_s, fm_elu, spill_tok=kts_s))
            else:
                brow = SC["brow"]
                prog.dma("gp", SC["gamt"][:, :], gam_t[:, :], "gwb")
                prog.dma("gp", brow[0:1, 0:D], bet_row_i[0:1, :], "gwb", key=(st, "gbt"))
                prog.wait("ve", (st, "gbt"))
                prog.wait("pe", (st, "gbt"))
                emit_proj(st, "q", wq_b, brow[:, 0:D],
                          mk_evac_qk(None, fm_lnb, spill_q4=True),
                          scale_tile=SC["gamt"])
                SC["wload_war_k"] = (st, "pmm", "q", 1, NTC - 1)
                emit_proj(st, "k", wk_b, brow[:, 0:D], mk_evac_qk(ks_s, fm_lnb),
                          scale_tile=SC["gamt"])

        def emit_attn_v(st, kind):
            is_a = kind == "attn_a"
            vaug = SC["vaug"]
            va_ones = vaug[:, :].rearrange("p (t h c) -> p t h c", h=H, c=129)[:, :, :, 128:129]
            prog.op("ve", lambda e: e.memset(va_ones, 1.0), inc="ve", key=(st, "vones"))
            SC["wload_war_v"] = (st, "pmm", "k", 1, NTC - 1)
            SC["wload_war_" + "v"] = (st, "pmm", "k", 1, NTC - 1)
            if is_a:
                brow = SC["brow"]
                prog.wait("gp", (st, "pmm", "q", 1, NTC - 1))
                prog.dma("gp", brow[0:1, 0:D], bv_a[0:1, :], "gwb", key=(st, "bv"))
                prog.wait("pe", (st, "bv"))
                emit_proj(st, "v", wv_a, brow[:, 0:D], mk_evac_v(st))
            else:
                emit_proj(st, "v", wv_b, None, mk_evac_v(st))

        def emit_attn_a_phaseB(st, res_src, res_dst):
            psa2, psTb = SC["psa2"], SC["psTb"]
            pso2, kvp2 = SC["pso2"], SC["kvp2"]
            ppj = SC["ppj"]
            wbuf, vaug = SC["wbuf"], SC["vaug"]
            state, ostgs = SC["state"], SC["ostgs"]
            state_bf2, atbf, obf = SC["state_bf2"], SC["atbf"], SC["obf"]
            ring_rtB = SC["ring_rtB"]
            qblkb, kblkb, ktblk = SC["qblkb"], SC["kblkb"], SC["ktblk"]
            brow = SC["brow"]
            vdone = (st, "pmm", "v", 1, NTC - 1)
            GSZ = 3
            NG = (H + GSZ - 1) // GSZ
            CA_H = 8
            CA_C = (CA_H + 1) * 129
            prog.wait("gp", (st, "pmm", "k", 1, NTC - 1))
            prog.dma("gp", brow[0:1, D:2 * D], bo_a[0:1, :], "gwb", key=(st, "bo"))
            for gk in [vdone, (st, "qsp", "k", NTC - 2, 1), (st, "qsp", "k", NTC - 1, 1)]:
                prog.wait("gp", gk)
            for cc in range(NCC):
                prog.dma("gp", wbuf[:, cc * D:(cc + 1) * D],
                         wo_a[cc * 128:(cc + 1) * 128, :], "gw", key=(st, "wo", cc))
            guard_keys = [vdone, (st, "qe", "k", 1, NTC - 1, 1),
                          (st, "fmd", "k", 1, NTC - 1), (st, "vev", 1, NTC - 1),
                          (st, "qsp", "k", NTC - 2, 1), (st, "qsp", "k", NTC - 1, 1),
                          (st, "ksp", NTC - 2, 0), (st, "ksp", NTC - 2, 1),
                          (st, "ksp", NTC - 1, 0), (st, "ksp", NTC - 1, 1)]
            for geng in ("pe", "ve", "ac", "sy"):
                for gk in guard_keys:
                    prog.wait(geng, gk)
            prog.wait("pe", (st, "bo"))

            def emit_add(n, g):
                lo = g * GSZ * 129
                w_ = (min(GSZ, H - g * GSZ)) * 129
                prog.wait("ve", (st, "kv", n, min(g * GSZ + GSZ, H) - 1))
                if n >= 1:
                    prog.wait("ve", (st, "castA" if g <= 2 else "castB", n - 1))
                if n == 0:
                    prog.op("ve", lambda e, lo=lo, w_=w_, g=g: e.tensor_copy(
                        state[:, lo:lo + w_], kvp2[g % 2][:, 0:w_]),
                        inc="ve", key=(st, "sup", n, g))
                else:
                    prog.op("ve", lambda e, lo=lo, w_=w_, g=g: e.tensor_tensor(
                        state[:, lo:lo + w_], state[:, lo:lo + w_],
                        kvp2[g % 2][:, 0:w_], ALU.add),
                        inc="ve", key=(st, "sup", n, g))
                SC[("kvp_war", g % 2)] = (st, "sup", n, g)

            def emit_castA(n):
                prog.wait("ac", (st, "sup", n, 2))
                if n >= 1:
                    prog.wait("ac", (st, "ob", n - 1, CA_H))
                prog.op("ac", lambda e, n=n: e.copy(
                    state_bf2[n % 2][:, 0:CA_C], state[:, 0:CA_C]),
                    inc="ac", key=(st, "castA", n))

            def emit_castB(n):
                prog.wait("ac", (st, "sup", n, NG - 1))
                if n >= 1:
                    prog.wait("ac", (st, "ob", n - 1, H - 1))
                prog.op("ac", lambda e, n=n: e.copy(
                    state_bf2[n % 2][:, CA_C:H * 129], state[:, CA_C:H * 129]),
                    inc="ac", key=(st, "castB", n))

            def mk_pso(n, h, qb):
                hp = h % 2

                def pso():
                    voff = n * 2064 + h * 129
                    prog.wait_if("pe", SC.get(("a_pso", hp)))
                    prog.wait("pe", (st, "atm", n, h))
                    if n > 0:
                        prog.wait("pe", (st, "castA" if h <= CA_H else "castB", n - 1))
                        prog.op("pe", lambda e, qb=qb, h=h, hp=hp, n=n: e.matmul(
                            pso2[hp][:, 0:129], qb[:, h * 128:(h + 1) * 128],
                            state_bf2[(n - 1) % 2][:, h * 129:(h + 1) * 129],
                            start=True, stop=False))
                    prog.op("pe", lambda e, voff=voff, n=n, hp=hp: e.matmul(
                        pso2[hp][:, 0:129], atbf[:, hp * 128:(hp + 1) * 128],
                        vaug[:, voff:voff + 129],
                        start=(n == 0), stop=True), inc="pe", key=(st, "ob", n, h))
                    SC[("a_atb", hp)] = (st, "ob", n, h)
                    prog.wait("ve", (st, "ob", n, h))
                    prog.op("ve", lambda e, hp=hp: e.tensor_scalar_add(
                        zcol[:, hp * 2:hp * 2 + 1], pso2[hp][:, 128:129], EPS_A))
                    prog.op("ve", lambda e, hp=hp: e.reciprocal(
                        zcol[:, hp * 2 + 1:hp * 2 + 2], zcol[:, hp * 2:hp * 2 + 1]))
                    sl = h % 4
                    prog.wait_if("ve", SC.get(("a_obf", sl)))
                    prog.op("ve", lambda e, hp=hp, sl=sl: e.tensor_scalar_mul(
                        obf[:, sl * 128:(sl + 1) * 128], pso2[hp][:, 0:128],
                        zcol[:, hp * 2 + 1:hp * 2 + 2]),
                        inc="ve", key=(st, "odv", n, h))
                    SC[("a_pso", hp)] = (st, "odv", n, h)
                return pso

            def mk_tail(n, h, ostg):
                sl = h % 4

                def tail():
                    prog.wait("pe", (st, "odv", n, h))
                    if sl == 0:
                        prog.wait_if("pe", SC.get("a_psTb"))
                    prog.op("pe", lambda e, sl=sl: e.transpose(
                        psTb[:, sl * 128:(sl + 1) * 128],
                        obf[:, sl * 128:(sl + 1) * 128], ident[:, :]),
                        inc="pe", key=(st, "ot", n, h))
                    SC[("a_obf", sl)] = (st, "ot", n, h)
                    if sl == 3:
                        g = h // 4
                        prog.wait("ac", (st, "ot", n, h))
                        if g == 0:
                            if n >= 2:
                                prog.wait("ac", (st, "pj", n - 2, 3))
                            else:
                                prog.wait("ac", vdone)
                        prog.op("ac", lambda e, ostg=ostg, h=h: e.copy(
                            ostg[:, (h - 3) * 128:(h + 1) * 128], psTb[:, :]),
                            inc="ac", key=(st, "oe", n, g))
                        SC["a_psTb"] = (st, "oe", n, g)
                    if h == 10 and n < NTC - 1:
                        emit_castA(n)
                return tail

            pend_p = [None]
            pend_t = [None, None]
            rt_slots = {}

            def emit_loads(m):
                for (bl, src_, nm, spk, pool) in ((qblkb[m % 2], qs_s, "qb", "q", "dc"),
                                                  (kblkb[m % 2], ks_s, "kb", "k", "dg"),):
                    if m >= 2:
                        prog.wait("sy", (st, "blku", nm, m - 2))
                    else:
                        prog.wait("sy", vdone)
                    prog.wait("sy", (st, "qsp", spk, m, 1))
                    prog.dma("sy", bl[:, :], src_[m, :, :], f"{pool}{m % 2}",
                             key=(st, nm, m))
                if m < NTC - 1:
                    if m >= 2:
                        prog.wait("gp", (st, "kv", m - 2, H - 1))
                    else:
                        prog.wait("gp", vdone)
                    prog.wait("gp", (st, "ksp", m, 0))
                    prog.wait("gp", (st, "ksp", m, 1))
                    prog.dma("gp", ktblk[m % 2][:, :], kts_s[m, :, :],
                             f"dg{2 + m % 2}", key=(st, "ktb", m))
                rs, rt, rwar = ring_rtB.acquire()
                if rwar is not None:
                    prog.wait_if("gp", rwar)
                else:
                    prog.wait("gp", vdone)
                prog.dma("gp", rt[:, :], res_src[m * 128:(m + 1) * 128, :], f"db{rs}",
                         key=(st, "rr", m))
                rt_slots[m] = (rs, rt)

            emit_loads(0)
            for n in range(NTC):
                last_n = n == NTC - 1
                if not last_n:
                    emit_loads(n + 1)
                qb, kb = qblkb[n % 2], kblkb[n % 2]
                ostg = ostgs[n % 2]
                rs, rt = rt_slots.pop(n)
                prog.wait("pe", (st, "kb", n))
                prog.wait("pe", (st, "qb", n))
                if not last_n:
                    prog.wait("pe", (st, "ktb", n))
                prog.wait("pe", (st, "vev", 0, n))
                prog.wait("pe", (st, "vev", 1, n))
                for h in range(H):
                    hp = h % 2
                    g = h // GSZ
                    prog.wait_if("pe", SC.get(("a_psa", hp)))
                    prog.op("pe", lambda e, kb=kb, qb=qb, h=h, hp=hp: e.matmul(
                        psa2[hp][:, :], kb[:, h * 128:(h + 1) * 128],
                        qb[:, h * 128:(h + 1) * 128], start=True, stop=True),
                        inc="pe", key=(st, "at", n, h))
                    if not last_n:
                        if h % GSZ == 0:
                            prog.wait_if("pe", SC.get(("kvp_war", g % 2)))
                        prog.op("pe", lambda e, h=h, g=g, n=n: e.matmul(
                            kvp2[g % 2][:, (h - g * GSZ) * 129:(h - g * GSZ + 1) * 129],
                            ktblk[n % 2][:, h * 128:(h + 1) * 128],
                            vaug[:, n * 2064 + h * 129: n * 2064 + (h + 1) * 129],
                            start=True, stop=True), inc="pe", key=(st, "kv", n, h))
                    prog.wait("ve", (st, "at", n, h))
                    prog.wait_if("ve", SC.get(("a_atb", hp)))
                    prog.op("ve", lambda e, hp=hp: e.tensor_tensor(
                        atbf[:, hp * 128:(hp + 1) * 128], psa2[hp][:, :],
                        triu[:, :], ALU.mult),
                        inc="ve", key=(st, "atm", n, h))
                    SC[("a_psa", hp)] = (st, "atm", n, h)
                    if not last_n and (h % GSZ == GSZ - 1 or h == H - 1):
                        emit_add(n, g)
                    if pend_p[0] is not None:
                        pend_p[0]()
                    pend_p[0] = mk_pso(n, h, qb)
                    if pend_t[0] is not None:
                        pend_t[0]()
                    pend_t[0] = pend_t[1]
                    pend_t[1] = mk_tail(n, h, ostg)
                pend_p[0]()
                pend_p[0] = None
                pend_t[0]()
                pend_t[1]()
                pend_t[0] = pend_t[1] = None
                if not last_n:
                    emit_castB(n)
                prog.alias((st, "blku", "qb", n), (st, "ob", n, H - 1))
                prog.alias((st, "blku", "kb", n), (st, "at", n, H - 1))
                prog.wait("pe", (st, "oe", n, 3))
                if n == 0:
                    prog.wait("pe", (st, "wo", NCC - 1))
                for pc in range(4):
                    if n >= 1 or pc >= 1:
                        pn_, pcc = (n, pc - 1) if pc >= 1 else (n - 1, 3)
                        prog.wait("pe", (st, "rad", pn_, pcc))
                    for hc in range(NCC):
                        prog.op("pe", lambda e, hc=hc, pc=pc, ostg=ostg: e.matmul(
                            ppj[:, :],
                            ostg[:, hc * 128:(hc + 1) * 128],
                            wbuf[:, hc * D + pc * 512: hc * D + (pc + 1) * 512],
                            start=(hc == 0), stop=False))
                    prog.op("pe", lambda e, pc=pc: e.matmul(
                        ppj[:, :], ones_row[0:1, :],
                        brow[0:1, D + pc * 512: D + (pc + 1) * 512],
                        start=False, stop=True),
                        inc="pe", key=(st, "pj", n, pc))
                    prog.wait("ve", (st, "pj", n, pc))
                    if pc == 0:
                        prog.wait("ve", (st, "rr", n))
                    prog.op("ve", lambda e, rt=rt, pc=pc: e.tensor_tensor(
                        rt[:, pc * 512:(pc + 1) * 512], rt[:, pc * 512:(pc + 1) * 512],
                        ppj[:, :], ALU.add),
                        inc="ve", key=(st, "rad", n, pc))
                prog.wait("sy", (st, "rad", n, 3))
                prog.dma("sy", res_dst[n * 128:(n + 1) * 128, :], rt[:, :], f"df{rs}",
                         key=(st, "rst", n))
                ring_rtB.consumer(rs, (st, "rst", n))

        def emit_w2prep(st):
            pps2 = SC["pps2"]
            wts = SC["p_wt"]
            stg2 = SC["p_stg"]
            for a in range(64):
                wt = wts[a % 4]
                pq = pps2[(a % 2) * 4:(a % 2) * 4 + 4]
                if a >= 4:
                    prog.wait("gp", (st, "s1m", a - 4))
                prog.dma("gp", wt[:, :], w2t[a * 128:(a + 1) * 128, :], f"da{a % 4}",
                         key=(st, "s1w", a))
                if a >= 2:
                    prog.wait("pe", (st, "s1e", a - 2))
                prog.wait("pe", (st, "s1w", a))
                for nq in range(4):
                    prog.op("pe", lambda e, wt=wt, nq=nq, pq=pq: e.matmul(
                        pq[nq][:, :], h128[:, :],
                        wt[:, nq * 512:(nq + 1) * 512], start=True, stop=True),
                        inc=None if nq < 3 else "pe",
                        key=None if nq < 3 else (st, "s1m", a))
                ee = "ac" if a % 2 == 0 else "ve"
                prog.wait(ee, (st, "s1m", a))
                bt = stg2[a % 4]
                if a >= 4:
                    prog.wait(ee, (st, "s1s", a - 4))
                for nq in range(4):
                    if ee == "ac":
                        prog.op("ac", lambda e, bt=bt, nq=nq, pq=pq: e.mul(
                            bt[:, nq * 512:(nq + 1) * 512], pq[nq][:, :], FWHT_SCALE),
                            inc=None if nq < 3 else "ac",
                            key=None if nq < 3 else (st, "s1e", a))
                    else:
                        prog.op("ve", lambda e, bt=bt, nq=nq, pq=pq: e.tensor_scalar_mul(
                            bt[:, nq * 512:(nq + 1) * 512], pq[nq][:, :], FWHT_SCALE),
                            inc=None if nq < 3 else "ve",
                            key=None if nq < 3 else (st, "s1e", a))
                prog.wait("sy", (st, "s1e", a))
                prog.dma("sy", b_s[a, :, :], bt[:, :],
                         ["de0", "de1", "dc0", "dc1"][a % 4], key=(st, "s1s", a))
            for j in range(128):
                bj = wts[j % 4]
                pq = pps2[(j % 2) * 4:(j % 2) * 4 + 4]
                if j >= 4:
                    prog.wait("sy", (st, "s2m", j - 4))
                if j < 4:
                    for aa in range(60, 64):
                        prog.wait("sy", (st, "s1s", aa))
                prog.dma("sy", bj[0:64, :], b_s[:, j, :], f"dg{j % 4}", key=(st, "s2l", j))
                if j >= 2:
                    prog.wait("pe", (st, "s2e", j - 2))
                prog.wait("pe", (st, "s2l", j))
                for nq in range(4):
                    prog.op("pe", lambda e, bj=bj, nq=nq, pq=pq: e.matmul(
                        pq[nq][0:64, :], h64[:, :],
                        bj[0:64, nq * 512:(nq + 1) * 512], start=True, stop=True),
                        inc=None if nq < 3 else "pe",
                        key=None if nq < 3 else (st, "s2m", j))
                ee = "ac" if j % 2 == 0 else "ve"
                prog.wait(ee, (st, "s2m", j))
                ct = stg2[j % 4]
                if j >= 4:
                    prog.wait(ee, (st, "s2s", j - 4))
                for nq in range(4):
                    if ee == "ac":
                        prog.op("ac", lambda e, ct=ct, nq=nq, pq=pq: e.copy(
                            ct[0:64, nq * 512:(nq + 1) * 512], pq[nq][0:64, :]),
                            inc=None if nq < 3 else "ac",
                            key=None if nq < 3 else (st, "s2e", j))
                    else:
                        prog.op("ve", lambda e, ct=ct, nq=nq, pq=pq: e.tensor_copy(
                            ct[0:64, nq * 512:(nq + 1) * 512], pq[nq][0:64, :]),
                            inc=None if nq < 3 else "ve",
                            key=None if nq < 3 else (st, "s2e", j))
                prog.wait("gp", (st, "s2e", j))
                prog.dma("gp", w2p_s[:, j, :], ct[0:64, :],
                         ["ga0", "ga1", "ga2", "ga3"][j % 4], key=(st, "s2s", j))

        def emit_ffn(st, res_src, res_dst, first_pass):
            psG, psL = SC["psG"], SC["psL"]
            ppcs = SC["ppcs"]
            acc, sg = SC["acc"], SC["sg"]
            w12t, w34t = SC["w12t"], SC["w34t"]
            hgt, w2gt = SC["hgt"], SC["w2gt"]
            ring_rtF = SC["ring_rtF"]
            xb = SC["xb"]
            xb_done = (st, "xe", NTC - 1, 3)
            xn_done = xb_done
            for half in range(2):
                for hc in range(NHC):
                    g, gi = hc // G, hc % G
                    w1tile, w3tile = w12t[hc % 2], w34t[hc % 2]
                    war = None
                    if hc >= 2:
                        war = (st, "lmm", half, hc - 2)
                    elif half == 1:
                        war = (st, "lmm", 0, NHC - 2 + hc)
                    else:
                        war = xn_done
                    prog.wait("gp", war)
                    prog.dma("gp", w1tile[:, :], w1t[hc, :, :], f"da{hc % 2}",
                             key=(st, "w1", half, hc))
                    prog.dma("gp", w3tile[:, :], w3t[hc, :, :], f"ga{hc % 2}",
                             key=(st, "w3", half, hc))
                    w2war = None
                    if g >= 1:
                        w2war = (st, "cdone", half, g - 1)
                    elif half == 1:
                        w2war = (st, "cdone", 0, NGR - 1)
                    else:
                        w2war = xn_done
                    prog.wait("sy", w2war)
                    prog.dma("sy", w2gt[gi][:, :], w2p_s[hc, :, :], "sw2",
                             key=(st, "w2", half, hc))
                    prog.wait("pe", (st, "w1", half, hc))
                    if hc == 0 and half == 0:
                        prog.wait("pe", xb_done)
                    if hc >= 1:
                        prog.wait("pe", (st, "sg", half, hc - 1))
                    elif half == 1:
                        prog.wait("pe", (st, "sg", 0, NHC - 1))
                    for cc in range(NCC):
                        for nq in range(2):
                            prog.op("pe", lambda e, cc=cc, nq=nq, half=half, w1tile=w1tile: e.matmul(
                                psG[:, nq * 512:(nq + 1) * 512],
                                w1tile[:, cc * 128:(cc + 1) * 128],
                                xb[:, cc * T + half * 1024 + nq * 512:
                                   cc * T + half * 1024 + (nq + 1) * 512],
                                start=(cc == 0), stop=(cc == NCC - 1)),
                                inc=None if (cc, nq) != (NCC - 1, 1) else "pe",
                                key=None if (cc, nq) != (NCC - 1, 1) else (st, "gmm", half, hc))
                    prog.wait("ac", (st, "gmm", half, hc))
                    hp = None
                    if hc >= 1:
                        hp = (half, hc - 1)
                    elif half == 1:
                        hp = (0, NHC - 1)
                    if hp is not None:
                        prog.wait("ac", (st, "hpr") + hp)
                    prog.op("ac", lambda e: e.activation(sg[:, :], psG[:, :], AF.Silu),
                            inc="ac", key=(st, "sg", half, hc))
                    prog.wait("pe", (st, "w3", half, hc))
                    if hp is not None:
                        prog.wait("pe", (st, "hpr") + hp)
                    for cc in range(NCC):
                        for nq in range(2):
                            prog.op("pe", lambda e, cc=cc, nq=nq, half=half, w3tile=w3tile: e.matmul(
                                psL[:, nq * 512:(nq + 1) * 512],
                                w3tile[:, cc * 128:(cc + 1) * 128],
                                xb[:, cc * T + half * 1024 + nq * 512:
                                   cc * T + half * 1024 + (nq + 1) * 512],
                                start=(cc == 0), stop=(cc == NCC - 1)),
                                inc=None if (cc, nq) != (NCC - 1, 1) else "pe",
                                key=None if (cc, nq) != (NCC - 1, 1) else (st, "lmm", half, hc))
                    prog.wait("ve", (st, "lmm", half, hc))
                    prog.wait("ve", (st, "sg", half, hc))
                    hwar = None
                    if g >= 1:
                        hwar = (st, "cdone", half, g - 1)
                    elif half == 1:
                        hwar = (st, "cdone", 0, NGR - 1)
                    else:
                        hwar = xn_done
                    if gi == 0 or (g == 0 and half == 0):
                        prog.wait("ve", hwar)
                    prog.op("ve", lambda e, htile=hgt[gi]: e.scalar_tensor_tensor(
                        htile[:, :], psL[:, :], 1.0, sg[:, :], ALU.bypass, ALU.mult),
                        inc="ve", key=(st, "hpr", half, hc))
                    if gi == G - 1:
                        prog.wait("pe", (st, "hpr", half, hc))
                        prog.wait("pe", (st, "w2", half, hc))
                        for tcl in range(8):
                            for dh in range(2):
                                ppc = ppcs[dh]
                                cwar = None
                                if tcl >= 1:
                                    cwar = (st, "cad", half, g, tcl - 1, dh)
                                elif g >= 1:
                                    cwar = (st, "cad", half, g - 1, 7, dh)
                                elif half == 1:
                                    cwar = (st, "cad", 0, NGR - 1, 7, dh)
                                if cwar is not None:
                                    prog.wait("pe", cwar)
                                for i in range(G):
                                    for nq in range(2):
                                        prog.op("pe", lambda e, ppc=ppc, i=i, nq=nq, tcl=tcl, dh=dh: e.matmul(
                                            ppc[:, nq * 512:(nq + 1) * 512],
                                            hgt[i][:, tcl * 128:(tcl + 1) * 128],
                                            w2gt[i][:, dh * 1024 + nq * 512:
                                                    dh * 1024 + (nq + 1) * 512],
                                            start=(i == 0), stop=(i == G - 1)),
                                            inc=None if (i, nq) != (G - 1, 1) else "pe",
                                            key=None if (i, nq) != (G - 1, 1) else (st, "cmm", half, g, tcl, dh))
                                prog.wait("ve", (st, "cmm", half, g, tcl, dh))
                                dsto = tcl * D + dh * 1024
                                tc_g = half * 8 + tcl
                                if g == 0:
                                    rs, rt, rwar = ring_rtF.acquire()
                                    prog.wait_if("sy", rwar)
                                    if half == 0 and g == 0 and tcl < 2:
                                        prog.wait("sy", xn_done)
                                    prog.dma("sy", rt[:, :],
                                             res_src[tc_g * 128:(tc_g + 1) * 128,
                                                     dh * 1024:(dh + 1) * 1024],
                                             f"db{rs}", key=(st, "frr", half, tcl, dh))
                                    prog.wait("ve", (st, "frr", half, tcl, dh))
                                    if half == 1:
                                        prog.wait("ve", (st, "fst", 0, tcl, dh))
                                    prog.op("ve", lambda e, ppc=ppc, dsto=dsto, rt=rt: e.tensor_tensor(
                                        acc[:, dsto:dsto + 1024], rt[:, :], ppc[:, :], ALU.add),
                                        inc="ve", key=(st, "cad", half, g, tcl, dh))
                                    ring_rtF.consumer(rs, (st, "cad", half, g, tcl, dh))
                                else:
                                    prog.op("ve", lambda e, ppc=ppc, dsto=dsto: e.tensor_tensor(
                                        acc[:, dsto:dsto + 1024], acc[:, dsto:dsto + 1024],
                                        ppc[:, :], ALU.add),
                                        inc="ve", key=(st, "cad", half, g, tcl, dh))
                                if g == NGR - 1:
                                    prog.wait("sy", (st, "cad", half, g, tcl, dh))
                                    slot = (tcl * 2 + dh) % 4
                                    prog.wait_if("sy", SC.get(("fst_prev", slot)))
                                    prog.dma("sy",
                                             res_dst[tc_g * 128:(tc_g + 1) * 128,
                                                     dh * 1024:(dh + 1) * 1024],
                                             acc[:, dsto:dsto + 1024],
                                             f"df{slot}",
                                             key=(st, "fst", half, tcl, dh))
                                    SC[("fst_prev", slot)] = (st, "fst", half, tcl, dh)
                        prog.alias((st, "cdone", half, g), (st, "cmm", half, g, 7, 1))

        def emit_attn_b_attention(st):
            psS, psO2, psTw = SC["psS"], SC["psO2"], SC["psTw"]
            qstr2, kpre, ostgb = SC["qstr2"], SC["kpre"], SC["ostgb"]
            vaug, atbf, obf = SC["vaug"], SC["atbf"], SC["obf"]
            sqf = SC["sqf"]
            vdone = (st, "pmm", "v", 1, NTC - 1)
            guard_keys = [vdone, (st, "qe", "k", 1, NTC - 1, 1),
                          (st, "fmd", "k", 1, NTC - 1), (st, "vev", 1, NTC - 1),
                          (st, "qsp", "k", NTC - 2, 1), (st, "qsp", "k", NTC - 1, 1),
                          (st, "qsp", "q", NTC - 2, 1), (st, "qsp", "q", NTC - 1, 1)]
            for geng in ("pe", "ve", "ac", "sy", "gp"):
                for gk in guard_keys:
                    prog.wait(geng, gk)
            for tc in range(NTC):
                prog.wait("gp", (st, "qsp", "k", tc, 1))
                prog.dma("gp", kpre[:, tc * D:(tc + 1) * D], ks_s[tc, :, :],
                         f"dg{tc % 4}", key=(st, "kld", tc))

            pend_t = [None, None]

            def emit_qld(m):
                if m >= 2:
                    prog.wait("sy", (st, "quse", m - 2))
                for tc in range(4 * m, 4 * m + 4):
                    prog.wait("sy", (st, "qsp", "q", tc, 1))
                prog.dma("sy", qstr2[m % 2][:, :], q4_s[m, :, :], f"dd{m % 2}",
                         key=(st, "qld", m))

            emit_qld(0)
            for s in range(4):
                nkb = 4 * s + 4
                if s < 3:
                    emit_qld(s + 1)
                qstr = qstr2[s % 2]
                prog.wait("pe", (st, "qld", s))
                for tc in range(4 * s, nkb):
                    prog.wait("pe", (st, "kld", tc))
                    prog.wait("pe", (st, "vev", 0, tc))
                    prog.wait("pe", (st, "vev", 1, tc))
                for h in range(H):
                    for kb in range(nkb):
                        par = kb % 2
                        ig = kb - 4 * s
                        qoff = max(ig, 0) * 128
                        prog.wait_if("pe", SC.get(("b_psS", par)))
                        prog.op("pe", lambda e, kb=kb, h=h, par=par, qoff=qoff, qstr=qstr: e.matmul(
                            psS[:, par * 512 + qoff: par * 512 + 512],
                            kpre[:, kb * D + h * 128: kb * D + (h + 1) * 128],
                            qstr[:, h * 512 + qoff: h * 512 + 512],
                            start=True, stop=True),
                            inc="pe", key=(st, "smm", s, h, kb))
                        wdt = 512 - qoff
                        if par == 0:
                            prog.wait("ve", (st, "smm", s, h, kb))
                            prog.op("ve", lambda e, par=par, qoff=qoff, wdt=wdt: e.tensor_copy(
                                sqf[:, 0:wdt], psS[:, par * 512 + qoff: par * 512 + 512]),
                                inc="ve", key=(st, "sqc", s, h, kb))
                            prog.op("ve", lambda e, kb=kb, qoff=qoff, wdt=wdt: e.tensor_tensor(
                                atbf[:, kb * 512 + qoff: kb * 512 + 512],
                                sqf[:, 0:wdt], sqf[:, 0:wdt], ALU.mult),
                                inc="ve", key=(st, "sq", s, h, kb))
                            SC[("b_psS", par)] = (st, "sqc", s, h, kb)
                        else:
                            prog.wait("ac", (st, "smm", s, h, kb))
                            prog.op("ac", lambda e, kb=kb, par=par, qoff=qoff: e.activation(
                                atbf[:, kb * 512 + qoff: kb * 512 + 512],
                                psS[:, par * 512 + qoff: par * 512 + 512], AF.Square),
                                inc="ac", key=(st, "sq", s, h, kb))
                            SC[("b_psS", par)] = (st, "sq", s, h, kb)
                        if ig >= 0:
                            if par == 1:
                                prog.wait("ve", (st, "sq", s, h, kb))
                            prog.op("ve", lambda e, kb=kb, ig=ig: e.tensor_tensor(
                                atbf[:, kb * 512 + ig * 128: kb * 512 + (ig + 1) * 128],
                                atbf[:, kb * 512 + ig * 128: kb * 512 + (ig + 1) * 128],
                                triu[:, :], ALU.mult),
                                inc="ve", key=(st, "msk", s, h, kb))
                    for ql in range(4):
                        qc = 4 * s + ql
                        op_ = ql % 2
                        prog.wait_if("pe", SC.get(("b_psO", op_)))
                        prog.wait("pe", (st, "msk", s, h, qc))
                        prog.wait("pe", (st, "sq", s, h, qc))
                        if qc >= 1:
                            prog.wait("pe", (st, "sq", s, h, qc - 1))
                        for kb in range(qc + 1):
                            prog.op("pe", lambda e, ql=ql, kb=kb, h=h, op_=op_, qc=qc: e.matmul(
                                psO2[op_][:, 0:129],
                                atbf[:, kb * 512 + ql * 128: kb * 512 + (ql + 1) * 128],
                                vaug[:, kb * 2064 + h * 129: kb * 2064 + (h + 1) * 129],
                                start=(kb == 0), stop=(kb == qc)),
                                inc="pe" if kb == qc else None,
                                key=(st, "oq", s, h, ql) if kb == qc else None)
                        prog.wait("ve", (st, "oq", s, h, ql))
                        prog.op("ve", lambda e, op_=op_: e.tensor_scalar_add(
                            zcol[:, op_:op_ + 1], psO2[op_][:, 128:129], EPS_B))
                        prog.op("ve", lambda e, op_=op_: e.reciprocal(
                            zcol[:, 4 + op_:5 + op_], zcol[:, op_:op_ + 1]))
                        prog.wait_if("ve", SC.get(("b_obf", ql)))
                        prog.op("ve", lambda e, op_=op_, ql=ql: e.tensor_scalar_mul(
                            obf[:, ql * 128:(ql + 1) * 128], psO2[op_][:, 0:128],
                            zcol[:, 4 + op_:5 + op_]),
                            inc="ve", key=(st, "odv", s, h, ql))
                        SC[("b_psO", op_)] = (st, "odv", s, h, ql)

                        def mk_tail(s=s, h=h, ql=ql):
                            def tail():
                                prog.wait("pe", (st, "odv", s, h, ql))
                                if h % 4 == 0:
                                    prog.wait_if("pe", SC.get(("b_psT", ql)))
                                prog.op("pe", lambda e, ql=ql, h=h: e.transpose(
                                    psTw[ql][:, (h % 4) * 128:(h % 4 + 1) * 128],
                                    obf[:, ql * 128:(ql + 1) * 128],
                                    ident[:, :]),
                                    inc="pe", key=(st, "bot", s, h, ql))
                                SC[("b_obf", ql)] = (st, "bot", s, h, ql)
                                if h % 4 == 3:
                                    g = h // 4
                                    prog.wait("ac", (st, "bot", s, h, ql))
                                    if g == 0:
                                        if s >= 1:
                                            prog.wait("ac", (st, "ost", s - 1, ql))
                                        else:
                                            prog.wait("ac", vdone)
                                    prog.op("ac", lambda e, ql=ql, h=h: e.copy(
                                        ostgb[ql][:, (h - 3) * 128:(h + 1) * 128],
                                        psTw[ql][:, :]),
                                        inc="ac", key=(st, "boe", s, g, ql))
                                    SC[("b_psT", ql)] = (st, "boe", s, g, ql)
                            return tail
                        if pend_t[0] is not None:
                            pend_t[0]()
                        pend_t[0] = pend_t[1]
                        pend_t[1] = mk_tail()
                pend_t[0]()
                pend_t[1]()
                pend_t[0] = pend_t[1] = None
                for ql in range(4):
                    prog.wait("sy", (st, "boe", s, 3, ql))
                    prog.dma("sy", os_s[4 * s + ql, :, :], ostgb[ql][:, :], f"df{ql}",
                             key=(st, "ost", s, ql))
                prog.alias((st, "quse", s), (st, "smm", s, H - 1, nkb - 1))

        def emit_attn_b_proj(st, res_src, res_dst):
            ppj = SC["ppj"]
            wbuf = SC["wbuf"]
            oblks = SC["oblks"]
            ring_rtB = SC["ring_rtB"]
            attn_done = (st, "quse", 3)
            guard_keys = [attn_done, (st, "odv", 3, H - 1, 3),
                          (st, "boe", 3, 3, 3), (st, "sq", 3, H - 1, 15)] + \
                         [(st, "ost", 3, ql) for ql in range(4)]
            for geng in ("pe", "ve", "ac", "sy", "gp"):
                for gk in guard_keys:
                    prog.wait(geng, gk)
            for cc in range(NCC):
                prog.dma("gp", wbuf[:, cc * D:(cc + 1) * D],
                         wo_b[cc * 128:(cc + 1) * 128, :], "gw", key=(st, "wo", cc))

            rt_slots = {}

            def emit_loads(m):
                ob = oblks[m % 2]
                if m >= 2:
                    prog.wait("sy", (st, "puse", m - 2))
                else:
                    prog.wait("sy", attn_done)
                prog.wait("sy", (st, "ost", m // 4, m % 4))
                prog.dma("sy", ob[:, :], os_s[m, :, :], f"dc{m % 2}", key=(st, "old", m))
                rs, rt, rwar = ring_rtB.acquire()
                if rwar is not None:
                    prog.wait_if("gp", rwar)
                else:
                    prog.wait("gp", attn_done)
                prog.dma("gp", rt[:, :], res_src[m * 128:(m + 1) * 128, :], f"db{rs}",
                         key=(st, "rr", m))
                rt_slots[m] = (rs, rt)

            emit_loads(0)
            for n in range(NTC):
                ob = oblks[n % 2]
                if n < NTC - 1:
                    emit_loads(n + 1)
                rs, rt = rt_slots.pop(n)
                prog.wait("pe", (st, "old", n))
                if n == 0:
                    prog.wait("pe", (st, "wo", NCC - 1))
                for pc in range(4):
                    pp = ppj[pc % 2]
                    if n >= 1 or pc >= 2:
                        pn_, pcc = (n, pc - 2) if pc >= 2 else (n - 1, pc + 2)
                        prog.wait("pe", (st, "rad", pn_, pcc))
                    for hc in range(NCC):
                        prog.op("pe", lambda e, pp=pp, ob=ob, hc=hc, pc=pc: e.matmul(
                            pp[:, :],
                            ob[:, hc * 128:(hc + 1) * 128],
                            wbuf[:, hc * D + pc * 512: hc * D + (pc + 1) * 512],
                            start=(hc == 0), stop=(hc == NCC - 1)),
                            inc=None if hc != NCC - 1 else "pe",
                            key=None if hc != NCC - 1 else (st, "pj", n, pc))
                    prog.wait("ve", (st, "pj", n, pc))
                    if pc == 0:
                        prog.wait("ve", (st, "rr", n))
                    prog.op("ve", lambda e, rt=rt, pp=pp, pc=pc: e.tensor_tensor(
                        rt[:, pc * 512:(pc + 1) * 512], rt[:, pc * 512:(pc + 1) * 512],
                        pp[:, :], ALU.add),
                        inc="ve", key=(st, "rad", n, pc))
                prog.alias((st, "puse", n), (st, "pj", n, 3))
                prog.wait("sy", (st, "rad", n, 3))
                prog.dma("sy", res_dst[n * 128:(n + 1) * 128, :], rt[:, :], f"de{rs}",
                         key=(st, "rst", n))
                ring_rtB.consumer(rs, (st, "rst", n))

        flushed = {e: 0 for e in ENGINES}

        def flush_block():
            for sn in prog.all_dma_sems():
                v = prog.cnt[sn]
                if v > 0:
                    eng = "gp" if sn in ("gw", "gwb") or sn.startswith(("da", "ga")) else "sy"
                    prog.ops[eng].append(
                        lambda e, sem=prog.sems[sn], v=v: e.wait_ge(sem, v))
            start_idx = dict(flushed)
            with nc.Block() as blk:
                def mk(eng_name):
                    def run(eng):
                        for f in prog.ops[eng_name][start_idx[eng_name]:]:
                            f(eng)
                    return run
                blk.tensor(mk("pe"))
                blk.vector(mk("ve"))
                blk.scalar(mk("ac"))
                blk.gpsimd(mk("gp"))
                blk.sync(mk("sy"))
            for e in ENGINES:
                flushed[e] = len(prog.ops[e])

        emit_const_loads()
        flush_block()

        if n_sub >= 2:
            with ExitStack() as s0:
                SC["pps2"] = [s0.enter_context(nc.psum_tensor(f"p_q{i}", [128, 512], F32))
                              for i in range(8)]
                SC["p_wt"] = [s0.enter_context(nc.sbuf_tensor(f"p_w{i}", [128, D], BF16))
                              for i in range(4)]
                SC["p_stg"] = [s0.enter_context(nc.sbuf_tensor(f"p_s{i}", [128, D], BF16))
                               for i in range(4)]
                emit_w2prep(90)
                flush_block()

        stage_specs = []
        if n_sub >= 1:
            stage_specs.append(("attn_a", 0))
        if n_sub >= 2:
            stage_specs.append(("ffn", 1))
        if n_sub >= 3:
            stage_specs.append(("attn_b", 2))
        if n_sub >= 4:
            stage_specs.append(("ffn", 3))

        for kind, idx in stage_specs:
            st = idx
            res_src, res_dst = res_seq[idx], res_seq[idx + 1]
            if kind in ("attn_a", "attn_b"):
                is_a = kind == "attn_a"
                with ExitStack() as sA:
                    SC["brow"] = sA.enter_context(nc.sbuf_tensor(f"s{st}_br", [1, 2 * D], BF16))
                    for par in range(2):
                        SC.pop(("xb_war", par), None)
                        SC.pop(("xb_rwar", par), None)
                        SC.pop(("elu_war", par), None)
                        SC.pop(("lnb_y_war", par), None)
                        SC.pop(("lnb_r_war", par), None)
                        SC.pop(("lnb_y2_war", par), None)
                    SC.pop("wload_war_q", None); SC.pop("wload_war_k", None)
                    SC.pop("wload_war_v", None)
                    sXW = ExitStack()
                    SC["xb"] = sXW.enter_context(nc.sbuf_tensor(f"s{st}_xb", [128, NCC * T], BF16))
                    SC["wbufH"] = sXW.enter_context(nc.sbuf_tensor(f"s{st}_wh", [128, NCC * 1024], BF16))
                    SC["sqbf"] = sXW.enter_context(nc.sbuf_tensor(f"s{st}_sq", [128, D], BF16))
                    SC["scr_c"] = sXW.enter_context(nc.sbuf_tensor(f"s{st}_sc", [128, 2048], F32))
                    SC["ybf"] = sXW.enter_context(nc.sbuf_tensor(f"s{st}_yb", [128, 2048], BF16))
                    SC["ring_xn"] = Ring([sXW.enter_context(
                        nc.sbuf_tensor(f"s{st}_xn{i}", [128, 1024], BF16)) for i in range(2)])
                    SC["ring_st"] = Ring([sXW.enter_context(
                        nc.sbuf_tensor(f"s{st}_st{i}", [128, 1024], BF16)) for i in range(2)])
                    if not is_a:
                        SC["gamt"] = sXW.enter_context(nc.sbuf_tensor(f"s{st}_gm", [128, D], BF16))
                        SC["y2bf"] = sXW.enter_context(nc.sbuf_tensor(f"s{st}_y2", [128, 2048], BF16))
                    SC["ring_pt"] = Ring([sXW.enter_context(
                        nc.psum_tensor(f"s{st}_pt{i}", [128, 512], BF16)) for i in range(2)])
                    SC["ring_pp"] = Ring([sXW.enter_context(
                        nc.psum_tensor(f"s{st}_pp{i}", [128, 1024], F32)) for i in range(3)])
                    sR = ExitStack()
                    SC["ring_rt"] = Ring([sR.enter_context(
                        nc.sbuf_tensor(f"s{st}_rt{i}", [128, D], F32)) for i in range(4)])
                    emit_xb(st, res_src)
                    sR.close()
                    emit_attn_phaseA(st, kind)
                    SC["vaug"] = sA.enter_context(
                        nc.sbuf_tensor(f"s{st}_va", [128, NTC * H * 129], BF16, side="right"))
                    emit_attn_v(st, kind)
                    sXW.close()
                    if is_a:
                        with ExitStack() as sB:
                            SC["wbuf"] = sB.enter_context(
                                nc.sbuf_tensor(f"s{st}_wo", [128, NCC * D], BF16))
                            SC["psa2"] = [sB.enter_context(
                                nc.psum_tensor(f"s{st}_psa{i}", [128, 128], F32))
                                for i in range(2)]
                            SC["psTb"] = sB.enter_context(
                                nc.psum_tensor(f"s{st}_psTb", [128, 512], BF16))
                            SC["pso2"] = [sB.enter_context(
                                nc.psum_tensor(f"s{st}_pso{i}", [128, 129], F32))
                                for i in range(2)]
                            SC["kvp2"] = [sB.enter_context(
                                nc.psum_tensor(f"s{st}_kvp{i}", [128, 387], F32))
                                for i in range(2)]
                            SC["ppj"] = sB.enter_context(
                                nc.psum_tensor(f"s{st}_ppj", [128, 512], F32))
                            SC["state"] = sB.enter_context(
                                nc.sbuf_tensor(f"s{st}_kv", [128, H * 129], F32))
                            SC["state_bf2"] = [sB.enter_context(
                                nc.sbuf_tensor(f"s{st}_kvb{i}", [128, H * 129], BF16))
                                for i in range(2)]
                            SC["atbf"] = sB.enter_context(
                                nc.sbuf_tensor(f"s{st}_at", [128, 256], BF16))
                            SC["obf"] = sB.enter_context(
                                nc.sbuf_tensor(f"s{st}_ob", [128, 512], BF16))
                            for kk in ("a_psTb",):
                                SC.pop(kk, None)
                            for hp_ in range(4):
                                for kk in ("a_psa", "a_atb", "a_pso", "a_obf",
                                           "kvp_war"):
                                    SC.pop((kk, hp_), None)
                            SC["ostgs"] = [sB.enter_context(
                                nc.sbuf_tensor(f"s{st}_og{i}", [128, D], BF16))
                                for i in range(2)]
                            SC["qblkb"] = [sB.enter_context(
                                nc.sbuf_tensor(f"s{st}_qb{i}", [128, D], BF16))
                                for i in range(2)]
                            SC["kblkb"] = [sB.enter_context(
                                nc.sbuf_tensor(f"s{st}_kb{i}", [128, D], BF16))
                                for i in range(2)]
                            SC["ktblk"] = [sB.enter_context(
                                nc.sbuf_tensor(f"s{st}_kt{i}", [128, D], BF16))
                                for i in range(2)]
                            SC["ring_rtB"] = Ring([sB.enter_context(
                                nc.sbuf_tensor(f"s{st}_rb{i}", [128, D], F32))
                                for i in range(2)])
                            emit_attn_a_phaseB(st, res_src, res_dst)
                            flush_block()
                    else:
                        with ExitStack() as sB:
                            SC["psS"] = sB.enter_context(
                                nc.psum_tensor(f"s{st}_psS", [128, 1024], F32))
                            SC["psO2"] = [sB.enter_context(
                                nc.psum_tensor(f"s{st}_pO{i}", [128, 129], F32))
                                for i in range(2)]
                            SC["psTw"] = [sB.enter_context(
                                nc.psum_tensor(f"s{st}_psTw{i}", [128, 512], BF16))
                                for i in range(4)]
                            SC["qstr2"] = [sB.enter_context(
                                nc.sbuf_tensor(f"s{st}_qs{i}", [128, H * 512], BF16))
                                for i in range(2)]
                            SC["kpre"] = sB.enter_context(
                                nc.sbuf_tensor(f"s{st}_kp", [128, NTC * D], BF16))
                            SC["ostgb"] = [sB.enter_context(
                                nc.sbuf_tensor(f"s{st}_ot{i}", [128, D], BF16))
                                for i in range(4)]
                            SC["atbf"] = sB.enter_context(
                                nc.sbuf_tensor(f"s{st}_atb", [128, NTC * 512], BF16))
                            SC["obf"] = sB.enter_context(
                                nc.sbuf_tensor(f"s{st}_obb", [128, 512], BF16))
                            SC["sqf"] = sB.enter_context(
                                nc.sbuf_tensor(f"s{st}_sqf", [128, 512], F32))
                            for kk in range(4):
                                for nm in ("b_psS", "b_psO", "b_psT", "b_obf"):
                                    SC.pop((nm, kk), None)
                            emit_attn_b_attention(st)
                        with ExitStack() as sC:
                            SC["ppj"] = [sC.enter_context(
                                nc.psum_tensor(f"s{st}_bpj{i}", [128, 512], F32))
                                for i in range(2)]
                            SC["wbuf"] = sC.enter_context(
                                nc.sbuf_tensor(f"s{st}_wob", [128, NCC * D], BF16))
                            SC["oblks"] = [sC.enter_context(
                                nc.sbuf_tensor(f"s{st}_obk{i}", [128, D], BF16))
                                for i in range(2)]
                            SC["ring_rtB"] = Ring([sC.enter_context(
                                nc.sbuf_tensor(f"s{st}_rc{i}", [128, D], F32))
                                for i in range(2)])
                            emit_attn_b_proj(st, res_src, res_dst)
                            flush_block()
            else:
                for par in range(2):
                    SC.pop(("xb_war", par), None)
                    SC.pop(("xb_rwar", par), None)
                for slot in range(4):
                    SC.pop(("fst_prev", slot), None)
                with ExitStack() as sF:
                    SC["xb"] = sF.enter_context(nc.sbuf_tensor(f"f{st}_xb", [128, NCC * T], BF16))
                    sX = ExitStack()
                    SC["sqbf"] = sX.enter_context(nc.sbuf_tensor(f"f{st}_sq", [128, D], BF16))
                    SC["ring_xn"] = Ring([sX.enter_context(
                        nc.sbuf_tensor(f"f{st}_xn{i}", [128, 1024], BF16)) for i in range(2)])
                    SC["ring_pt"] = Ring([sX.enter_context(
                        nc.psum_tensor(f"f{st}_pt{i}", [128, 512], BF16)) for i in range(2)])
                    SC["ring_rt"] = Ring([sX.enter_context(
                        nc.sbuf_tensor(f"f{st}_rt{i}", [128, D], F32)) for i in range(4)])
                    emit_xb(st, res_src)
                    sX.close()
                    with ExitStack() as sG:
                        SC["psG"] = sG.enter_context(nc.psum_tensor(f"f{st}_pG", [128, 1024], F32))
                        SC["psL"] = sG.enter_context(nc.psum_tensor(f"f{st}_pL", [128, 1024], F32))
                        SC["ppcs"] = [sG.enter_context(nc.psum_tensor(f"f{st}_pc{i}", [128, 1024], F32))
                                      for i in range(2)]
                        SC["acc"] = sG.enter_context(nc.sbuf_tensor(f"f{st}_ac", [128, 8 * D], F32))
                        SC["sg"] = sG.enter_context(nc.sbuf_tensor(f"f{st}_sg", [128, 1024], F32))
                        SC["w12t"] = [sG.enter_context(nc.sbuf_tensor(f"f{st}_w1{i}", [128, D], BF16))
                                      for i in range(2)]
                        SC["w34t"] = [sG.enter_context(nc.sbuf_tensor(f"f{st}_w3{i}", [128, D], BF16))
                                      for i in range(2)]
                        SC["hgt"] = [sG.enter_context(nc.sbuf_tensor(f"f{st}_hg{i}", [128, 1024], BF16))
                                     for i in range(G)]
                        SC["w2gt"] = [sG.enter_context(nc.sbuf_tensor(f"f{st}_w2{i}", [128, D], BF16))
                                      for i in range(G)]
                        SC["ring_rtF"] = Ring([sG.enter_context(
                            nc.sbuf_tensor(f"f{st}_rr{i}", [128, 1024], F32)) for i in range(2)])
                        emit_ffn(st, res_src, res_dst, first_pass=(idx == 1))
                        flush_block()

        for sn in prog.all_dma_sems():
            if prog.cnt[sn] > 0:
                prog.op("sy", lambda e, sn=sn: e.wait_ge(sems[sn], prog.cnt[sn]))
        flush_block()

    return nc


def _mk_hadamard(n):
    h = np.array([[1.0]], dtype=np.float64)
    while h.shape[0] < n:
        h = np.block([[h, h], [h, -h]])
    return h


def make_inmaps(inputs):
    bf = ml_dtypes.bfloat16
    x = np.ascontiguousarray(np.asarray(inputs["L0_attn_a_x"], dtype=np.float32))
    B = x.shape[0]

    def f32(a):
        return np.ascontiguousarray(np.asarray(a, dtype=np.float32))

    common = {
        "wq_a": f32(inputs["L0_attn_a_Wq"]), "wk_a": f32(inputs["L0_attn_a_Wk"]),
        "wv_a": f32(inputs["L0_attn_a_Wv"]), "wo_a": f32(inputs["L0_attn_a_Wo"]),
        "bq_a": f32(inputs["L0_attn_a_bq"]).reshape(1, D),
        "bk_a": f32(inputs["L0_attn_a_bk"]).reshape(1, D),
        "bv_a": f32(inputs["L0_attn_a_bv"]).reshape(1, D),
        "bo_a": f32(inputs["L0_attn_a_bo"]).reshape(1, D),
        "wq_b": np.ascontiguousarray(f32(inputs["L0_attn_b_Wq"]).T),
        "wk_b": np.ascontiguousarray(f32(inputs["L0_attn_b_Wk"]).T),
        "wv_b": np.ascontiguousarray(f32(inputs["L0_attn_b_Wv"]).T),
        "wo_b": np.ascontiguousarray(f32(inputs["L0_attn_b_Wo"]).T),
        "gam_t": np.ascontiguousarray(np.tile(f32(inputs["L0_attn_b_gamma"]), (128, H))),
        "bet_row": np.ascontiguousarray(np.tile(f32(inputs["L0_attn_b_beta"]), H)).reshape(1, D),
        "w1t": np.ascontiguousarray(
            f32(inputs["L0_ffn_W1"]).T.reshape(NCC, 128, NHC, 128).transpose(2, 1, 0, 3)
        ).reshape(NHC, 128, D),
        "w3t": np.ascontiguousarray(
            f32(inputs["L0_ffn_W3"]).T.reshape(NCC, 128, NHC, 128).transpose(2, 1, 0, 3)
        ).reshape(NHC, 128, D),
        "w2t": np.ascontiguousarray(f32(inputs["L0_ffn_W2"]).T),
        "ident_c": np.eye(128, dtype=bf),
        "triu_c": np.triu(np.ones((128, 128))).astype(bf),
        "h128_c": _mk_hadamard(128).astype(bf),
        "h64_c": _mk_hadamard(64).astype(bf),
    }
    return [dict(common, x=np.ascontiguousarray(x[b])) for b in range(B)], B


def kernel(**inputs):
    in_maps, B = make_inmaps(inputs)
    nc = build(N_SUB)
    res = run_bass_kernel_spmd(nc, in_maps, core_ids=list(range(B)))
    return np.stack([np.asarray(r["out"], dtype=np.float32) for r in res.results], axis=0)
